# version 02
# baseline (speedup 1.0000x reference)
import contextlib
import numpy as np
import ml_dtypes
import concourse.bass as bass
import concourse.mybir as mybir
from concourse.bass_utils import run_bass_kernel_spmd

F32 = mybir.dt.float32
BF16 = mybir.dt.bfloat16
I32 = mybir.dt.int32
AF = mybir.ActivationFunctionType
ALU = mybir.AluOpType
AX = mybir.AxisListType

NC = 8
S = 8192
D = 2048
H = 16
CH = 512
T = 1024
NCH = 16
DC = 16
HHALF = 4096
EPS_LN = 1e-5
SCALE_L1 = float(1.0 / np.sqrt(np.float32(128.0)))
NEG_BIG = -1e30
MAGIC = 1.5 * 2.0 ** 23
BF = ml_dtypes.bfloat16
ENGINES = ("sync", "scalar", "vector", "tensor", "gpsimd")
CSEM = {"scalar": "Ts", "vector": "Tv", "tensor": "Tt", "gpsimd": "Tg"}
QSUF = {"sync": "Y", "scalar": "S", "gpsimd": "G"}
CUMU = ("Din", "Dout")


def heads_of(core):
    return (core, core + 8)


class Prog:

    def __init__(self, nc):
        self.nc = nc
        self.es = contextlib.ExitStack()
        self.ops = {e: [] for e in ENGINES}
        self.sems = {}
        self.cnt = {}
        self.lastwait = {}
        self.track = {}

    def sem(self, name):
        if name not in self.sems:
            self.sems[name] = self.es.enter_context(self.nc.semaphore(name))
            self.cnt[name] = 0
        return self.sems[name]

    def sb(self, name, shape, dtype):
        return self.es.enter_context(self.nc.sbuf_tensor(name, shape, dtype))

    def ps(self, name, shape, dtype=F32):
        return self.es.enter_context(self.nc.psum_tensor(name, shape, dtype))

    def _emit_waits(self, eng, toks):
        need = {}
        for (sname, val) in toks:
            if val > need.get(sname, 0):
                need[sname] = val
        out = []
        for sname, val in need.items():
            key = (eng, sname)
            if self.lastwait.get(key, 0) < val:
                self.lastwait[key] = val
                out.append((self.sem(sname), val))
        return out

    def _dep_toks(self, reads, writes):
        toks = []
        for t in reads:
            rec = self.track.get(t)
            if rec and rec[0] is not None:
                toks.append(rec[0])
        for t in writes:
            rec = self.track.get(t)
            if rec:
                if rec[0] is not None:
                    toks.append(rec[0])
                toks.extend(rec[1].items())
        toks = [(s, self.cnt[s]) if s.startswith(CUMU) else (s, v) for (s, v) in toks]
        return toks

    def op(self, eng, fn, reads=(), writes=(), dma=False, noinc=False, dsem=None):
        waits = self._emit_waits(eng, self._dep_toks(reads, writes))
        sname = (dsem + QSUF[eng]) if dma else CSEM[eng]
        sem = self.sem(sname)
        if noinc:
            tok = None
        else:
            self.cnt[sname] += 16 if dma else 1
            tok = (sname, self.cnt[sname])
        inc_by = 16 if dma else 1

        def closure(e, waits=waits, fn=fn, sem=sem, inc_by=inc_by, noinc=noinc):
            for (s, v) in waits:
                e.wait_ge(s, v)
            ins = fn(e)
            if not noinc:
                ins.then_inc(sem, inc_by)

        self.ops[eng].append(closure)
        if tok is not None:
            for t in writes:
                self.track[t] = [tok, {}]
            for t in reads:
                rec = self.track.setdefault(t, [None, {}])
                if tok[1] > rec[1].get(tok[0], 0):
                    rec[1][tok[0]] = tok[1]
        return tok

    def raw(self, eng, fn):
        self.ops[eng].append(lambda e, fn=fn: fn(e))

    def multi(self, eng, fns, reads=(), writes=(), dma=False, dsem=None):
        n = len(fns)
        if n == 1:
            return self.op(eng, fns[0], reads=reads, writes=writes, dma=dma, dsem=dsem)
        if dma:
            self.op(eng, fns[0], reads=reads, writes=writes, dma=True, dsem=dsem)
            sname = dsem + QSUF[eng]
            sem = self.sem(sname)
            for f in fns[1:]:
                self.cnt[sname] += 16
                self.ops[eng].append(lambda e, f=f, sem=sem: f(e).then_inc(sem, 16))
            tok = (sname, self.cnt[sname])
            for t in writes:
                self.track[t] = [tok, {}]
            return tok
        self.op(eng, fns[0], reads=reads, writes=writes, dma=dma, noinc=True)
        for f in fns[1:-1]:
            self.raw(eng, f)
        return self.op(eng, fns[-1], reads=reads, writes=writes, dma=dma)

    def finish(self, out_tags):
        toks = []
        for t in out_tags:
            rec = self.track.get(t)
            if rec and rec[0] is not None:
                toks.append(rec[0])
        waits = self._emit_waits("sync", toks)

        def closure(e, waits=waits):
            for (s, v) in waits:
                e.wait_ge(s, v)
        self.ops["sync"].append(closure)

    def emit(self):
        with self.nc.Block() as blk:
            for e in ENGINES:
                if not self.ops[e]:
                    continue
                dec = getattr(blk, e)

                def body(engobj, e=e):
                    for f in self.ops[e]:
                        f(engobj)
                dec(body)



class Common:
    def __init__(self, p, wbuf_cols=DC * 256):
        self.p = p
        self.xsq = [p.sb("xsq0", [128, T], BF16), p.sb("xsq1", [128, T], BF16)]
        self.p_stat = p.ps("p_stat", [1, 2 * T])
        self.p_mean = self.p_stat[:, 0:T]
        self.p_sq = self.p_stat[:, T:2 * T]
        self.srow = p.sb("statsrow", [1, T], F32)[:, :]
        self.srow2 = p.sb("statsrow2", [1, T], F32)[:, :]
        self.Rrow = p.sb("rrow", [1, T], F32)[:, :]
        self.MRrow = p.sb("mrrow", [1, T], F32)[:, :]
        self.Rrep = p.sb("Rrep", [128, T], F32)
        self.MRrep = p.sb("MRrep", [128, T], F32)
        self.ones_f32 = p.sb("ones_f32", [128, 1], F32)[:, :]
        self.ones_bf = p.sb("ones_bf", [128, 1], BF16)[:, :]
        self.ones_row = p.sb("ones_row", [1, 128], F32)[:, :]
        self.eps1 = p.sb("eps1", [1, 1], F32)
        _tl = p.sb("tlnbuf", [128, T], F32)
        self.tln = [_tl, _tl]
        self.wbuf = [p.sb("wbufA", [128, wbuf_cols], BF16),
                     p.sb("wbufB", [128, wbuf_cols], BF16)]
        self.pproj = [p.ps("pprojA", [128, 512]), p.ps("pprojB", [128, 512])]
        self.p_R = self.pproj[0]
        self.p_MR = self.pproj[1]
        p.op("gpsimd", lambda e: e.memset(self.ones_f32[:, :], 1.0), writes=("ones",))
        p.op("gpsimd", lambda e: e.memset(self.ones_bf[:, :], 1.0), writes=("ones",))
        p.op("gpsimd", lambda e: e.memset(self.ones_row[:, :], 1.0), writes=("ones",))
        p.op("gpsimd", lambda e: e.memset(self.eps1[:, :], EPS_LN), writes=("ones",))

    def ln_stats(self, src_ap_of, src_tag_of, src_is_f32, tcols):
        p = self.p
        nh = (tcols + 511) // 512
        for d in range(DC):
            src = src_ap_of(d)
            stag = src_tag_of(d)
            sq = self.xsq[d % 2]
            p.op("scalar", lambda e, s=src, o=sq, w=tcols: e.activation(o[:, :w], s, AF.Square),
                 reads=(stag,), writes=(f"xsq{d % 2}",))
            for th in range(nh):
                w = min(512, tcols - th * 512)
                sl = slice(th * 512, th * 512 + w)
                first, last = (d == 0), (d == DC - 1)
                ones = self.ones_f32 if src_is_f32 else self.ones_bf
                p.op("tensor", lambda e, s=src, sl=sl, f=first, l=last, o=ones:
                     e.matmul(self.p_mean[:, sl], o, s[:, sl], start=f, stop=l),
                     reads=(stag, "ones"),
                     writes=("p_mean",) if (first or last) else ())
                p.op("tensor", lambda e, sl=sl, q=sq, f=first, l=last:
                     e.matmul(self.p_sq[:, sl], self.ones_bf, q[:, sl], start=f, stop=l),
                     reads=(f"xsq{d % 2}", "ones"),
                     writes=("p_sq",) if (first or last) else ())
        self.ln_finish(tcols)

    def ln_finish(self, tcols):
        p = self.p
        w = tcols
        nh = (tcols + 511) // 512
        p.op("vector", lambda e: e.tensor_scalar(self.srow[:, :w], self.p_mean[:, :w],
                                                 1.0 / D, None, ALU.mult),
             reads=("p_mean",), writes=("srow",))
        p.op("vector", lambda e: e.tensor_tensor(self.srow2[:, :w], self.srow[:, :w],
                                                 self.srow[:, :w], ALU.mult),
             reads=("srow",), writes=("srow2",))
        p.op("vector", lambda e: e.scalar_tensor_tensor(self.srow2[:, :w], self.p_sq[:, :w],
                                                        1.0 / D, self.srow2[:, :w],
                                                        ALU.mult, ALU.subtract),
             reads=("p_sq", "srow2"), writes=("srow2",))
        p.op("scalar", lambda e: e.activation(self.srow2[:, :w], self.srow2[:, :w],
                                              AF.Sqrt, bias=self.eps1[0:1, 0:1]),
             reads=("srow2", "ones"), writes=("srow2",))
        p.op("vector", lambda e: e.reciprocal(self.Rrow[:, :w], self.srow2[:, :w]),
             reads=("srow2",), writes=("Rrow",))
        p.op("vector", lambda e: e.tensor_tensor(self.MRrow[:, :w], self.srow[:, :w],
                                                 self.Rrow[:, :w], ALU.mult),
             reads=("srow", "Rrow"), writes=("MRrow",))
        for th in range(nh):
            w2 = min(512, tcols - th * 512)
            sl = slice(th * 512, th * 512 + w2)
            p.op("tensor", lambda e, sl=sl, w2=w2:
                 e.matmul(self.p_R[:, :w2], self.ones_row, self.Rrow[:, sl], start=True, stop=True),
                 reads=("Rrow", "ones"), writes=("pproj0",))
            p.op("vector", lambda e, sl=sl, w2=w2: e.tensor_copy(self.Rrep[:, sl], self.p_R[:, :w2]),
                 reads=("pproj0",), writes=("Rrep",))
            p.op("tensor", lambda e, sl=sl, w2=w2:
                 e.matmul(self.p_MR[:, :w2], self.ones_row, self.MRrow[:, sl], start=True, stop=True),
                 reads=("MRrow", "ones"), writes=("pproj1",))
            p.op("vector", lambda e, sl=sl, w2=w2: e.tensor_copy(self.MRrep[:, sl], self.p_MR[:, :w2]),
                 reads=("pproj1",), writes=("MRrep",))

    def ln_pre(self, src_ap_of, src_tag_of, dst_ap_of, dst_tag_of, tcols, alias_of=None):
        p = self.p
        self.ln_stats(src_ap_of, src_tag_of, True, tcols)
        for d in range(DC):
            w = tcols
            sl = d % 2
            p.op("vector", lambda e, d=d, w=w, sl=sl: e.tensor_tensor(
                self.tln[sl][:, :w], src_ap_of(d), self.Rrep[:, :w], ALU.mult),
                 reads=(src_tag_of(d), "Rrep"), writes=("tln0", "tln1"))
            wt = (dst_tag_of(d),) if alias_of is None else (dst_tag_of(d), alias_of(d))
            p.op("vector", lambda e, d=d, w=w, sl=sl: e.tensor_tensor(
                dst_ap_of(d), self.tln[sl][:, :w], self.MRrep[:, :w], ALU.subtract),
                 reads=(f"tln{sl}", "MRrep"), writes=wt)

    def ln_post_add(self, y_ap_of, y_tag_of, y_is_f32, x_ap_of, x_tag_of, tcols):
        p = self.p
        self.ln_stats(y_ap_of, y_tag_of, y_is_f32, tcols)
        for d in range(DC):
            w = tcols
            sl = d % 2
            p.op("vector", lambda e, d=d, w=w, sl=sl: e.tensor_tensor(
                self.tln[sl][:, :w], y_ap_of(d), self.Rrep[:, :w], ALU.mult),
                 reads=(y_tag_of(d), "Rrep"), writes=("tln0", "tln1"))
            p.op("vector", lambda e, d=d, w=w, sl=sl: e.tensor_tensor(
                self.tln[sl][:, :w], self.tln[sl][:, :w], self.MRrep[:, :w], ALU.subtract),
                 reads=(f"tln{sl}", "MRrep"), writes=("tln0", "tln1"))
            p.op("vector", lambda e, d=d, w=w, sl=sl: e.tensor_tensor(
                x_ap_of(d), x_ap_of(d), self.tln[sl][:, :w], ALU.add),
                 reads=(f"tln{sl}", x_tag_of(d)), writes=(x_tag_of(d),))

    def projection(self, w_dram, act_sb, act_tag_of, act_cols, tcols,
                   out_cb, cast_dma, bias_sb=None, bias_tag=None):
        p = self.p
        nth = (tcols + 511) // 512
        for og in range(8):
            wb = self.wbuf[og % 2]
            wtag = f"wb{og % 2}"
            eng = "gpsimd" if cast_dma else "scalar"
            fns = []
            for d in range(DC):
                src = w_dram[d * 128:(d + 1) * 128, og * 256:(og + 1) * 256]
                fns.append(lambda e, s=src, d=d, wb=wb:
                           e.dma_start(out=wb[:, d * 256:(d + 1) * 256], in_=s))
            p.multi(eng, fns, reads=(), writes=(wtag,), dma=True, dsem=f"Dw{og % 2}")
            for oc2 in range(2):
                oc = og * 2 + oc2
                for th in range(nth):
                    w = min(512, tcols - th * 512)
                    pi = (oc2 * nth + th) % 2
                    ps = self.pproj[pi]
                    pstag = f"pproj{pi}"
                    fns = []
                    for d in range(DC):
                        l = wb[:, d * 256 + oc2 * 128: d * 256 + oc2 * 128 + 128]
                        r = act_sb[:, d * act_cols + th * 512: d * act_cols + th * 512 + w]
                        fns.append(lambda e, l=l, r=r, f=(d == 0), la=(d == DC - 1), ps=ps, w=w:
                                   e.matmul(ps[:, :w], l, r, start=f, stop=la))
                    reads = (wtag,) + tuple(act_tag_of(d) for d in range(DC))
                    p.multi("tensor", fns, reads=reads, writes=(pstag,))
                    out_cb(oc, th, ps[:, :w], pstag, bias_sb, bias_tag)


def evac_simple(p, dst_ap, dst_tag):
    def cb(oc, th, ps_ap, pstag, bias_sb, bias_tag):
        da = dst_ap(oc, th)
        wt = dst_tag(oc, th)
        wt = wt if isinstance(wt, tuple) else (wt,)
        if bias_sb is not None:
            bc = bias_sb[:, oc:oc + 1]
            p.op("scalar", lambda e, ps_ap=ps_ap, da=da, bc=bc:
                 e.activation(da, ps_ap, AF.Identity, bias=bc),
                 reads=(pstag, bias_tag), writes=wt)
        else:
            p.op("scalar", lambda e, ps_ap=ps_ap, da=da: e.copy(da, ps_ap),
                 reads=(pstag,), writes=wt)
    return cb



def build_launch_a():
    nc = bass.Bass("TRN2", target_bir_lowering=False)
    xT = nc.dram_tensor("xT", [D, T], F32, kind="ExternalInput").ap()
    wq0T = nc.dram_tensor("wq0T", [D, D], F32, kind="ExternalInput").ap()
    wk0T = nc.dram_tensor("wk0T", [D, D], F32, kind="ExternalInput").ap()
    wv0T = nc.dram_tensor("wv0T", [D, D], F32, kind="ExternalInput").ap()
    ccat_in = nc.dram_tensor("ccat", [128, T], F32, kind="ExternalInput").ap()
    scat_in = nc.dram_tensor("scat", [128, T], F32, kind="ExternalInput").ap()
    w0T = nc.dram_tensor("w0T", [D, D], F32, kind="ExternalInput").ap()
    w0T_rows = nc.dram_tensor("w0T_rows", [256, D], F32, kind="ExternalInput").ap()
    wgu = nc.dram_tensor("wgu", [2048, 2048], BF16, kind="ExternalInput").ap()
    wpr = nc.dram_tensor("wpr", [256, 8192], BF16, kind="ExternalInput").ap()
    ident_in = nc.dram_tensor("identf", [128, 128], F32, kind="ExternalInput").ap()

    q0 = nc.dram_tensor("q0", [D, T], F32, kind="ExternalOutput").ap()
    k0 = nc.dram_tensor("k0", [D, T], F32, kind="ExternalOutput").ap()
    v0 = nc.dram_tensor("v0", [T, D], F32, kind="ExternalOutput").ap()
    wq_hat = nc.dram_tensor("wq_hat", [256, D], BF16, kind="ExternalOutput").ap()
    wgu_hat = nc.dram_tensor("wgu_hat", [2048, 2048], BF16, kind="ExternalOutput").ap()
    wpr_hat = nc.dram_tensor("wpr_hat", [256, 8192], BF16, kind="ExternalOutput").ap()

    p = Prog(nc)
    cm = Common(p)
    lnx = p.sb("lnx", [128, DC * T], BF16)
    q_sb = p.sb("q_sb", [128, DC * T], BF16)
    k_sb = p.sb("k_sb", [128, DC * T], BF16)
    vst = p.sb("vst", [128, 512], BF16)
    vtt = p.sb("vtt", [128, 8 * 128], BF16)
    xch = [p.sb("xch0", [128, T], F32), p.sb("xch1", [128, T], F32)]
    ccat = p.sb("ccat_sb", [128, T], BF16)
    scat = p.sb("scat_sb", [128, T], BF16)
    ident_bf = p.sb("ident_bf", [128, 128], BF16)
    rt1 = p.sb("rt1", [128, T], BF16)
    rt2 = p.sb("rt2", [128, T], BF16)
    p_tr = p.ps("p_tr", [128, 128], BF16)

    qw = p.sb("qw", [128, 1024], BF16)
    qs_f = p.sb("qs_f", [128, 16], F32)
    qs_r = p.sb("qs_r", [128, 16], F32)
    qs_b = p.sb("qs_b", [128, 16], BF16)
    qt = p.sb("qt", [128, 1024], F32)
    qi = p.sb("qi", [128, 1024], I32)
    qq = p.sb("qq", [128, 1024], BF16)
    qh = p.sb("qh", [128, 1024], BF16)
    wsc_col = p.sb("wsc_col", [128, 1], F32)
    wsc_ps = cm.pproj[0][0:1, 0:1]
    wsc_rep = p.sb("wsc_rep", [128, 1], F32)
    p_rep1 = cm.pproj[1][:, 0:1]

    p.op("gpsimd", lambda e: e.dma_start(out=ident_bf[:, :], in_=ident_in),
         writes=("ident",), dma=True, dsem="Din")
    p.op("gpsimd", lambda e: e.dma_start(out=ccat[:, :], in_=ccat_in),
         writes=("ccat",), dma=True, dsem="Din")
    p.op("gpsimd", lambda e: e.dma_start(out=scat[:, :], in_=scat_in),
         writes=("scat",), dma=True, dsem="Din")

    def ternary(nrows, w_in, w_out, out_tag):
        npieces = w_in.shape[1] // 1024
        for r in range(nrows // 128):
            for pc in range(npieces):
                csl = slice(pc * 1024, pc * 1024 + 1024)
                p.op("scalar", lambda e, r=r, csl=csl, w_in=w_in:
                     e.dma_start(out=qw[:, :], in_=w_in[r * 128:(r + 1) * 128, csl]),
                     reads=(), writes=("qw",), dma=True, dsem="Dqw")
                g3 = bass.AP(qw, 0, [[1024, 128], [64, 16], [1, 64]])
                p.op("vector", lambda e, g3=g3: e.reduce_sum(qs_f[:, :], g3, axis=AX.X,
                                                             apply_absolute_value=True),
                     reads=("qw",), writes=("qs_f",))
                p.op("vector", lambda e: e.tensor_scalar(qs_f[:, :], qs_f[:, :],
                                                         1.0 / 64, 1e-8, ALU.mult, ALU.max),
                     reads=("qs_f",), writes=("qs_f",))
                p.op("vector", lambda e: e.tensor_copy(qs_b[:, :], qs_f[:, :]),
                     reads=("qs_f",), writes=("qs_b",))
                p.op("vector", lambda e: e.tensor_copy(qs_f[:, :], qs_b[:, :]),
                     reads=("qs_b",), writes=("qs_f",))
                p.op("vector", lambda e: e.reciprocal(qs_r[:, :], qs_f[:, :]),
                     reads=("qs_f",), writes=("qs_r",))
                rb = bass.AP(qs_r, 0, [[16, 128], [1, 16], [0, 64]])
                t3 = bass.AP(qt, 0, [[1024, 128], [64, 16], [1, 64]])
                p.op("vector", lambda e, g3=g3, rb=rb, t3=t3: e.tensor_tensor(t3, g3, rb, ALU.mult),
                     reads=("qw", "qs_r"), writes=("qt",))
                p.op("vector", lambda e: e.tensor_scalar(qt[:, :], qt[:, :], MAGIC, MAGIC,
                                                         ALU.add, ALU.subtract),
                     reads=("qt",), writes=("qt",))
                p.op("vector", lambda e: e.tensor_scalar(qi[:, :], qt[:, :], 0.0, None, ALU.add),
                     reads=("qt",), writes=("qi",))
                p.op("vector", lambda e: e.tensor_scalar(qq[:, :], qi[:, :], -1.0, 1.0,
                                                         ALU.max, ALU.min),
                     reads=("qi",), writes=("qq",))
                sb3 = bass.AP(qs_b, 0, [[16, 128], [1, 16], [0, 64]])
                q3 = bass.AP(qq, 0, [[1024, 128], [64, 16], [1, 64]])
                h3 = bass.AP(qh, 0, [[1024, 128], [64, 16], [1, 64]])
                p.op("vector", lambda e, q3=q3, sb3=sb3, h3=h3: e.tensor_tensor(h3, q3, sb3, ALU.mult),
                     reads=("qq", "qs_b"), writes=("qh",))
                p.op("scalar", lambda e, r=r, csl=csl, w_out=w_out:
                     e.dma_start(out=w_out[r * 128:(r + 1) * 128, csl], in_=qh[:, :]),
                     reads=("qh",), writes=(out_tag,), dma=True, dsem="Dout")

    ternary(2048, wgu, wgu_hat, "out_gu")
    ternary(256, wpr, wpr_hat, "out_pr")

    for r in range(2 * DC):
        p.op("gpsimd", lambda e, r=r: e.dma_start(
            out=qt[:, :], in_=w0T[(r // 2) * 128:(r // 2 + 1) * 128,
                                  (r % 2) * 1024:(r % 2) * 1024 + 1024]),
             writes=("qt",), dma=True, dsem="Dqt")
        p.op("vector", lambda e: e.reduce_sum(qs_f[:, 0:1], qt[:, :], axis=AX.X,
                                              apply_absolute_value=True),
             reads=("qt",), writes=("qs_f",))
        if r == 0:
            p.op("vector", lambda e: e.tensor_copy(wsc_col[:, :], qs_f[:, 0:1]),
                 reads=("qs_f",), writes=("wsc_col",))
        else:
            p.op("vector", lambda e: e.tensor_tensor(wsc_col[:, :], wsc_col[:, :],
                                                     qs_f[:, 0:1], ALU.add),
                 reads=("qs_f", "wsc_col"), writes=("wsc_col",))
    p.op("tensor", lambda e: e.matmul(wsc_ps[:, :], cm.ones_f32, wsc_col[:, :],
                                      start=True, stop=True),
         reads=("wsc_col", "ones"), writes=("pproj0",))
    p.op("vector", lambda e: e.tensor_scalar(wsc_col[0:1, 0:1], wsc_ps[:, :],
                                             1.0 / (D * D), 1e-6, ALU.mult, ALU.max),
         reads=("pproj0",), writes=("wsc_col",))
    p.op("vector", lambda e: e.reciprocal(wsc_col[0:1, 0:1], wsc_col[0:1, 0:1]),
         reads=("wsc_col",), writes=("wsc_col",))
    p.op("tensor", lambda e: e.matmul(p_rep1[:, :], cm.ones_row, wsc_col[0:1, 0:1],
                                      start=True, stop=True),
         reads=("wsc_col", "ones"), writes=("pproj1",))
    p.op("vector", lambda e: e.tensor_copy(wsc_rep[:, :], p_rep1[:, :]),
         reads=("pproj1",), writes=("wsc_rep",))
    for r in range(4):
        rr, cc_ = r // 2, (r % 2) * 1024
        p.op("gpsimd", lambda e, rr=rr, cc_=cc_: e.dma_start(
            out=qt[:, :], in_=w0T_rows[rr * 128:(rr + 1) * 128, cc_:cc_ + 1024]),
             writes=("qt",), dma=True, dsem="Dqt")
        p.op("vector", lambda e: e.tensor_scalar(qt[:, :], qt[:, :], wsc_rep[:, 0:1], None, ALU.mult),
             reads=("qt", "wsc_rep"), writes=("qt",))
        p.op("vector", lambda e: e.tensor_scalar(qt[:, :], qt[:, :], MAGIC, MAGIC,
                                                 ALU.add, ALU.subtract),
             reads=("qt",), writes=("qt",))
        p.op("vector", lambda e: e.tensor_scalar(qi[:, :], qt[:, :], 0.0, None, ALU.add),
             reads=("qt",), writes=("qi",))
        p.op("vector", lambda e: e.tensor_scalar(qq[:, :], qi[:, :], -1.0, 1.0, ALU.max, ALU.min),
             reads=("qi",), writes=("qq",))
        p.op("scalar", lambda e, rr=rr, cc_=cc_: e.dma_start(
            out=wq_hat[rr * 128:(rr + 1) * 128, cc_:cc_ + 1024], in_=qq[:, :]),
             reads=("qq",), writes=("out_wq",), dma=True, dsem="Dout")

    def x_stream(d, slot):
        p.op("sync", lambda e, d=d, slot=slot: e.dma_start(
            out=xch[slot][:, :], in_=xT[d * 128:(d + 1) * 128, :]),
            writes=(f"xch{slot}",), dma=True, dsem=f"Dx{slot}")

    for d in range(DC):
        slot = d % 2
        x_stream(d, slot)
        p.op("scalar", lambda e, slot=slot: e.activation(cm.xsq[slot][:, :], xch[slot][:, :], AF.Square),
             reads=(f"xch{slot}",), writes=(f"xsq{slot}",))
        for th in range(2):
            sl = slice(th * 512, th * 512 + 512)
            first, last = (d == 0), (d == DC - 1)
            p.op("tensor", lambda e, sl=sl, slot=slot, f=first, l=last:
                 e.matmul(cm.p_mean[:, sl], cm.ones_f32, xch[slot][:, sl], start=f, stop=l),
                 reads=(f"xch{slot}", "ones"), writes=("p_mean",) if (first or last) else ())
            p.op("tensor", lambda e, sl=sl, slot=slot, f=first, l=last:
                 e.matmul(cm.p_sq[:, sl], cm.ones_bf, cm.xsq[slot][:, sl], start=f, stop=l),
                 reads=(f"xsq{slot}", "ones"), writes=("p_sq",) if (first or last) else ())
    cm.ln_finish(T)
    for d in range(DC):
        slot = d % 2
        x_stream(d, slot)
        p.op("vector", lambda e, slot=slot: e.tensor_tensor(cm.tln[slot][:, :], xch[slot][:, :],
                                                            cm.Rrep[:, :], ALU.mult),
             reads=(f"xch{slot}", "Rrep"), writes=("tln0", "tln1"))
        p.op("vector", lambda e, d=d, slot=slot: e.tensor_tensor(
            lnx[:, d * T:(d + 1) * T], cm.tln[slot][:, :], cm.MRrep[:, :], ALU.subtract),
             reads=(f"tln{slot}", "MRrep"), writes=(f"lnx_{d}",))

    def dst_chunk(dst):
        return lambda oc, th: dst[:, oc * T + th * 512: oc * T + th * 512 + 512]

    cm.projection(wq0T, lnx, lambda d: f"lnx_{d}", T, T,
                  evac_simple(p, dst_chunk(q_sb), lambda oc, th: f"q_{oc}"), True)
    cm.projection(wk0T, lnx, lambda d: f"lnx_{d}", T, T,
                  evac_simple(p, dst_chunk(k_sb), lambda oc, th: f"k_{oc}"), True)

    def v_cb(oc, th, ps_ap, pstag, bias_sb, bias_tag):
        p.op("scalar", lambda e, ps_ap=ps_ap: e.copy(vst[:, :], ps_ap),
             reads=(pstag,), writes=("vst",))
        for tt4 in range(4):
            tt = th * 4 + tt4
            p.op("tensor", lambda e, tt4=tt4: e.transpose(
                p_tr[:, :], vst[:, tt4 * 128:(tt4 + 1) * 128], ident_bf[:, :]),
                 reads=("vst", "ident"), writes=("p_tr",))
            dst = vtt[:, tt * 128:(tt + 1) * 128]
            p.op("scalar", lambda e, dst=dst: e.copy(dst, p_tr[:, :]),
                 reads=("p_tr",), writes=("vtt",))
        if th == 1:
            dst = bass.AP(v0.tensor, oc * 128, [[D, 128], [128 * D, 8], [1, 128]])
            src_ = bass.AP(vtt, 0, [[8 * 128, 128], [128, 8], [1, 128]])
            p.op("gpsimd", lambda e, dst=dst, src_=src_: e.dma_start(out=dst, in_=src_),
                 reads=("vtt",), writes=("out_v",), dma=True, dsem="Dout")
    cm.projection(wv0T, lnx, lambda d: f"lnx_{d}", T, T, v_cb, True)

    def rope(sb, pref):
        for hc in range(DC):
            sl = slice(hc * T, (hc + 1) * T)
            rtag = f"{pref}_{hc}"
            p.op("sync", lambda e, sl=sl, sb=sb: e.dma_start(out=rt2[0:64, :], in_=sb[64:128, sl]),
                 reads=(rtag,), writes=("rt2",), dma=True, dsem="Drt")
            p.op("sync", lambda e, sl=sl, sb=sb: e.dma_start(out=rt2[64:128, :], in_=sb[0:64, sl]),
                 reads=(rtag,), writes=("rt2",), dma=True, dsem="Drt")
            p.op("vector", lambda e, sl=sl, sb=sb: e.tensor_tensor(rt1[:, :], sb[:, sl], ccat[:, :], ALU.mult),
                 reads=(rtag, "ccat"), writes=("rt1",))
            p.op("vector", lambda e: e.tensor_tensor(rt2[:, :], rt2[:, :], scat[:, :], ALU.mult),
                 reads=("rt2", "scat"), writes=("rt2",))
            p.op("vector", lambda e, sl=sl, sb=sb: e.tensor_tensor(sb[:, sl], rt1[:, :], rt2[:, :], ALU.add),
                 reads=("rt1", "rt2"), writes=(f"{pref}r_{hc}", rtag))
    rope(q_sb, "q")
    rope(k_sb, "k")

    p.op("gpsimd", lambda e: e.dma_start(out=q0.rearrange("(d p) t -> p d t", p=128), in_=q_sb[:, :]),
         reads=tuple(f"qr_{h}" for h in range(DC)), writes=("out_q",), dma=True, dsem="Dout")
    p.op("gpsimd", lambda e: e.dma_start(out=k0.rearrange("(d p) t -> p d t", p=128), in_=k_sb[:, :]),
         reads=tuple(f"kr_{h}" for h in range(DC)), writes=("out_k",), dma=True, dsem="Dout")
    p.finish(["out_q", "out_k", "out_v", "out_gu", "out_pr", "out_wq"])
    p.emit()
    return nc



def build_attention(layer):
    nc = bass.Bass("TRN2", target_bir_lowering=False)
    qh_in = nc.dram_tensor("qh", [2 * 128, S], F32, kind="ExternalInput").ap()
    kh_in = nc.dram_tensor("kh", [2 * 128, S], F32, kind="ExternalInput").ap()
    vh_in = nc.dram_tensor("vh", [2, S, 128], F32, kind="ExternalInput").ap()
    ident_in = nc.dram_tensor("identf", [128, 128], F32, kind="ExternalInput").ap()
    if layer == 0:
        masks_in = nc.dram_tensor("masks", [4 * 128, 512], F32, kind="ExternalInput").ap()
    else:
        dts_in = nc.dram_tensor("dts", [2 * 5 * 128, 512], F32, kind="ExternalInput").ap()
        btab_in = nc.dram_tensor("btab", [128, 2 * 68], F32, kind="ExternalInput").ap()
    attn_out = nc.dram_tensor("attn", [2 * 128, S], F32, kind="ExternalOutput").ap()

    p = Prog(nc)
    q_sb = p.sb("q_sb", [128, 2 * S], BF16)
    k_sb = p.sb("k_sb", [128, 2 * S], BF16)
    v_sb = p.sb("v_sb", [128, 2 * 64 * 132], BF16)
    ao_sb = p.sb("ao_sb", [128, 2 * S], BF16)
    ident_bf = p.sb("ident_bf", [128, 128], BF16)
    s2 = [p.sb("s2a", [128, 512], F32), p.sb("s2b", [128, 512], F32)]
    p4 = [p.sb("p4a", [128, 512], BF16), p.sb("p4b", [128, 512], BF16)]
    att_st = p.sb("att_st", [128, 128], BF16)
    dcol = p.sb("dcol", [128, 1], F32)
    rden = p.sb("rden", [128, 1], F32)
    if layer == 0:
        masks = p.sb("masks_sb", [128, 4 * 512], BF16)
    else:
        dts = p.sb("dts_sb", [128, 2 * 5 * 512], F32)
        btab = p.sb("btab_sb", [128, 2 * 68], F32)
    p_s = [p.ps("p_s0", [128, 512]), p.ps("p_s1", [128, 512])]
    p_av = [p.ps(f"p_av{i}", [128, 132]) for i in range(4)]
    p_tr = p.ps("p_tr", [128, 128], BF16)

    p.op("gpsimd", lambda e: e.memset(v_sb[:, :], 1.0), writes=("v",))
    p.op("gpsimd", lambda e: e.dma_start(out=ident_bf[:, :], in_=ident_in),
         writes=("ident",), dma=True, dsem="Din")
    if layer == 0:
        p.op("gpsimd", lambda e: e.dma_start(
            out=masks[:, :], in_=masks_in.rearrange("(m p) f -> p m f", p=128)),
            writes=("masks",), dma=True, dsem="Din")
    else:
        p.op("gpsimd", lambda e: e.dma_start(
            out=dts[:, :], in_=dts_in.rearrange("(m p) f -> p m f", p=128)),
            writes=("dts",), dma=True, dsem="Din")
        p.op("sync", lambda e: e.dma_start(out=btab[:, :], in_=btab_in),
             writes=("btab",), dma=True, dsem="Din")
    p.op("gpsimd", lambda e: e.dma_start(
        out=q_sb[:, :], in_=qh_in.rearrange("(s p) t -> p s t", p=128)),
        writes=("q",), dma=True, dsem="Din")
    p.op("gpsimd", lambda e: e.dma_start(
        out=k_sb[:, :], in_=kh_in.rearrange("(s p) t -> p s t", p=128)),
        writes=("k",), dma=True, dsem="Din")
    for s in range(2):
        for kg in range(8):
            src = vh_in[s, kg * 1024:(kg + 1) * 1024, :].rearrange("(kt p) c -> p kt c", p=128)
            dst = bass.AP(v_sb, s * 64 * 132 + kg * 8 * 132,
                          [[2 * 64 * 132, 128], [132, 8], [1, 128]])
            p.op("gpsimd", lambda e, src=src, dst=dst: e.dma_start(out=dst, in_=src),
                 writes=("v",), dma=True, dsem="Din")

    for slot in range(2):
        for g in range(NCH):
            nkt = 4 * (g + 1)
            qcol = slot * S + g * 512
            for kt in range(nkt):
                pi = kt % 2
                pstag = f"p_s{pi}"
                l = k_sb[:, slot * S + kt * 128: slot * S + kt * 128 + 128]
                r = q_sb[:, qcol: qcol + 512]
                p.op("tensor", lambda e, l=l, r=r, pi=pi:
                     e.matmul(p_s[pi][:, :], l, r, start=True, stop=True),
                     reads=("k", "q"), writes=(pstag,))
                if layer == 0:
                    p.op("scalar", lambda e, pi=pi: e.activation(s2[pi][:, :], p_s[pi][:, :], AF.Square),
                         reads=(pstag,), writes=(f"s2{pi}",))
                    p.op("vector", lambda e, pi=pi: e.tensor_tensor(
                        p4[pi][:, :], s2[pi][:, :], s2[pi][:, :], ALU.mult),
                         reads=(f"s2{pi}",), writes=(f"p4{pi}",))
                    if kt >= 4 * g:
                        r_ = kt - 4 * g
                        p.op("vector", lambda e, pi=pi, r_=r_: e.tensor_tensor(
                            p4[pi][:, :], p4[pi][:, :], masks[:, r_ * 512:(r_ + 1) * 512], ALU.mult),
                             reads=("masks", f"p4{pi}"), writes=(f"p4{pi}",))
                else:
                    dv = 1 + (kt - 4 * g) if kt >= 4 * g else 0
                    dsl = slice((slot * 5 + dv) * 512, (slot * 5 + dv) * 512 + 512)
                    p.op("vector", lambda e, pi=pi, dsl=dsl: e.scalar_tensor_tensor(
                        s2[pi][:, :], p_s[pi][:, :], SCALE_L1, dts[:, dsl], ALU.mult, ALU.add),
                         reads=(pstag, "dts"), writes=(f"s2{pi}",))
                    dk = 4 * g - kt
                    bcol = btab[:, slot * 68 + dk + 3: slot * 68 + dk + 4]
                    p.op("scalar", lambda e, pi=pi, bcol=bcol: e.activation(
                        p4[pi][:, :], s2[pi][:, :], AF.Exp, bias=bcol),
                         reads=(f"s2{pi}", "btab"), writes=(f"p4{pi}",))
                first = (kt == 0)
                last = (kt == nkt - 1)
                vrhs = bass.AP(v_sb, slot * 64 * 132 + kt * 132,
                               [[2 * 64 * 132, 128], [1, 129]])
                lhs0 = p4[pi][:, 0:128]
                if first:
                    p.op("tensor", lambda e, lhs0=lhs0, vrhs=vrhs:
                         e.matmul(p_av[0][:, :129], lhs0, vrhs, start=True, stop=False),
                         reads=(f"p4{pi}", "v"),
                         writes=("p_av0", "p_av1", "p_av2", "p_av3"), noinc=True)
                else:
                    p.op("tensor", lambda e, lhs0=lhs0, vrhs=vrhs, last=last:
                         e.matmul(p_av[0][:, :129], lhs0, vrhs, start=False, stop=last),
                         reads=(f"p4{pi}",), writes=(), noinc=True)
                for qs in (1, 2):
                    lh = p4[pi][:, qs * 128:(qs + 1) * 128]
                    p.raw("tensor", lambda e, lh=lh, vrhs=vrhs, first=first, last=last, qs=qs:
                          e.matmul(p_av[qs][:, :129], lh, vrhs, start=first, stop=last))
                lh3 = p4[pi][:, 384:512]
                if last:
                    p.op("tensor", lambda e, lh3=lh3, vrhs=vrhs, first=first:
                         e.matmul(p_av[3][:, :129], lh3, vrhs, start=first, stop=True),
                         reads=(f"p4{pi}", "v"),
                         writes=("p_av0", "p_av1", "p_av2", "p_av3"))
                else:
                    p.op("tensor", lambda e, lh3=lh3, vrhs=vrhs, first=first:
                         e.matmul(p_av[3][:, :129], lh3, vrhs, start=first, stop=False),
                         reads=(f"p4{pi}", "v"), writes=())
            for qs in range(4):
                ptag = "p_av3"
                if layer == 0:
                    p.op("vector", lambda e, qs=qs: e.tensor_scalar(
                        dcol[:, :], p_av[qs][:, 128:129], 1e-6, None, ALU.max),
                         reads=(ptag,), writes=("dcol",))
                else:
                    p.op("vector", lambda e, qs=qs: e.tensor_copy(dcol[:, :], p_av[qs][:, 128:129]),
                         reads=(ptag,), writes=("dcol",))
                p.op("vector", lambda e: e.reciprocal(rden[:, :], dcol[:, :]),
                     reads=("dcol",), writes=("rden",))
                p.op("vector", lambda e, qs=qs: e.tensor_scalar(
                    att_st[:, :], p_av[qs][:, 0:128], rden[:, 0:1], None, ALU.mult),
                     reads=(ptag, "rden"), writes=("att_st",))
                p.op("tensor", lambda e: e.transpose(p_tr[:, :], att_st[:, :], ident_bf[:, :]),
                     reads=("att_st", "ident"), writes=("p_tr",))
                dst = ao_sb[:, slot * S + g * 512 + qs * 128: slot * S + g * 512 + qs * 128 + 128]
                p.op("scalar", lambda e, dst=dst: e.copy(dst, p_tr[:, :]),
                     reads=("p_tr",), writes=("ao",))

    p.op("gpsimd", lambda e: e.dma_start(
        out=attn_out.rearrange("(s p) t -> p s t", p=128), in_=ao_sb[:, :]),
        reads=("ao",), writes=("out_a",), dma=True, dsem="Dout")
    p.finish(["out_a"])
    p.emit()
    return nc



def build_launch_c():
    nc = bass.Bass("TRN2", target_bir_lowering=False)
    xT = nc.dram_tensor("xT", [D, T], F32, kind="ExternalInput").ap()
    attn0 = nc.dram_tensor("attn0", [D, T], F32, kind="ExternalInput").ap()
    wo0T = nc.dram_tensor("wo0T", [D, D], F32, kind="ExternalInput").ap()
    wq_hatT = nc.dram_tensor("wq_hatT", [D, D], BF16, kind="ExternalInput").ap()
    wq1 = nc.dram_tensor("wq1", [D, D], F32, kind="ExternalInput").ap()
    wk1 = nc.dram_tensor("wk1", [D, D], F32, kind="ExternalInput").ap()
    wv1 = nc.dram_tensor("wv1", [D, D], F32, kind="ExternalInput").ap()
    biases = nc.dram_tensor("biases", [128, 3 * DC], F32, kind="ExternalInput").ap()
    ident_in = nc.dram_tensor("identf", [128, 128], F32, kind="ExternalInput").ap()

    q1 = nc.dram_tensor("q1", [D, T], F32, kind="ExternalOutput").ap()
    k1 = nc.dram_tensor("k1", [D, T], F32, kind="ExternalOutput").ap()
    v1 = nc.dram_tensor("v1", [T, D], F32, kind="ExternalOutput").ap()
    x2 = nc.dram_tensor("x2", [D, T], F32, kind="ExternalOutput").ap()

    p = Prog(nc)
    cm = Common(p)
    x_sb = p.sb("x_sb", [128, DC * T], F32)
    lnb = p.sb("lnb", [128, DC * T], BF16)
    y_sb = p.sb("y_sb", [128, DC * T], BF16)
    vst = p.sb("vst", [128, 512], BF16)
    vtt = p.sb("vtt", [128, 8 * 128], BF16)
    ident_bf = p.sb("ident_bf", [128, 128], BF16)
    bias_sb = p.sb("bias_sb", [128, 3 * DC], F32)
    p_tr = p.ps("p_tr", [128, 128], BF16)
    amax = p.sb("amax", [128, 8], F32)
    mtmp = p.sb("mtmp", [128, 1], F32)
    rqrow = cm.Rrow
    rqrep = p.sb("rqrep", [128, T], F32)
    p_rq = cm.pproj[0]
    qi = p.sb("qi", [128, T], I32)
    ident_f32 = p.sb("ident_f32", [128, 128], F32)
    rq8 = p.sb("rq8", [8, 128], F32)

    p.op("gpsimd", lambda e: e.dma_start(out=ident_bf[:, :], in_=ident_in),
         writes=("ident",), dma=True, dsem="Din")
    p.op("sync", lambda e: e.dma_start(out=ident_f32[:, :], in_=ident_in),
         writes=("identf32",), dma=True, dsem="Din")
    p.op("sync", lambda e: e.dma_start(out=bias_sb[:, :], in_=biases),
         writes=("biasb",), dma=True, dsem="Din")
    p.op("sync", lambda e: e.dma_start(out=x_sb[:, :], in_=xT.rearrange("(d p) t -> p d t", p=128)),
         writes=tuple(f"x_{d}" for d in range(DC)), dma=True, dsem="Din")
    p.op("gpsimd", lambda e: e.dma_start(
        out=lnb[:, :], in_=attn0.rearrange("(d p) t -> p d t", p=128)),
        writes=tuple(f"a0_{d}" for d in range(DC)), dma=True, dsem="Din")

    def ych(oc, th):
        return y_sb[:, oc * T + th * 512: oc * T + th * 512 + 512]

    cm.projection(wo0T, lnb, lambda d: f"a0_{d}", T, T,
                  evac_simple(p, ych, lambda oc, th: f"y_{oc}"), True)
    cm.ln_post_add(lambda d: y_sb[:, d * T:(d + 1) * T], lambda d: f"y_{d}", False,
                   lambda d: x_sb[:, d * T:(d + 1) * T], lambda d: f"x_{d}", T)
    cm.ln_pre(lambda d: x_sb[:, d * T:(d + 1) * T], lambda d: f"x_{d}",
              lambda d: lnb[:, d * T:(d + 1) * T], lambda d: f"ln_{d}", T,
              alias_of=lambda d: f"a0_{d}")

    for tt in range(8):
        for d in range(DC):
            src = lnb[:, d * T + tt * 128: d * T + tt * 128 + 128]
            p.op("tensor", lambda e, src=src: e.transpose(p_tr[:, :], src, ident_bf[:, :]),
                 reads=(f"ln_{d}", "ident"), writes=("p_tr",))
            if d == 0:
                p.op("vector", lambda e, tt=tt: e.reduce_max(
                    amax[:, tt:tt + 1], p_tr[:, :], axis=AX.X, apply_absolute_value=True),
                     reads=("p_tr",), writes=("amax",))
            else:
                p.op("vector", lambda e: e.reduce_max(
                    mtmp[:, :], p_tr[:, :], axis=AX.X, apply_absolute_value=True),
                     reads=("p_tr",), writes=("mtmp",))
                p.op("vector", lambda e, tt=tt: e.tensor_tensor(
                    amax[:, tt:tt + 1], amax[:, tt:tt + 1], mtmp[:, :], ALU.max),
                     reads=("mtmp", "amax"), writes=("amax",))
    p.op("vector", lambda e: e.tensor_scalar(amax[:, :], amax[:, :], 1e-8, None, ALU.max),
         reads=("amax",), writes=("amax",))
    p.op("vector", lambda e: e.reciprocal(amax[:, :], amax[:, :]),
         reads=("amax",), writes=("amax",))
    p.op("vector", lambda e: e.tensor_scalar(amax[:, :], amax[:, :], 127.0, None, ALU.mult),
         reads=("amax",), writes=("amax",))
    p.op("tensor", lambda e: e.transpose(cm.pproj[0][0:8, 0:128], amax[:, :], ident_f32[:, :]),
         reads=("amax", "identf32"), writes=("pproj0",))
    p.op("vector", lambda e: e.tensor_copy(rq8[:, :], cm.pproj[0][0:8, 0:128]),
         reads=("pproj0",), writes=("rq8",))
    rq_dst = bass.AP(rqrow.tensor, 0, [[T, 1], [1, T]])
    p.op("sync", lambda e: e.dma_start(out=rq_dst, in_=rq8[:, :]),
         reads=("rq8",), writes=("Rrow",), dma=True, dsem="Drq")
    for th in range(2):
        sl = slice(th * 512, th * 512 + 512)
        p.op("tensor", lambda e, sl=sl: e.matmul(p_rq[:, :512], cm.ones_row, rqrow[:, sl],
                                                 start=True, stop=True),
             reads=("Rrow", "ones"), writes=("pproj0",))
        p.op("vector", lambda e, sl=sl: e.tensor_copy(rqrep[:, sl], p_rq[:, :512]),
             reads=("pproj0",), writes=("rqrep",))
    for d in range(DC):
        sl = d % 2
        p.op("vector", lambda e, d=d, sl=sl: e.tensor_tensor(
            cm.tln[sl][:, :], lnb[:, d * T:(d + 1) * T], rqrep[:, :], ALU.mult),
             reads=(f"ln_{d}", "rqrep"), writes=("tln0", "tln1"))
        p.op("vector", lambda e, sl=sl: e.tensor_scalar(cm.tln[sl][:, :], cm.tln[sl][:, :],
                                                        MAGIC, MAGIC, ALU.add, ALU.subtract),
             reads=(f"tln{sl}",), writes=("tln0", "tln1"))
        p.op("vector", lambda e, sl=sl: e.tensor_scalar(qi[:, :], cm.tln[sl][:, :], 0.0, None, ALU.add),
             reads=(f"tln{sl}",), writes=("qi",))
        p.op("vector", lambda e, d=d: e.tensor_scalar(
            lnb[:, d * T:(d + 1) * T], qi[:, :], 0.0, None, ALU.add),
             reads=("qi",), writes=(f"xq_{d}", f"ln_{d}"))
    cm.projection(wq_hatT, lnb, lambda d: f"xq_{d}", T, T,
                  evac_simple(p, ych, lambda oc, th: f"y_{oc}"), False)
    cm.ln_post_add(lambda d: y_sb[:, d * T:(d + 1) * T], lambda d: f"y_{d}", False,
                   lambda d: x_sb[:, d * T:(d + 1) * T], lambda d: f"x_{d}", T)
    p.op("sync", lambda e: e.dma_start(out=x2.rearrange("(d p) t -> p d t", p=128), in_=x_sb[:, :]),
         reads=tuple(f"x_{d}" for d in range(DC)), writes=("out_x2",), dma=True, dsem="Dout")
    cm.ln_pre(lambda d: x_sb[:, d * T:(d + 1) * T], lambda d: f"x_{d}",
              lambda d: lnb[:, d * T:(d + 1) * T], lambda d: f"ln2_{d}", T,
              alias_of=lambda d: f"xq_{d}")

    def qdst(oc, th):
        return y_sb[:, oc * T + th * 512: oc * T + th * 512 + 512]
    cm.projection(wq1, lnb, lambda d: f"ln2_{d}", T, T,
                  evac_simple(p, qdst, lambda oc, th: (f"qq_{oc}", f"y_{oc}")), True,
                  bias_sb=bias_sb[:, 0:DC], bias_tag="biasb")
    p.op("gpsimd", lambda e: e.dma_start(out=q1.rearrange("(d p) t -> p d t", p=128), in_=y_sb[:, :]),
         reads=tuple(f"qq_{d}" for d in range(DC)), writes=("out_q1",), dma=True, dsem="Dout")
    cm.projection(wk1, lnb, lambda d: f"ln2_{d}", T, T,
                  evac_simple(p, qdst, lambda oc, th: (f"kk_{oc}", f"qq_{oc}")), True,
                  bias_sb=bias_sb[:, DC:2 * DC], bias_tag="biasb")
    p.op("gpsimd", lambda e: e.dma_start(out=k1.rearrange("(d p) t -> p d t", p=128), in_=y_sb[:, :]),
         reads=tuple(f"kk_{d}" for d in range(DC)), writes=("out_k1",), dma=True, dsem="Dout")

    def v_cb(oc, th, ps_ap, pstag, bias_sb_, bias_tag_):
        bc = bias_sb_[:, oc:oc + 1]
        p.op("scalar", lambda e, ps_ap=ps_ap, bc=bc: e.activation(vst[:, :], ps_ap, AF.Identity, bias=bc),
             reads=(pstag, bias_tag_), writes=("vst",))
        for tt4 in range(4):
            tt = th * 4 + tt4
            p.op("tensor", lambda e, tt4=tt4: e.transpose(
                p_tr[:, :], vst[:, tt4 * 128:(tt4 + 1) * 128], ident_bf[:, :]),
                 reads=("vst", "ident"), writes=("p_tr",))
            dst = vtt[:, tt * 128:(tt + 1) * 128]
            p.op("scalar", lambda e, dst=dst: e.copy(dst, p_tr[:, :]),
                 reads=("p_tr",), writes=("vtt",))
        if th == 1:
            dst = bass.AP(v1.tensor, oc * 128, [[D, 128], [128 * D, 8], [1, 128]])
            src_ = bass.AP(vtt, 0, [[8 * 128, 128], [128, 8], [1, 128]])
            p.op("gpsimd", lambda e, dst=dst, src_=src_: e.dma_start(out=dst, in_=src_),
                 reads=("vtt",), writes=("out_v1",), dma=True, dsem="Dout")
    cm.projection(wv1, lnb, lambda d: f"ln2_{d}", T, T, v_cb, True,
                  bias_sb=bias_sb[:, 2 * DC:3 * DC], bias_tag="biasb")
    p.finish(["out_q1", "out_k1", "out_v1", "out_x2"])
    p.emit()
    return nc



def build_launch_e():
    nc = bass.Bass("TRN2", target_bir_lowering=False)
    x2 = nc.dram_tensor("x2", [D, T], F32, kind="ExternalInput").ap()
    attn1 = nc.dram_tensor("attn1", [D, T], F32, kind="ExternalInput").ap()
    wo1 = nc.dram_tensor("wo1", [D, D], F32, kind="ExternalInput").ap()
    bo_in = nc.dram_tensor("bo", [128, DC], F32, kind="ExternalInput").ap()
    wguT = nc.dram_tensor("wguT", [D, 16384], BF16, kind="ExternalInput").ap()
    wprT = nc.dram_tensor("wprT", [8192, D], BF16, kind="ExternalInput").ap()
    xout = nc.dram_tensor("xout", [D, T], F32, kind="ExternalOutput").ap()

    p = Prog(nc)
    cm = Common(p)
    x_sb = p.sb("x_sb", [128, DC * T], F32)
    u1 = p.sb("u1", [128, DC * T], BF16)
    yl = p.sb("yl", [128, DC * T], BF16)
    y3 = p.sb("y3", [128, DC * 512], BF16)
    bo_sb = p.sb("bo_sb", [128, DC], F32)
    gs = p.sb("gs", [128, 512], BF16)
    _pg = p.ps("p_g", [128, 512])
    _pu = p.ps("p_u", [128, 512])
    p_g = [_pg, _pg]
    p_u = [_pu, _pu]

    p.op("sync", lambda e: e.dma_start(out=bo_sb[:, :], in_=bo_in), writes=("bo",), dma=True, dsem="Din")
    p.op("sync", lambda e: e.dma_start(out=x_sb[:, :], in_=x2.rearrange("(d p) t -> p d t", p=128)),
         writes=tuple(f"x_{d}" for d in range(DC)), dma=True, dsem="Din")
    p.op("gpsimd", lambda e: e.dma_start(
        out=u1[:, :], in_=attn1.rearrange("(d p) t -> p d t", p=128)),
        writes=tuple(f"a1_{d}" for d in range(DC)), dma=True, dsem="Din")

    def ych(oc, th):
        return yl[:, oc * T + th * 512: oc * T + th * 512 + 512]
    cm.projection(wo1, u1, lambda d: f"a1_{d}", T, T,
                  evac_simple(p, ych, lambda oc, th: f"y_{oc}"), True,
                  bias_sb=bo_sb, bias_tag="bo")
    cm.ln_post_add(lambda d: yl[:, d * T:(d + 1) * T], lambda d: f"y_{d}", False,
                   lambda d: x_sb[:, d * T:(d + 1) * T], lambda d: f"x_{d}", T)
    cm.ln_pre(lambda d: x_sb[:, d * T:(d + 1) * T], lambda d: f"x_{d}",
              lambda d: yl[:, d * T:(d + 1) * T], lambda d: f"ln3_{d}", T,
              alias_of=lambda d: f"y_{d}")

    for th in range(2):
        for hh in range(2):
            for hc in range(32):
                hrow = hh * HHALF + hc * 128
                wb = cm.wbuf[hc % 2]
                wtag = f"wb{hc % 2}"
                fns = []
                for d in range(DC):
                    gsrc = wguT[d * 128:(d + 1) * 128, hrow:hrow + 128]
                    usrc = wguT[d * 128:(d + 1) * 128, 8192 + hrow: 8192 + hrow + 128]
                    fns.append(lambda e, s=gsrc, d=d, wb=wb: e.dma_start(
                        out=wb[:, d * 256:d * 256 + 128], in_=s))
                    fns.append(lambda e, s=usrc, d=d, wb=wb: e.dma_start(
                        out=wb[:, d * 256 + 128:d * 256 + 256], in_=s))
                p.multi("scalar", fns, reads=(), writes=(wtag,), dma=True, dsem=f"Dw{hc % 2}")
                pi = hc % 2
                fng, fnu = [], []
                for d in range(DC):
                    lg = wb[:, d * 256:d * 256 + 128]
                    lu = wb[:, d * 256 + 128:d * 256 + 256]
                    r = yl[:, d * T + th * 512: d * T + th * 512 + 512]
                    fng.append(lambda e, l=lg, r=r, f=(d == 0), la=(d == DC - 1), pi=pi:
                               e.matmul(p_g[pi][:, :], l, r, start=f, stop=la))
                    fnu.append(lambda e, l=lu, r=r, f=(d == 0), la=(d == DC - 1), pi=pi:
                               e.matmul(p_u[pi][:, :], l, r, start=f, stop=la))
                reads = (wtag,) + tuple(f"ln3_{d}" for d in range(DC))
                p.multi("tensor", fng, reads=reads, writes=("p_g0", "p_g1"))
                p.multi("tensor", fnu, reads=reads, writes=("p_u0", "p_u1"))
                p.op("scalar", lambda e, pi=pi: e.activation(gs[:, :], p_g[pi][:, :], AF.Silu),
                     reads=(f"p_g{pi}",), writes=("gs",))
                hdst = u1[:, hc * 512:(hc + 1) * 512]
                p.op("vector", lambda e, pi=pi, hdst=hdst: e.tensor_tensor(
                    hdst, gs[:, :], p_u[pi][:, :], ALU.mult),
                     reads=("gs", f"p_u{pi}"), writes=(f"h_{hc}", f"a1_{hc // 2}"))
            for oc in range(DC):
                wb = cm.wbuf[oc % 2]
                wtag = f"wb{oc % 2}"
                src = wprT[hh * HHALF:(hh + 1) * HHALF, oc * 128:(oc + 1) * 128]
                src_r = src.rearrange("(hc p) c -> p hc c", p=128)
                fns = []
                for piece in range(4):
                    fns.append(lambda e, piece=piece, wb=wb, src_r=src_r: e.dma_start(
                        out=wb[:, piece * 1024:(piece + 1) * 1024],
                        in_=src_r[:, piece * 8:(piece + 1) * 8, :]))
                p.multi("sync", fns, reads=(), writes=(wtag,), dma=True, dsem=f"Dw{oc % 2}")
                pi = oc % 2
                fns = []
                for hc in range(32):
                    l = wb[:, hc * 128:(hc + 1) * 128]
                    r = u1[:, hc * 512:(hc + 1) * 512]
                    fns.append(lambda e, l=l, r=r, f=(hc == 0), la=(hc == 31), pi=pi:
                               e.matmul(cm.pproj[pi][:, :], l, r, start=f, stop=la))
                reads = (wtag,) + tuple(f"h_{hc}" for hc in range(32))
                p.multi("tensor", fns, reads=reads, writes=(f"pproj{pi}",))
                ydst = y3[:, oc * 512:(oc + 1) * 512]
                if hh == 0:
                    p.op("scalar", lambda e, pi=pi, ydst=ydst: e.copy(ydst, cm.pproj[pi][:, :]),
                         reads=(f"pproj{pi}",), writes=(f"y3_{oc}",))
                else:
                    p.op("vector", lambda e, pi=pi, ydst=ydst: e.tensor_tensor(
                        ydst, ydst, cm.pproj[pi][:, :], ALU.add),
                         reads=(f"pproj{pi}", f"y3_{oc}"), writes=(f"y3_{oc}",))
        cm.ln_post_add(lambda d: y3[:, d * 512:(d + 1) * 512], lambda d: f"y3_{d}", False,
                       lambda d, th=th: x_sb[:, d * T + th * 512: d * T + th * 512 + 512],
                       lambda d: f"x_{d}", 512)

    p.op("sync", lambda e: e.dma_start(out=xout.rearrange("(d p) t -> p d t", p=128), in_=x_sb[:, :]),
         reads=tuple(f"x_{d}" for d in range(DC)), writes=("out_x",), dma=True, dsem="Dout")
    p.finish(["out_x"])
    p.emit()
    return nc



def host_tables():
    half = 64
    theta = 10000.0 ** (-np.arange(half, dtype=np.float64) * 2.0 / 128.0)
    pos = np.arange(S, dtype=np.float64)
    ang = pos[:, None] * theta[None, :]
    cos = np.cos(ang).astype(np.float32)
    sin = np.sin(ang).astype(np.float32)
    ident = np.eye(128, dtype=np.float32)
    ppp = np.arange(128)[:, None]
    fff = np.arange(512)[None, :]
    masks = np.stack([(fff >= ppp + 128 * r).astype(np.float32) for r in range(4)], 0)
    start = 2.0 ** (-8.0 / H)
    slopes = np.array([start * start ** i for i in range(H)], dtype=np.float64)
    d0 = (ppp - fff).astype(np.float64)
    return cos, sin, ident, masks, slopes, d0


_CACHE = {}


def _ncs():
    if "a" not in _CACHE:
        _CACHE["a"] = build_launch_a()
        _CACHE["b0"] = build_attention(0)
        _CACHE["c"] = build_launch_c()
        _CACHE["b1"] = build_attention(1)
        _CACHE["e"] = build_launch_e()
    return _CACHE


def kernel(L0_attn_x, L0_attn_Wq, L0_attn_Wk, L0_attn_Wv, L0_attn_Wo,
           L0_ffn_weight, L1_attn_wq, L1_attn_bq, L1_attn_wk, L1_attn_bk,
           L1_attn_wv, L1_attn_bv, L1_attn_wo, L1_attn_bo,
           L1_ffn_w_gate_up, L1_ffn_w_proj):
    cos, sin, ident, masks, slopes, d0 = host_tables()
    ncs = _ncs()
    x = np.ascontiguousarray(np.asarray(L0_attn_x, dtype=np.float32).reshape(S, D))
    wq0T = np.ascontiguousarray(np.asarray(L0_attn_Wq, np.float32).T)
    wk0T = np.ascontiguousarray(np.asarray(L0_attn_Wk, np.float32).T)
    wv0T = np.ascontiguousarray(np.asarray(L0_attn_Wv, np.float32).T)
    wo0T = np.ascontiguousarray(np.asarray(L0_attn_Wo, np.float32).T)
    w0T = np.ascontiguousarray(np.asarray(L0_ffn_weight, np.float32).T)
    wgu_bf = np.asarray(L1_ffn_w_gate_up, np.float32).astype(BF)
    wpr_bf = np.asarray(L1_ffn_w_proj, np.float32).astype(BF)

    tok_rows = {c: np.r_[np.arange(512 * c, 512 * c + 512),
                         np.arange(512 * (15 - c), 512 * (15 - c) + 512)]
                for c in range(NC)}

    in_a = []
    for c in range(NC):
        rows = tok_rows[c]
        cc = cos[rows].T
        ss = sin[rows].T
        in_a.append({
            "xT": np.ascontiguousarray(x[rows].T),
            "wq0T": wq0T, "wk0T": wk0T, "wv0T": wv0T,
            "ccat": np.ascontiguousarray(np.concatenate([cc, cc], 0)),
            "scat": np.ascontiguousarray(np.concatenate([-ss, ss], 0)),
            "w0T": w0T,
            "w0T_rows": np.ascontiguousarray(w0T[256 * c:256 * c + 256]),
            "wgu": np.ascontiguousarray(wgu_bf[2048 * c:2048 * (c + 1)]),
            "wpr": np.ascontiguousarray(wpr_bf[256 * c:256 * (c + 1)]),
            "identf": ident,
        })
    ra = run_bass_kernel_spmd(ncs["a"], in_a, core_ids=list(range(NC))).results

    q0f = np.zeros((H, 128, S), np.float32)
    k0f = np.zeros((H, 128, S), np.float32)
    v0f = np.zeros((H, S, 128), np.float32)
    for c in range(NC):
        rows = tok_rows[c]
        q0c = np.asarray(ra[c]["q0"])
        k0c = np.asarray(ra[c]["k0"])
        v0c = np.asarray(ra[c]["v0"])
        for h in range(H):
            q0f[h][:, rows] = q0c[h * 128:(h + 1) * 128]
            k0f[h][:, rows] = k0c[h * 128:(h + 1) * 128]
            v0f[h][rows, :] = v0c[:, h * 128:(h + 1) * 128]
    wq_hatT = np.ascontiguousarray(
        np.concatenate([np.asarray(ra[c]["wq_hat"]) for c in range(NC)], 0))
    wgu_hat = np.concatenate([np.asarray(ra[c]["wgu_hat"]) for c in range(NC)], 0)
    wpr_hat = np.concatenate([np.asarray(ra[c]["wpr_hat"]) for c in range(NC)], 0)
    wguT = np.ascontiguousarray(wgu_hat.T)
    wprT = np.ascontiguousarray(wpr_hat.T)

    in_b = []
    for c in range(NC):
        hs = heads_of(c)
        in_b.append({
            "qh": np.ascontiguousarray(np.concatenate([q0f[h] for h in hs], 0)),
            "kh": np.ascontiguousarray(np.concatenate([k0f[h] for h in hs], 0)),
            "vh": np.ascontiguousarray(np.stack([v0f[h] for h in hs], 0)),
            "identf": ident,
            "masks": np.ascontiguousarray(masks.reshape(4 * 128, 512)),
        })
    rb = run_bass_kernel_spmd(ncs["b0"], in_b, core_ids=list(range(NC))).results
    attn0f = np.zeros((H, 128, S), np.float32)
    for c in range(NC):
        a = np.asarray(rb[c]["attn"])
        hs = heads_of(c)
        attn0f[hs[0]] = a[0:128]
        attn0f[hs[1]] = a[128:256]

    bq = np.asarray(L1_attn_bq, np.float32).reshape(DC, 128).T
    bk = np.asarray(L1_attn_bk, np.float32).reshape(DC, 128).T
    bv = np.asarray(L1_attn_bv, np.float32).reshape(DC, 128).T
    biases = np.ascontiguousarray(np.concatenate([bq, bk, bv], 1))
    in_c = []
    for c in range(NC):
        rows = tok_rows[c]
        in_c.append({
            "xT": in_a[c]["xT"],
            "attn0": np.ascontiguousarray(attn0f[:, :, rows].reshape(D, T)),
            "wo0T": wo0T, "wq_hatT": wq_hatT,
            "wq1": np.asarray(L1_attn_wq, np.float32),
            "wk1": np.asarray(L1_attn_wk, np.float32),
            "wv1": np.asarray(L1_attn_wv, np.float32),
            "biases": biases, "identf": ident,
        })
    rc = run_bass_kernel_spmd(ncs["c"], in_c, core_ids=list(range(NC))).results
    q1f = np.zeros((H, 128, S), np.float32)
    k1f = np.zeros((H, 128, S), np.float32)
    v1f = np.zeros((H, S, 128), np.float32)
    x2f = {}
    for c in range(NC):
        rows = tok_rows[c]
        q1c = np.asarray(rc[c]["q1"])
        k1c = np.asarray(rc[c]["k1"])
        v1c = np.asarray(rc[c]["v1"])
        x2f[c] = np.asarray(rc[c]["x2"])
        for h in range(H):
            q1f[h][:, rows] = q1c[h * 128:(h + 1) * 128]
            k1f[h][:, rows] = k1c[h * 128:(h + 1) * 128]
            v1f[h][rows, :] = v1c[:, h * 128:(h + 1) * 128]

    in_d = []
    for c in range(NC):
        hs = heads_of(c)
        dts = []
        for h in hs:
            dh = slopes[h] * d0
            vars_ = [dh]
            for r in range(4):
                m = (np.arange(512)[None, :] < np.arange(128)[:, None] + 128 * r)
                vars_.append(np.where(m, NEG_BIG, dh))
            dts.append(np.stack(vars_, 0))
        dts = np.ascontiguousarray(np.stack(dts, 0).astype(np.float32).reshape(2 * 5 * 128, 512))
        btab = np.zeros((128, 2 * 68), np.float32)
        for si, h in enumerate(hs):
            for dk in range(-3, 64):
                btab[:, si * 68 + dk + 3] = np.float32(-slopes[h] * 128.0 * dk)
        in_d.append({
            "qh": np.ascontiguousarray(np.concatenate([q1f[h] for h in hs], 0)),
            "kh": np.ascontiguousarray(np.concatenate([k1f[h] for h in hs], 0)),
            "vh": np.ascontiguousarray(np.stack([v1f[h] for h in hs], 0)),
            "identf": ident, "dts": dts, "btab": btab,
        })
    rd = run_bass_kernel_spmd(ncs["b1"], in_d, core_ids=list(range(NC))).results
    attn1f = np.zeros((H, 128, S), np.float32)
    for c in range(NC):
        a = np.asarray(rd[c]["attn"])
        hs = heads_of(c)
        attn1f[hs[0]] = a[0:128]
        attn1f[hs[1]] = a[128:256]

    bo = np.ascontiguousarray(np.asarray(L1_attn_bo, np.float32).reshape(DC, 128).T)
    in_e = []
    for c in range(NC):
        rows = tok_rows[c]
        in_e.append({
            "x2": x2f[c],
            "attn1": np.ascontiguousarray(attn1f[:, :, rows].reshape(D, T)),
            "wo1": np.asarray(L1_attn_wo, np.float32),
            "bo": bo, "wguT": wguT, "wprT": wprT,
        })
    re_ = run_bass_kernel_spmd(ncs["e"], in_e, core_ids=list(range(NC))).results

    out = np.zeros((S, D), np.float32)
    for c in range(NC):
        rows = tok_rows[c]
        out[rows] = np.asarray(re_[c]["xout"]).T
    return out.reshape(1, S, D)


# version 03
# speedup vs baseline: 1.5141x; 1.5141x over previous
import contextlib
import numpy as np
import ml_dtypes
import concourse.bass as bass
import concourse.mybir as mybir
from concourse.bass_utils import run_bass_kernel_spmd

F32 = mybir.dt.float32
BF16 = mybir.dt.bfloat16
I32 = mybir.dt.int32
AF = mybir.ActivationFunctionType
ALU = mybir.AluOpType
AX = mybir.AxisListType

NC = 8
S = 8192
D = 2048
H = 16
CH = 512
T = 1024
NCH = 16
DC = 16
HHALF = 4096
EPS_LN = 1e-5
SCALE_L1 = float(1.0 / np.sqrt(np.float32(128.0)))
NEG_BIG = -1e30
MAGIC = 1.5 * 2.0 ** 23
BF = ml_dtypes.bfloat16
ENGINES = ("sync", "scalar", "vector", "tensor", "gpsimd")
CSEM = {"scalar": "Ts", "vector": "Tv", "tensor": "Tt", "gpsimd": "Tg"}
QSUF = {"sync": "Y", "scalar": "S", "gpsimd": "G"}
CUMU = ("Din", "Dout")


def heads_of(core):
    return (core, core + 8)


class Prog:

    def __init__(self, nc):
        self.nc = nc
        self.es = contextlib.ExitStack()
        self.ops = {e: [] for e in ENGINES}
        self.sems = {}
        self.cnt = {}
        self.lastwait = {}
        self.track = {}

    def sem(self, name):
        if name not in self.sems:
            self.sems[name] = self.es.enter_context(self.nc.semaphore(name))
            self.cnt[name] = 0
        return self.sems[name]

    def sb(self, name, shape, dtype):
        return self.es.enter_context(self.nc.sbuf_tensor(name, shape, dtype))

    def ps(self, name, shape, dtype=F32):
        return self.es.enter_context(self.nc.psum_tensor(name, shape, dtype))

    def _emit_waits(self, eng, toks):
        need = {}
        for (sname, val) in toks:
            if val > need.get(sname, 0):
                need[sname] = val
        out = []
        for sname, val in need.items():
            key = (eng, sname)
            if self.lastwait.get(key, 0) < val:
                self.lastwait[key] = val
                out.append((self.sem(sname), val))
        return out

    def _dep_toks(self, reads, writes):
        toks = []
        for t in reads:
            rec = self.track.get(t)
            if rec and rec[0] is not None:
                toks.append(rec[0])
        for t in writes:
            rec = self.track.get(t)
            if rec:
                if rec[0] is not None:
                    toks.append(rec[0])
                toks.extend(rec[1].items())
        toks = [(s, self.cnt[s]) if s.startswith(CUMU) else (s, v) for (s, v) in toks]
        return toks

    def op(self, eng, fn, reads=(), writes=(), dma=False, noinc=False, dsem=None):
        waits = self._emit_waits(eng, self._dep_toks(reads, writes))
        sname = (dsem + QSUF[eng]) if dma else CSEM[eng]
        sem = self.sem(sname)
        if noinc:
            tok = None
        else:
            self.cnt[sname] += 16 if dma else 1
            tok = (sname, self.cnt[sname])
        inc_by = 16 if dma else 1

        def closure(e, waits=waits, fn=fn, sem=sem, inc_by=inc_by, noinc=noinc):
            for (s, v) in waits:
                e.wait_ge(s, v)
            ins = fn(e)
            if not noinc:
                ins.then_inc(sem, inc_by)

        self.ops[eng].append(closure)
        if tok is not None:
            for t in writes:
                self.track[t] = [tok, {}]
            for t in reads:
                rec = self.track.setdefault(t, [None, {}])
                if tok[1] > rec[1].get(tok[0], 0):
                    rec[1][tok[0]] = tok[1]
        return tok

    def raw(self, eng, fn):
        self.ops[eng].append(lambda e, fn=fn: fn(e))

    def multi(self, eng, fns, reads=(), writes=(), dma=False, dsem=None):
        n = len(fns)
        if n == 1:
            return self.op(eng, fns[0], reads=reads, writes=writes, dma=dma, dsem=dsem)
        if dma:
            self.op(eng, fns[0], reads=reads, writes=writes, dma=True, dsem=dsem)
            sname = dsem + QSUF[eng]
            sem = self.sem(sname)
            for f in fns[1:]:
                self.cnt[sname] += 16
                self.ops[eng].append(lambda e, f=f, sem=sem: f(e).then_inc(sem, 16))
            tok = (sname, self.cnt[sname])
            for t in writes:
                self.track[t] = [tok, {}]
            return tok
        self.op(eng, fns[0], reads=reads, writes=writes, dma=dma, noinc=True)
        for f in fns[1:-1]:
            self.raw(eng, f)
        return self.op(eng, fns[-1], reads=reads, writes=writes, dma=dma)

    def finish(self, out_tags):
        toks = []
        for t in out_tags:
            rec = self.track.get(t)
            if rec and rec[0] is not None:
                toks.append(rec[0])
        waits = self._emit_waits("sync", toks)

        def closure(e, waits=waits):
            for (s, v) in waits:
                e.wait_ge(s, v)
        self.ops["sync"].append(closure)

    def emit(self):
        with self.nc.Block() as blk:
            for e in ENGINES:
                if not self.ops[e]:
                    continue
                dec = getattr(blk, e)

                def body(engobj, e=e):
                    for f in self.ops[e]:
                        f(engobj)
                dec(body)



class Common:
    def __init__(self, p, wbuf_cols=DC * 256):
        self.p = p
        self.xsq = [p.sb("xsq0", [128, T], BF16), p.sb("xsq1", [128, T], BF16)]
        self.p_stat = p.ps("p_stat", [1, 2 * T])
        self.p_mean = self.p_stat[:, 0:T]
        self.p_sq = self.p_stat[:, T:2 * T]
        self.srow = p.sb("statsrow", [1, T], F32)[:, :]
        self.srow2 = p.sb("statsrow2", [1, T], F32)[:, :]
        self.Rrow = p.sb("rrow", [1, T], F32)[:, :]
        self.MRrow = p.sb("mrrow", [1, T], F32)[:, :]
        self.Rrep = p.sb("Rrep", [128, T], F32)
        self.MRrep = p.sb("MRrep", [128, T], F32)
        self.ones_f32 = p.sb("ones_f32", [128, 1], F32)[:, :]
        self.ones_bf = p.sb("ones_bf", [128, 1], BF16)[:, :]
        self.ones_row = p.sb("ones_row", [1, 128], F32)[:, :]
        self.eps1 = p.sb("eps1", [1, 1], F32)
        _tl = p.sb("tlnbuf", [128, T], F32)
        self.tln = [_tl, _tl]
        self.wbuf = [p.sb("wbufA", [128, wbuf_cols], BF16),
                     p.sb("wbufB", [128, wbuf_cols], BF16)]
        self.pproj = [p.ps("pprojA", [128, 512]), p.ps("pprojB", [128, 512])]
        self.p_R = self.pproj[0]
        self.p_MR = self.pproj[1]
        p.op("gpsimd", lambda e: e.memset(self.ones_f32[:, :], 1.0), writes=("ones",))
        p.op("gpsimd", lambda e: e.memset(self.ones_bf[:, :], 1.0), writes=("ones",))
        p.op("gpsimd", lambda e: e.memset(self.ones_row[:, :], 1.0), writes=("ones",))
        p.op("gpsimd", lambda e: e.memset(self.eps1[:, :], EPS_LN), writes=("ones",))

    def ln_stats(self, src_ap_of, src_tag_of, src_is_f32, tcols):
        p = self.p
        nh = (tcols + 511) // 512
        for d in range(DC):
            src = src_ap_of(d)
            stag = src_tag_of(d)
            sq = self.xsq[d % 2]
            p.op("scalar", lambda e, s=src, o=sq, w=tcols: e.activation(o[:, :w], s, AF.Square),
                 reads=(stag,), writes=(f"xsq{d % 2}",))
            for th in range(nh):
                w = min(512, tcols - th * 512)
                sl = slice(th * 512, th * 512 + w)
                first, last = (d == 0), (d == DC - 1)
                ones = self.ones_f32 if src_is_f32 else self.ones_bf
                p.op("tensor", lambda e, s=src, sl=sl, f=first, l=last, o=ones:
                     e.matmul(self.p_mean[:, sl], o, s[:, sl], start=f, stop=l),
                     reads=(stag, "ones"),
                     writes=("p_mean",) if (first or last) else ())
                p.op("tensor", lambda e, sl=sl, q=sq, f=first, l=last:
                     e.matmul(self.p_sq[:, sl], self.ones_bf, q[:, sl], start=f, stop=l),
                     reads=(f"xsq{d % 2}", "ones"),
                     writes=("p_sq",) if (first or last) else ())
        self.ln_finish(tcols)

    def ln_finish(self, tcols):
        p = self.p
        w = tcols
        nh = (tcols + 511) // 512
        p.op("vector", lambda e: e.tensor_scalar(self.srow[:, :w], self.p_mean[:, :w],
                                                 1.0 / D, None, ALU.mult),
             reads=("p_mean",), writes=("srow",))
        p.op("vector", lambda e: e.tensor_tensor(self.srow2[:, :w], self.srow[:, :w],
                                                 self.srow[:, :w], ALU.mult),
             reads=("srow",), writes=("srow2",))
        p.op("vector", lambda e: e.scalar_tensor_tensor(self.srow2[:, :w], self.p_sq[:, :w],
                                                        1.0 / D, self.srow2[:, :w],
                                                        ALU.mult, ALU.subtract),
             reads=("p_sq", "srow2"), writes=("srow2",))
        p.op("scalar", lambda e: e.activation(self.srow2[:, :w], self.srow2[:, :w],
                                              AF.Sqrt, bias=self.eps1[0:1, 0:1]),
             reads=("srow2", "ones"), writes=("srow2",))
        p.op("vector", lambda e: e.reciprocal(self.Rrow[:, :w], self.srow2[:, :w]),
             reads=("srow2",), writes=("Rrow",))
        p.op("vector", lambda e: e.tensor_tensor(self.MRrow[:, :w], self.srow[:, :w],
                                                 self.Rrow[:, :w], ALU.mult),
             reads=("srow", "Rrow"), writes=("MRrow",))
        for th in range(nh):
            w2 = min(512, tcols - th * 512)
            sl = slice(th * 512, th * 512 + w2)
            p.op("tensor", lambda e, sl=sl, w2=w2:
                 e.matmul(self.p_R[:, :w2], self.ones_row, self.Rrow[:, sl], start=True, stop=True),
                 reads=("Rrow", "ones"), writes=("pproj0",))
            p.op("vector", lambda e, sl=sl, w2=w2: e.tensor_copy(self.Rrep[:, sl], self.p_R[:, :w2]),
                 reads=("pproj0",), writes=("Rrep",))
            p.op("tensor", lambda e, sl=sl, w2=w2:
                 e.matmul(self.p_MR[:, :w2], self.ones_row, self.MRrow[:, sl], start=True, stop=True),
                 reads=("MRrow", "ones"), writes=("pproj1",))
            p.op("vector", lambda e, sl=sl, w2=w2: e.tensor_copy(self.MRrep[:, sl], self.p_MR[:, :w2]),
                 reads=("pproj1",), writes=("MRrep",))

    def ln_pre(self, src_ap_of, src_tag_of, dst_ap_of, dst_tag_of, tcols, alias_of=None):
        p = self.p
        self.ln_stats(src_ap_of, src_tag_of, True, tcols)
        for d in range(DC):
            w = tcols
            sl = d % 2
            p.op("vector", lambda e, d=d, w=w, sl=sl: e.tensor_tensor(
                self.tln[sl][:, :w], src_ap_of(d), self.Rrep[:, :w], ALU.mult),
                 reads=(src_tag_of(d), "Rrep"), writes=("tln0", "tln1"))
            wt = (dst_tag_of(d),) if alias_of is None else (dst_tag_of(d), alias_of(d))
            p.op("vector", lambda e, d=d, w=w, sl=sl: e.tensor_tensor(
                dst_ap_of(d), self.tln[sl][:, :w], self.MRrep[:, :w], ALU.subtract),
                 reads=(f"tln{sl}", "MRrep"), writes=wt)

    def ln_post_add(self, y_ap_of, y_tag_of, y_is_f32, x_ap_of, x_tag_of, tcols):
        p = self.p
        self.ln_stats(y_ap_of, y_tag_of, y_is_f32, tcols)
        for d in range(DC):
            w = tcols
            sl = d % 2
            p.op("vector", lambda e, d=d, w=w, sl=sl: e.tensor_tensor(
                self.tln[sl][:, :w], y_ap_of(d), self.Rrep[:, :w], ALU.mult),
                 reads=(y_tag_of(d), "Rrep"), writes=("tln0", "tln1"))
            p.op("vector", lambda e, d=d, w=w, sl=sl: e.tensor_tensor(
                self.tln[sl][:, :w], self.tln[sl][:, :w], self.MRrep[:, :w], ALU.subtract),
                 reads=(f"tln{sl}", "MRrep"), writes=("tln0", "tln1"))
            p.op("vector", lambda e, d=d, w=w, sl=sl: e.tensor_tensor(
                x_ap_of(d), x_ap_of(d), self.tln[sl][:, :w], ALU.add),
                 reads=(f"tln{sl}", x_tag_of(d)), writes=(x_tag_of(d),))

    def projection(self, w_dram, act_sb, act_tag_of, act_cols, tcols,
                   out_cb, cast_dma, bias_sb=None, bias_tag=None):
        p = self.p
        nth = (tcols + 511) // 512
        for og in range(8):
            wb = self.wbuf[og % 2]
            wtag = f"wb{og % 2}"
            eng = "gpsimd" if cast_dma else "scalar"
            wsrc = w_dram[:, og * 256:(og + 1) * 256].rearrange("(d p) c -> p d c", p=128)
            p.op(eng, lambda e, s=wsrc, wb=wb: e.dma_start(out=wb[:, :], in_=s),
                 reads=(), writes=(wtag,), dma=True, dsem=f"Dw{og % 2}")
            for oc2 in range(2):
                oc = og * 2 + oc2
                for th in range(nth):
                    w = min(512, tcols - th * 512)
                    pi = (oc2 * nth + th) % 2
                    ps = self.pproj[pi]
                    pstag = f"pproj{pi}"
                    fns = []
                    for d in range(DC):
                        l = wb[:, d * 256 + oc2 * 128: d * 256 + oc2 * 128 + 128]
                        r = act_sb[:, d * act_cols + th * 512: d * act_cols + th * 512 + w]
                        fns.append(lambda e, l=l, r=r, f=(d == 0), la=(d == DC - 1), ps=ps, w=w:
                                   e.matmul(ps[:, :w], l, r, start=f, stop=la))
                    reads = (wtag,) + tuple(act_tag_of(d) for d in range(DC))
                    p.multi("tensor", fns, reads=reads, writes=(pstag,))
                    out_cb(oc, th, ps[:, :w], pstag, bias_sb, bias_tag)


def evac_simple(p, dst_ap, dst_tag):
    def cb(oc, th, ps_ap, pstag, bias_sb, bias_tag):
        da = dst_ap(oc, th)
        wt = dst_tag(oc, th)
        wt = wt if isinstance(wt, tuple) else (wt,)
        if bias_sb is not None:
            bc = bias_sb[:, oc:oc + 1]
            p.op("scalar", lambda e, ps_ap=ps_ap, da=da, bc=bc:
                 e.activation(da, ps_ap, AF.Identity, bias=bc),
                 reads=(pstag, bias_tag), writes=wt)
        else:
            p.op("scalar", lambda e, ps_ap=ps_ap, da=da: e.copy(da, ps_ap),
                 reads=(pstag,), writes=wt)
    return cb



def build_launch_a():
    nc = bass.Bass("TRN2", target_bir_lowering=False)
    xT = nc.dram_tensor("xT", [D, T], F32, kind="ExternalInput").ap()
    wq0T = nc.dram_tensor("wq0T", [D, D], F32, kind="ExternalInput").ap()
    wk0T = nc.dram_tensor("wk0T", [D, D], F32, kind="ExternalInput").ap()
    wv0T = nc.dram_tensor("wv0T", [D, D], F32, kind="ExternalInput").ap()
    ccat_in = nc.dram_tensor("ccat", [128, T], F32, kind="ExternalInput").ap()
    scat_in = nc.dram_tensor("scat", [128, T], F32, kind="ExternalInput").ap()
    w0T = nc.dram_tensor("w0T", [D, D], F32, kind="ExternalInput").ap()
    w0T_rows = nc.dram_tensor("w0T_rows", [256, D], F32, kind="ExternalInput").ap()
    wgu = nc.dram_tensor("wgu", [2048, 2048], BF16, kind="ExternalInput").ap()
    wpr = nc.dram_tensor("wpr", [256, 8192], BF16, kind="ExternalInput").ap()
    ident_in = nc.dram_tensor("identf", [128, 128], F32, kind="ExternalInput").ap()

    q0 = nc.dram_tensor("q0", [D, T], F32, kind="ExternalOutput").ap()
    k0 = nc.dram_tensor("k0", [D, T], F32, kind="ExternalOutput").ap()
    v0 = nc.dram_tensor("v0", [T, D], F32, kind="ExternalOutput").ap()
    wq_hat = nc.dram_tensor("wq_hat", [256, D], BF16, kind="ExternalOutput").ap()
    wgu_hat = nc.dram_tensor("wgu_hat", [2048, 2048], BF16, kind="ExternalOutput").ap()
    wpr_hat = nc.dram_tensor("wpr_hat", [256, 8192], BF16, kind="ExternalOutput").ap()

    p = Prog(nc)
    cm = Common(p)
    lnx = p.sb("lnx", [128, DC * T], BF16)
    q_sb = p.sb("q_sb", [128, DC * T], BF16)
    k_sb = p.sb("k_sb", [128, DC * T], BF16)
    vst = p.sb("vst", [128, 512], BF16)
    vtt = p.sb("vtt", [128, 8 * 128], BF16)
    xch = [p.sb("xch0", [128, T], F32), p.sb("xch1", [128, T], F32)]
    ccat = p.sb("ccat_sb", [128, T], BF16)
    scat = p.sb("scat_sb", [128, T], BF16)
    ident_bf = p.sb("ident_bf", [128, 128], BF16)
    rt1 = p.sb("rt1", [128, T], BF16)
    rt2 = p.sb("rt2", [128, T], BF16)
    p_tr = p.ps("p_tr", [128, 128], BF16)

    qw = p.sb("qw", [128, 1024], BF16)
    qs_f = p.sb("qs_f", [128, 16], F32)
    qs_r = p.sb("qs_r", [128, 16], F32)
    qs_b = p.sb("qs_b", [128, 16], BF16)
    qt = p.sb("qt", [128, 1024], F32)
    qi = p.sb("qi", [128, 1024], I32)
    qq = p.sb("qq", [128, 1024], BF16)
    qh = p.sb("qh", [128, 1024], BF16)
    wsc_col = p.sb("wsc_col", [128, 1], F32)
    wsc_ps = cm.pproj[0][0:1, 0:1]
    wsc_rep = p.sb("wsc_rep", [128, 1], F32)
    p_rep1 = cm.pproj[1][:, 0:1]

    p.op("gpsimd", lambda e: e.dma_start(out=ident_bf[:, :], in_=ident_in),
         writes=("ident",), dma=True, dsem="Din")
    p.op("gpsimd", lambda e: e.dma_start(out=ccat[:, :], in_=ccat_in),
         writes=("ccat",), dma=True, dsem="Din")
    p.op("gpsimd", lambda e: e.dma_start(out=scat[:, :], in_=scat_in),
         writes=("scat",), dma=True, dsem="Din")

    def ternary(nrows, w_in, w_out, out_tag):
        npieces = w_in.shape[1] // 1024
        for r in range(nrows // 128):
            for pc in range(npieces):
                csl = slice(pc * 1024, pc * 1024 + 1024)
                p.op("scalar", lambda e, r=r, csl=csl, w_in=w_in:
                     e.dma_start(out=qw[:, :], in_=w_in[r * 128:(r + 1) * 128, csl]),
                     reads=(), writes=("qw",), dma=True, dsem="Dqw")
                g3 = bass.AP(qw, 0, [[1024, 128], [64, 16], [1, 64]])
                p.op("vector", lambda e, g3=g3: e.reduce_sum(qs_f[:, :], g3, axis=AX.X,
                                                             apply_absolute_value=True),
                     reads=("qw",), writes=("qs_f",))
                p.op("vector", lambda e: e.tensor_scalar(qs_f[:, :], qs_f[:, :],
                                                         1.0 / 64, 1e-8, ALU.mult, ALU.max),
                     reads=("qs_f",), writes=("qs_f",))
                p.op("vector", lambda e: e.tensor_copy(qs_b[:, :], qs_f[:, :]),
                     reads=("qs_f",), writes=("qs_b",))
                p.op("vector", lambda e: e.tensor_copy(qs_f[:, :], qs_b[:, :]),
                     reads=("qs_b",), writes=("qs_f",))
                p.op("vector", lambda e: e.reciprocal(qs_r[:, :], qs_f[:, :]),
                     reads=("qs_f",), writes=("qs_r",))
                rb = bass.AP(qs_r, 0, [[16, 128], [1, 16], [0, 64]])
                t3 = bass.AP(qt, 0, [[1024, 128], [64, 16], [1, 64]])
                p.op("vector", lambda e, g3=g3, rb=rb, t3=t3: e.tensor_tensor(t3, g3, rb, ALU.mult),
                     reads=("qw", "qs_r"), writes=("qt",))
                p.op("vector", lambda e: e.tensor_scalar(qt[:, :], qt[:, :], MAGIC, MAGIC,
                                                         ALU.add, ALU.subtract),
                     reads=("qt",), writes=("qt",))
                p.op("vector", lambda e: e.tensor_scalar(qi[:, :], qt[:, :], 0.0, None, ALU.add),
                     reads=("qt",), writes=("qi",))
                p.op("vector", lambda e: e.tensor_scalar(qq[:, :], qi[:, :], -1.0, 1.0,
                                                         ALU.max, ALU.min),
                     reads=("qi",), writes=("qq",))
                sb3 = bass.AP(qs_b, 0, [[16, 128], [1, 16], [0, 64]])
                q3 = bass.AP(qq, 0, [[1024, 128], [64, 16], [1, 64]])
                h3 = bass.AP(qh, 0, [[1024, 128], [64, 16], [1, 64]])
                p.op("vector", lambda e, q3=q3, sb3=sb3, h3=h3: e.tensor_tensor(h3, q3, sb3, ALU.mult),
                     reads=("qq", "qs_b"), writes=("qh",))
                p.op("scalar", lambda e, r=r, csl=csl, w_out=w_out:
                     e.dma_start(out=w_out[r * 128:(r + 1) * 128, csl], in_=qh[:, :]),
                     reads=("qh",), writes=(out_tag,), dma=True, dsem="Dout")

    ternary(2048, wgu, wgu_hat, "out_gu")
    ternary(256, wpr, wpr_hat, "out_pr")

    for r in range(2 * DC):
        p.op("gpsimd", lambda e, r=r: e.dma_start(
            out=qt[:, :], in_=w0T[(r // 2) * 128:(r // 2 + 1) * 128,
                                  (r % 2) * 1024:(r % 2) * 1024 + 1024]),
             writes=("qt",), dma=True, dsem="Dqt")
        p.op("vector", lambda e: e.reduce_sum(qs_f[:, 0:1], qt[:, :], axis=AX.X,
                                              apply_absolute_value=True),
             reads=("qt",), writes=("qs_f",))
        if r == 0:
            p.op("vector", lambda e: e.tensor_copy(wsc_col[:, :], qs_f[:, 0:1]),
                 reads=("qs_f",), writes=("wsc_col",))
        else:
            p.op("vector", lambda e: e.tensor_tensor(wsc_col[:, :], wsc_col[:, :],
                                                     qs_f[:, 0:1], ALU.add),
                 reads=("qs_f", "wsc_col"), writes=("wsc_col",))
    p.op("tensor", lambda e: e.matmul(wsc_ps[:, :], cm.ones_f32, wsc_col[:, :],
                                      start=True, stop=True),
         reads=("wsc_col", "ones"), writes=("pproj0",))
    p.op("vector", lambda e: e.tensor_scalar(wsc_col[0:1, 0:1], wsc_ps[:, :],
                                             1.0 / (D * D), 1e-6, ALU.mult, ALU.max),
         reads=("pproj0",), writes=("wsc_col",))
    p.op("vector", lambda e: e.reciprocal(wsc_col[0:1, 0:1], wsc_col[0:1, 0:1]),
         reads=("wsc_col",), writes=("wsc_col",))
    p.op("tensor", lambda e: e.matmul(p_rep1[:, :], cm.ones_row, wsc_col[0:1, 0:1],
                                      start=True, stop=True),
         reads=("wsc_col", "ones"), writes=("pproj1",))
    p.op("vector", lambda e: e.tensor_copy(wsc_rep[:, :], p_rep1[:, :]),
         reads=("pproj1",), writes=("wsc_rep",))
    for r in range(4):
        rr, cc_ = r // 2, (r % 2) * 1024
        p.op("gpsimd", lambda e, rr=rr, cc_=cc_: e.dma_start(
            out=qt[:, :], in_=w0T_rows[rr * 128:(rr + 1) * 128, cc_:cc_ + 1024]),
             writes=("qt",), dma=True, dsem="Dqt")
        p.op("vector", lambda e: e.tensor_scalar(qt[:, :], qt[:, :], wsc_rep[:, 0:1], None, ALU.mult),
             reads=("qt", "wsc_rep"), writes=("qt",))
        p.op("vector", lambda e: e.tensor_scalar(qt[:, :], qt[:, :], MAGIC, MAGIC,
                                                 ALU.add, ALU.subtract),
             reads=("qt",), writes=("qt",))
        p.op("vector", lambda e: e.tensor_scalar(qi[:, :], qt[:, :], 0.0, None, ALU.add),
             reads=("qt",), writes=("qi",))
        p.op("vector", lambda e: e.tensor_scalar(qq[:, :], qi[:, :], -1.0, 1.0, ALU.max, ALU.min),
             reads=("qi",), writes=("qq",))
        p.op("scalar", lambda e, rr=rr, cc_=cc_: e.dma_start(
            out=wq_hat[rr * 128:(rr + 1) * 128, cc_:cc_ + 1024], in_=qq[:, :]),
             reads=("qq",), writes=("out_wq",), dma=True, dsem="Dout")

    def x_stream(d, slot):
        p.op("sync", lambda e, d=d, slot=slot: e.dma_start(
            out=xch[slot][:, :], in_=xT[d * 128:(d + 1) * 128, :]),
            writes=(f"xch{slot}",), dma=True, dsem=f"Dx{slot}")

    for d in range(DC):
        slot = d % 2
        x_stream(d, slot)
        p.op("scalar", lambda e, slot=slot: e.activation(cm.xsq[slot][:, :], xch[slot][:, :], AF.Square),
             reads=(f"xch{slot}",), writes=(f"xsq{slot}",))
        for th in range(2):
            sl = slice(th * 512, th * 512 + 512)
            first, last = (d == 0), (d == DC - 1)
            p.op("tensor", lambda e, sl=sl, slot=slot, f=first, l=last:
                 e.matmul(cm.p_mean[:, sl], cm.ones_f32, xch[slot][:, sl], start=f, stop=l),
                 reads=(f"xch{slot}", "ones"), writes=("p_mean",) if (first or last) else ())
            p.op("tensor", lambda e, sl=sl, slot=slot, f=first, l=last:
                 e.matmul(cm.p_sq[:, sl], cm.ones_bf, cm.xsq[slot][:, sl], start=f, stop=l),
                 reads=(f"xsq{slot}", "ones"), writes=("p_sq",) if (first or last) else ())
    cm.ln_finish(T)
    for d in range(DC):
        slot = d % 2
        x_stream(d, slot)
        p.op("vector", lambda e, slot=slot: e.tensor_tensor(cm.tln[slot][:, :], xch[slot][:, :],
                                                            cm.Rrep[:, :], ALU.mult),
             reads=(f"xch{slot}", "Rrep"), writes=("tln0", "tln1"))
        p.op("vector", lambda e, d=d, slot=slot: e.tensor_tensor(
            lnx[:, d * T:(d + 1) * T], cm.tln[slot][:, :], cm.MRrep[:, :], ALU.subtract),
             reads=(f"tln{slot}", "MRrep"), writes=(f"lnx_{d}",))

    def dst_chunk(dst):
        return lambda oc, th: dst[:, oc * T + th * 512: oc * T + th * 512 + 512]

    cm.projection(wq0T, lnx, lambda d: f"lnx_{d}", T, T,
                  evac_simple(p, dst_chunk(q_sb), lambda oc, th: f"q_{oc}"), True)
    cm.projection(wk0T, lnx, lambda d: f"lnx_{d}", T, T,
                  evac_simple(p, dst_chunk(k_sb), lambda oc, th: f"k_{oc}"), True)

    def v_cb(oc, th, ps_ap, pstag, bias_sb, bias_tag):
        p.op("scalar", lambda e, ps_ap=ps_ap: e.copy(vst[:, :], ps_ap),
             reads=(pstag,), writes=("vst",))
        for tt4 in range(4):
            tt = th * 4 + tt4
            p.op("tensor", lambda e, tt4=tt4: e.transpose(
                p_tr[:, :], vst[:, tt4 * 128:(tt4 + 1) * 128], ident_bf[:, :]),
                 reads=("vst", "ident"), writes=("p_tr",))
            dst = vtt[:, tt * 128:(tt + 1) * 128]
            p.op("scalar", lambda e, dst=dst: e.copy(dst, p_tr[:, :]),
                 reads=("p_tr",), writes=("vtt",))
        if th == 1:
            dst = bass.AP(v0.tensor, oc * 128, [[D, 128], [128 * D, 8], [1, 128]])
            src_ = bass.AP(vtt, 0, [[8 * 128, 128], [128, 8], [1, 128]])
            p.op("gpsimd", lambda e, dst=dst, src_=src_: e.dma_start(out=dst, in_=src_),
                 reads=("vtt",), writes=("out_v",), dma=True, dsem="Dout")
    cm.projection(wv0T, lnx, lambda d: f"lnx_{d}", T, T, v_cb, True)

    def rope(sb, pref):
        for hc in range(DC):
            sl = slice(hc * T, (hc + 1) * T)
            rtag = f"{pref}_{hc}"
            p.op("sync", lambda e, sl=sl, sb=sb: e.dma_start(out=rt2[0:64, :], in_=sb[64:128, sl]),
                 reads=(rtag,), writes=("rt2",), dma=True, dsem="Drt")
            p.op("sync", lambda e, sl=sl, sb=sb: e.dma_start(out=rt2[64:128, :], in_=sb[0:64, sl]),
                 reads=(rtag,), writes=("rt2",), dma=True, dsem="Drt")
            p.op("vector", lambda e, sl=sl, sb=sb: e.tensor_tensor(rt1[:, :], sb[:, sl], ccat[:, :], ALU.mult),
                 reads=(rtag, "ccat"), writes=("rt1",))
            p.op("vector", lambda e: e.tensor_tensor(rt2[:, :], rt2[:, :], scat[:, :], ALU.mult),
                 reads=("rt2", "scat"), writes=("rt2",))
            p.op("vector", lambda e, sl=sl, sb=sb: e.tensor_tensor(sb[:, sl], rt1[:, :], rt2[:, :], ALU.add),
                 reads=("rt1", "rt2"), writes=(f"{pref}r_{hc}", rtag))
    rope(q_sb, "q")
    rope(k_sb, "k")

    p.op("gpsimd", lambda e: e.dma_start(out=q0.rearrange("(d p) t -> p d t", p=128), in_=q_sb[:, :]),
         reads=tuple(f"qr_{h}" for h in range(DC)), writes=("out_q",), dma=True, dsem="Dout")
    p.op("gpsimd", lambda e: e.dma_start(out=k0.rearrange("(d p) t -> p d t", p=128), in_=k_sb[:, :]),
         reads=tuple(f"kr_{h}" for h in range(DC)), writes=("out_k",), dma=True, dsem="Dout")
    p.finish(["out_q", "out_k", "out_v", "out_gu", "out_pr", "out_wq"])
    p.emit()
    return nc



def build_attention(layer):
    nc = bass.Bass("TRN2", target_bir_lowering=False)
    qh_in = nc.dram_tensor("qh", [2 * 128, S], F32, kind="ExternalInput").ap()
    kh_in = nc.dram_tensor("kh", [2 * 128, S], F32, kind="ExternalInput").ap()
    vh_in = nc.dram_tensor("vh", [2, S, 128], F32, kind="ExternalInput").ap()
    ident_in = nc.dram_tensor("identf", [128, 128], F32, kind="ExternalInput").ap()
    if layer == 0:
        masks_in = nc.dram_tensor("masks", [4 * 128, 512], F32, kind="ExternalInput").ap()
    else:
        dts_in = nc.dram_tensor("dts", [2 * 5 * 128, 512], F32, kind="ExternalInput").ap()
        btab_in = nc.dram_tensor("btab", [128, 2 * 68], F32, kind="ExternalInput").ap()
    attn_out = nc.dram_tensor("attn", [2 * 128, S], F32, kind="ExternalOutput").ap()

    p = Prog(nc)
    q_sb = p.sb("q_sb", [128, 2 * S], BF16)
    k_sb = p.sb("k_sb", [128, 2 * S], BF16)
    v_sb = p.sb("v_sb", [128, 2 * 64 * 132], BF16)
    ao_sb = p.sb("ao_sb", [128, 2 * S], BF16)
    ident_bf = p.sb("ident_bf", [128, 128], BF16)
    s2 = [p.sb("s2a", [128, 512], F32), p.sb("s2b", [128, 512], F32)]
    p4 = [p.sb("p4a", [128, 512], BF16), p.sb("p4b", [128, 512], BF16)]
    att_st = p.sb("att_st", [128, 128], BF16)
    dcol = p.sb("dcol", [128, 1], F32)
    rden = p.sb("rden", [128, 1], F32)
    if layer == 0:
        masks = p.sb("masks_sb", [128, 4 * 512], BF16)
    else:
        dts = p.sb("dts_sb", [128, 2 * 5 * 512], F32)
        btab = p.sb("btab_sb", [128, 2 * 68], F32)
    p_s = [p.ps("p_s0", [128, 512]), p.ps("p_s1", [128, 512])]
    p_av = [p.ps(f"p_av{i}", [128, 132]) for i in range(4)]
    p_tr = p.ps("p_tr", [128, 128], BF16)

    p.op("gpsimd", lambda e: e.memset(v_sb[:, :], 1.0), writes=("v",))
    p.op("gpsimd", lambda e: e.dma_start(out=ident_bf[:, :], in_=ident_in),
         writes=("ident",), dma=True, dsem="Din")
    if layer == 0:
        p.op("gpsimd", lambda e: e.dma_start(
            out=masks[:, :], in_=masks_in.rearrange("(m p) f -> p m f", p=128)),
            writes=("masks",), dma=True, dsem="Din")
    else:
        p.op("gpsimd", lambda e: e.dma_start(
            out=dts[:, :], in_=dts_in.rearrange("(m p) f -> p m f", p=128)),
            writes=("dts",), dma=True, dsem="Din")
        p.op("sync", lambda e: e.dma_start(out=btab[:, :], in_=btab_in),
             writes=("btab",), dma=True, dsem="Din")
    p.op("gpsimd", lambda e: e.dma_start(
        out=q_sb[:, :], in_=qh_in.rearrange("(s p) t -> p s t", p=128)),
        writes=("q",), dma=True, dsem="Din")
    p.op("gpsimd", lambda e: e.dma_start(
        out=k_sb[:, :], in_=kh_in.rearrange("(s p) t -> p s t", p=128)),
        writes=("k",), dma=True, dsem="Din")
    for s in range(2):
        for kg in range(8):
            src = vh_in[s, kg * 1024:(kg + 1) * 1024, :].rearrange("(kt p) c -> p kt c", p=128)
            dst = bass.AP(v_sb, s * 64 * 132 + kg * 8 * 132,
                          [[2 * 64 * 132, 128], [132, 8], [1, 128]])
            p.op("gpsimd", lambda e, src=src, dst=dst: e.dma_start(out=dst, in_=src),
                 writes=("v",), dma=True, dsem="Din")

    for slot in range(2):
        for g in range(NCH):
            nkt = 4 * (g + 1)
            qcol = slot * S + g * 512

            def emit_qk(kt, slot=slot, qcol=qcol):
                pi = kt % 2
                l = k_sb[:, slot * S + kt * 128: slot * S + kt * 128 + 128]
                r = q_sb[:, qcol: qcol + 512]
                p.op("tensor", lambda e, l=l, r=r, pi=pi:
                     e.matmul(p_s[pi][:, :], l, r, start=True, stop=True),
                     reads=("k", "q"), writes=(f"p_s{pi}",))

            def emit_proc(kt, slot=slot, g=g, nkt=nkt):
                pi = kt % 2
                pstag = f"p_s{pi}"
                if layer == 0:
                    p.op("scalar", lambda e, pi=pi: e.activation(s2[pi][:, :], p_s[pi][:, :], AF.Square),
                         reads=(pstag,), writes=(f"s2{pi}",))
                    p.op("vector", lambda e, pi=pi: e.tensor_tensor(
                        p4[pi][:, :], s2[pi][:, :], s2[pi][:, :], ALU.mult),
                         reads=(f"s2{pi}",), writes=(f"p4{pi}",))
                    if kt >= 4 * g:
                        r_ = kt - 4 * g
                        p.op("vector", lambda e, pi=pi, r_=r_: e.tensor_tensor(
                            p4[pi][:, :], p4[pi][:, :], masks[:, r_ * 512:(r_ + 1) * 512], ALU.mult),
                             reads=("masks", f"p4{pi}"), writes=(f"p4{pi}",))
                else:
                    dv = 1 + (kt - 4 * g) if kt >= 4 * g else 0
                    dsl = slice((slot * 5 + dv) * 512, (slot * 5 + dv) * 512 + 512)
                    p.op("vector", lambda e, pi=pi, dsl=dsl: e.scalar_tensor_tensor(
                        s2[pi][:, :], p_s[pi][:, :], SCALE_L1, dts[:, dsl], ALU.mult, ALU.add),
                         reads=(pstag, "dts"), writes=(f"s2{pi}",))
                    dk = 4 * g - kt
                    bcol = btab[:, slot * 68 + dk + 3: slot * 68 + dk + 4]
                    p.op("scalar", lambda e, pi=pi, bcol=bcol: e.activation(
                        p4[pi][:, :], s2[pi][:, :], AF.Exp, bias=bcol),
                         reads=(f"s2{pi}", "btab"), writes=(f"p4{pi}",))
                first = (kt == 0)
                last = (kt == nkt - 1)
                vrhs = bass.AP(v_sb, slot * 64 * 132 + kt * 132,
                               [[2 * 64 * 132, 128], [1, 129]])
                lhs0 = p4[pi][:, 0:128]
                if first:
                    p.op("tensor", lambda e, lhs0=lhs0, vrhs=vrhs, last=last:
                         e.matmul(p_av[0][:, :129], lhs0, vrhs, start=True, stop=last),
                         reads=(f"p4{pi}", "v"),
                         writes=("p_av0", "p_av1", "p_av2", "p_av3"), noinc=True)
                else:
                    p.op("tensor", lambda e, lhs0=lhs0, vrhs=vrhs, last=last:
                         e.matmul(p_av[0][:, :129], lhs0, vrhs, start=False, stop=last),
                         reads=(f"p4{pi}",), writes=(), noinc=True)
                for qs in (1, 2):
                    lh = p4[pi][:, qs * 128:(qs + 1) * 128]
                    p.raw("tensor", lambda e, lh=lh, vrhs=vrhs, first=first, last=last, qs=qs:
                          e.matmul(p_av[qs][:, :129], lh, vrhs, start=first, stop=last))
                lh3 = p4[pi][:, 384:512]
                if last:
                    p.op("tensor", lambda e, lh3=lh3, vrhs=vrhs, first=first:
                         e.matmul(p_av[3][:, :129], lh3, vrhs, start=first, stop=True),
                         reads=(f"p4{pi}", "v"),
                         writes=("p_av0", "p_av1", "p_av2", "p_av3"))
                else:
                    p.op("tensor", lambda e, lh3=lh3, vrhs=vrhs, first=first:
                         e.matmul(p_av[3][:, :129], lh3, vrhs, start=first, stop=False),
                         reads=(f"p4{pi}", "v"), writes=())

            emit_qk(0)
            for kt in range(1, nkt):
                emit_qk(kt)
                emit_proc(kt - 1)
            emit_proc(nkt - 1)
            for qs in range(4):
                ptag = "p_av3"
                if layer == 0:
                    p.op("vector", lambda e, qs=qs: e.tensor_scalar(
                        dcol[:, :], p_av[qs][:, 128:129], 1e-6, None, ALU.max),
                         reads=(ptag,), writes=("dcol",))
                else:
                    p.op("vector", lambda e, qs=qs: e.tensor_copy(dcol[:, :], p_av[qs][:, 128:129]),
                         reads=(ptag,), writes=("dcol",))
                p.op("vector", lambda e: e.reciprocal(rden[:, :], dcol[:, :]),
                     reads=("dcol",), writes=("rden",))
                p.op("vector", lambda e, qs=qs: e.tensor_scalar(
                    att_st[:, :], p_av[qs][:, 0:128], rden[:, 0:1], None, ALU.mult),
                     reads=(ptag, "rden"), writes=("att_st",))
                p.op("tensor", lambda e: e.transpose(p_tr[:, :], att_st[:, :], ident_bf[:, :]),
                     reads=("att_st", "ident"), writes=("p_tr",))
                dst = ao_sb[:, slot * S + g * 512 + qs * 128: slot * S + g * 512 + qs * 128 + 128]
                p.op("scalar", lambda e, dst=dst: e.copy(dst, p_tr[:, :]),
                     reads=("p_tr",), writes=("ao",))

    p.op("gpsimd", lambda e: e.dma_start(
        out=attn_out.rearrange("(s p) t -> p s t", p=128), in_=ao_sb[:, :]),
        reads=("ao",), writes=("out_a",), dma=True, dsem="Dout")
    p.finish(["out_a"])
    p.emit()
    return nc



def build_launch_c():
    nc = bass.Bass("TRN2", target_bir_lowering=False)
    xT = nc.dram_tensor("xT", [D, T], F32, kind="ExternalInput").ap()
    attn0 = nc.dram_tensor("attn0", [D, T], F32, kind="ExternalInput").ap()
    wo0T = nc.dram_tensor("wo0T", [D, D], F32, kind="ExternalInput").ap()
    wq_hatT = nc.dram_tensor("wq_hatT", [D, D], BF16, kind="ExternalInput").ap()
    wq1 = nc.dram_tensor("wq1", [D, D], F32, kind="ExternalInput").ap()
    wk1 = nc.dram_tensor("wk1", [D, D], F32, kind="ExternalInput").ap()
    wv1 = nc.dram_tensor("wv1", [D, D], F32, kind="ExternalInput").ap()
    biases = nc.dram_tensor("biases", [128, 3 * DC], F32, kind="ExternalInput").ap()
    ident_in = nc.dram_tensor("identf", [128, 128], F32, kind="ExternalInput").ap()

    q1 = nc.dram_tensor("q1", [D, T], F32, kind="ExternalOutput").ap()
    k1 = nc.dram_tensor("k1", [D, T], F32, kind="ExternalOutput").ap()
    v1 = nc.dram_tensor("v1", [T, D], F32, kind="ExternalOutput").ap()
    x2 = nc.dram_tensor("x2", [D, T], F32, kind="ExternalOutput").ap()

    p = Prog(nc)
    cm = Common(p)
    x_sb = p.sb("x_sb", [128, DC * T], F32)
    lnb = p.sb("lnb", [128, DC * T], BF16)
    y_sb = p.sb("y_sb", [128, DC * T], BF16)
    vst = p.sb("vst", [128, 512], BF16)
    vtt = p.sb("vtt", [128, 8 * 128], BF16)
    ident_bf = p.sb("ident_bf", [128, 128], BF16)
    bias_sb = p.sb("bias_sb", [128, 3 * DC], F32)
    p_tr = p.ps("p_tr", [128, 128], BF16)
    amax = p.sb("amax", [128, 8], F32)
    mtmp = p.sb("mtmp", [128, 1], F32)
    rqrow = cm.Rrow
    rqrep = p.sb("rqrep", [128, T], F32)
    p_rq = cm.pproj[0]
    qi = p.sb("qi", [128, T], I32)
    ident_f32 = p.sb("ident_f32", [128, 128], F32)
    rq8 = p.sb("rq8", [8, 128], F32)

    p.op("gpsimd", lambda e: e.dma_start(out=ident_bf[:, :], in_=ident_in),
         writes=("ident",), dma=True, dsem="Din")
    p.op("sync", lambda e: e.dma_start(out=ident_f32[:, :], in_=ident_in),
         writes=("identf32",), dma=True, dsem="Din")
    p.op("sync", lambda e: e.dma_start(out=bias_sb[:, :], in_=biases),
         writes=("biasb",), dma=True, dsem="Din")
    p.op("sync", lambda e: e.dma_start(out=x_sb[:, :], in_=xT.rearrange("(d p) t -> p d t", p=128)),
         writes=tuple(f"x_{d}" for d in range(DC)), dma=True, dsem="Din")
    p.op("gpsimd", lambda e: e.dma_start(
        out=lnb[:, :], in_=attn0.rearrange("(d p) t -> p d t", p=128)),
        writes=tuple(f"a0_{d}" for d in range(DC)), dma=True, dsem="Din")

    def ych(oc, th):
        return y_sb[:, oc * T + th * 512: oc * T + th * 512 + 512]

    cm.projection(wo0T, lnb, lambda d: f"a0_{d}", T, T,
                  evac_simple(p, ych, lambda oc, th: f"y_{oc}"), True)
    cm.ln_post_add(lambda d: y_sb[:, d * T:(d + 1) * T], lambda d: f"y_{d}", False,
                   lambda d: x_sb[:, d * T:(d + 1) * T], lambda d: f"x_{d}", T)
    cm.ln_pre(lambda d: x_sb[:, d * T:(d + 1) * T], lambda d: f"x_{d}",
              lambda d: lnb[:, d * T:(d + 1) * T], lambda d: f"ln_{d}", T,
              alias_of=lambda d: f"a0_{d}")

    for tt in range(8):
        for d in range(DC):
            src = lnb[:, d * T + tt * 128: d * T + tt * 128 + 128]
            p.op("tensor", lambda e, src=src: e.transpose(p_tr[:, :], src, ident_bf[:, :]),
                 reads=(f"ln_{d}", "ident"), writes=("p_tr",))
            if d == 0:
                p.op("vector", lambda e, tt=tt: e.reduce_max(
                    amax[:, tt:tt + 1], p_tr[:, :], axis=AX.X, apply_absolute_value=True),
                     reads=("p_tr",), writes=("amax",))
            else:
                p.op("vector", lambda e: e.reduce_max(
                    mtmp[:, :], p_tr[:, :], axis=AX.X, apply_absolute_value=True),
                     reads=("p_tr",), writes=("mtmp",))
                p.op("vector", lambda e, tt=tt: e.tensor_tensor(
                    amax[:, tt:tt + 1], amax[:, tt:tt + 1], mtmp[:, :], ALU.max),
                     reads=("mtmp", "amax"), writes=("amax",))
    p.op("vector", lambda e: e.tensor_scalar(amax[:, :], amax[:, :], 1e-8, None, ALU.max),
         reads=("amax",), writes=("amax",))
    p.op("vector", lambda e: e.reciprocal(amax[:, :], amax[:, :]),
         reads=("amax",), writes=("amax",))
    p.op("vector", lambda e: e.tensor_scalar(amax[:, :], amax[:, :], 127.0, None, ALU.mult),
         reads=("amax",), writes=("amax",))
    p.op("tensor", lambda e: e.transpose(cm.pproj[0][0:8, 0:128], amax[:, :], ident_f32[:, :]),
         reads=("amax", "identf32"), writes=("pproj0",))
    p.op("vector", lambda e: e.tensor_copy(rq8[:, :], cm.pproj[0][0:8, 0:128]),
         reads=("pproj0",), writes=("rq8",))
    rq_dst = bass.AP(rqrow.tensor, 0, [[T, 1], [1, T]])
    p.op("sync", lambda e: e.dma_start(out=rq_dst, in_=rq8[:, :]),
         reads=("rq8",), writes=("Rrow",), dma=True, dsem="Drq")
    for th in range(2):
        sl = slice(th * 512, th * 512 + 512)
        p.op("tensor", lambda e, sl=sl: e.matmul(p_rq[:, :512], cm.ones_row, rqrow[:, sl],
                                                 start=True, stop=True),
             reads=("Rrow", "ones"), writes=("pproj0",))
        p.op("vector", lambda e, sl=sl: e.tensor_copy(rqrep[:, sl], p_rq[:, :512]),
             reads=("pproj0",), writes=("rqrep",))
    for d in range(DC):
        sl = d % 2
        p.op("vector", lambda e, d=d, sl=sl: e.tensor_tensor(
            cm.tln[sl][:, :], lnb[:, d * T:(d + 1) * T], rqrep[:, :], ALU.mult),
             reads=(f"ln_{d}", "rqrep"), writes=("tln0", "tln1"))
        p.op("vector", lambda e, sl=sl: e.tensor_scalar(cm.tln[sl][:, :], cm.tln[sl][:, :],
                                                        MAGIC, MAGIC, ALU.add, ALU.subtract),
             reads=(f"tln{sl}",), writes=("tln0", "tln1"))
        p.op("vector", lambda e, sl=sl: e.tensor_scalar(qi[:, :], cm.tln[sl][:, :], 0.0, None, ALU.add),
             reads=(f"tln{sl}",), writes=("qi",))
        p.op("vector", lambda e, d=d: e.tensor_scalar(
            lnb[:, d * T:(d + 1) * T], qi[:, :], 0.0, None, ALU.add),
             reads=("qi",), writes=(f"xq_{d}", f"ln_{d}"))
    cm.projection(wq_hatT, lnb, lambda d: f"xq_{d}", T, T,
                  evac_simple(p, ych, lambda oc, th: f"y_{oc}"), False)
    cm.ln_post_add(lambda d: y_sb[:, d * T:(d + 1) * T], lambda d: f"y_{d}", False,
                   lambda d: x_sb[:, d * T:(d + 1) * T], lambda d: f"x_{d}", T)
    p.op("sync", lambda e: e.dma_start(out=x2.rearrange("(d p) t -> p d t", p=128), in_=x_sb[:, :]),
         reads=tuple(f"x_{d}" for d in range(DC)), writes=("out_x2",), dma=True, dsem="Dout")
    cm.ln_pre(lambda d: x_sb[:, d * T:(d + 1) * T], lambda d: f"x_{d}",
              lambda d: lnb[:, d * T:(d + 1) * T], lambda d: f"ln2_{d}", T,
              alias_of=lambda d: f"xq_{d}")

    def qdst(oc, th):
        return y_sb[:, oc * T + th * 512: oc * T + th * 512 + 512]
    cm.projection(wq1, lnb, lambda d: f"ln2_{d}", T, T,
                  evac_simple(p, qdst, lambda oc, th: (f"qq_{oc}", f"y_{oc}")), True,
                  bias_sb=bias_sb[:, 0:DC], bias_tag="biasb")
    p.op("gpsimd", lambda e: e.dma_start(out=q1.rearrange("(d p) t -> p d t", p=128), in_=y_sb[:, :]),
         reads=tuple(f"qq_{d}" for d in range(DC)), writes=("out_q1",), dma=True, dsem="Dout")
    cm.projection(wk1, lnb, lambda d: f"ln2_{d}", T, T,
                  evac_simple(p, qdst, lambda oc, th: (f"kk_{oc}", f"qq_{oc}")), True,
                  bias_sb=bias_sb[:, DC:2 * DC], bias_tag="biasb")
    p.op("gpsimd", lambda e: e.dma_start(out=k1.rearrange("(d p) t -> p d t", p=128), in_=y_sb[:, :]),
         reads=tuple(f"kk_{d}" for d in range(DC)), writes=("out_k1",), dma=True, dsem="Dout")

    def v_cb(oc, th, ps_ap, pstag, bias_sb_, bias_tag_):
        bc = bias_sb_[:, oc:oc + 1]
        p.op("scalar", lambda e, ps_ap=ps_ap, bc=bc: e.activation(vst[:, :], ps_ap, AF.Identity, bias=bc),
             reads=(pstag, bias_tag_), writes=("vst",))
        for tt4 in range(4):
            tt = th * 4 + tt4
            p.op("tensor", lambda e, tt4=tt4: e.transpose(
                p_tr[:, :], vst[:, tt4 * 128:(tt4 + 1) * 128], ident_bf[:, :]),
                 reads=("vst", "ident"), writes=("p_tr",))
            dst = vtt[:, tt * 128:(tt + 1) * 128]
            p.op("scalar", lambda e, dst=dst: e.copy(dst, p_tr[:, :]),
                 reads=("p_tr",), writes=("vtt",))
        if th == 1:
            dst = bass.AP(v1.tensor, oc * 128, [[D, 128], [128 * D, 8], [1, 128]])
            src_ = bass.AP(vtt, 0, [[8 * 128, 128], [128, 8], [1, 128]])
            p.op("gpsimd", lambda e, dst=dst, src_=src_: e.dma_start(out=dst, in_=src_),
                 reads=("vtt",), writes=("out_v1",), dma=True, dsem="Dout")
    cm.projection(wv1, lnb, lambda d: f"ln2_{d}", T, T, v_cb, True,
                  bias_sb=bias_sb[:, 2 * DC:3 * DC], bias_tag="biasb")
    p.finish(["out_q1", "out_k1", "out_v1", "out_x2"])
    p.emit()
    return nc



def build_launch_e():
    nc = bass.Bass("TRN2", target_bir_lowering=False)
    x2 = nc.dram_tensor("x2", [D, T], F32, kind="ExternalInput").ap()
    attn1 = nc.dram_tensor("attn1", [D, T], F32, kind="ExternalInput").ap()
    wo1 = nc.dram_tensor("wo1", [D, D], F32, kind="ExternalInput").ap()
    bo_in = nc.dram_tensor("bo", [128, DC], F32, kind="ExternalInput").ap()
    wguT = nc.dram_tensor("wguT", [D, 16384], BF16, kind="ExternalInput").ap()
    wprT = nc.dram_tensor("wprT", [8192, D], BF16, kind="ExternalInput").ap()
    xout = nc.dram_tensor("xout", [D, T], F32, kind="ExternalOutput").ap()

    p = Prog(nc)
    cm = Common(p)
    x_sb = p.sb("x_sb", [128, DC * T], F32)
    u1 = p.sb("u1", [128, DC * T], BF16)
    yl = p.sb("yl", [128, DC * T], BF16)
    y3 = p.sb("y3", [128, DC * 512], BF16)
    bo_sb = p.sb("bo_sb", [128, DC], F32)
    gs = p.sb("gs", [128, 512], BF16)
    _pg = p.ps("p_g", [128, 512])
    _pu = p.ps("p_u", [128, 512])
    p_g = [_pg, _pg]
    p_u = [_pu, _pu]

    p.op("sync", lambda e: e.dma_start(out=bo_sb[:, :], in_=bo_in), writes=("bo",), dma=True, dsem="Din")
    p.op("sync", lambda e: e.dma_start(out=x_sb[:, :], in_=x2.rearrange("(d p) t -> p d t", p=128)),
         writes=tuple(f"x_{d}" for d in range(DC)), dma=True, dsem="Din")
    p.op("gpsimd", lambda e: e.dma_start(
        out=u1[:, :], in_=attn1.rearrange("(d p) t -> p d t", p=128)),
        writes=tuple(f"a1_{d}" for d in range(DC)), dma=True, dsem="Din")

    def ych(oc, th):
        return yl[:, oc * T + th * 512: oc * T + th * 512 + 512]
    cm.projection(wo1, u1, lambda d: f"a1_{d}", T, T,
                  evac_simple(p, ych, lambda oc, th: f"y_{oc}"), True,
                  bias_sb=bo_sb, bias_tag="bo")
    cm.ln_post_add(lambda d: yl[:, d * T:(d + 1) * T], lambda d: f"y_{d}", False,
                   lambda d: x_sb[:, d * T:(d + 1) * T], lambda d: f"x_{d}", T)
    cm.ln_pre(lambda d: x_sb[:, d * T:(d + 1) * T], lambda d: f"x_{d}",
              lambda d: yl[:, d * T:(d + 1) * T], lambda d: f"ln3_{d}", T,
              alias_of=lambda d: f"y_{d}")

    for th in range(2):
        for hh in range(2):
            for hc in range(32):
                hrow = hh * HHALF + hc * 128
                wb = cm.wbuf[hc % 2]
                wtag = f"wb{hc % 2}"
                gsrc = wguT[:, hrow:hrow + 128].rearrange("(d p) c -> p d c", p=128)
                usrc = wguT[:, 8192 + hrow: 8192 + hrow + 128].rearrange("(d p) c -> p d c", p=128)
                gdst = bass.AP(wb, 0, [[4096, 128], [256, 16], [1, 128]])
                udst = bass.AP(wb, 128, [[4096, 128], [256, 16], [1, 128]])
                fns = [lambda e, s=gsrc, gdst=gdst: e.dma_start(out=gdst, in_=s),
                       lambda e, s=usrc, udst=udst: e.dma_start(out=udst, in_=s)]
                p.multi("scalar", fns, reads=(), writes=(wtag,), dma=True, dsem=f"Dw{hc % 2}")
                pi = hc % 2
                fng, fnu = [], []
                for d in range(DC):
                    lg = wb[:, d * 256:d * 256 + 128]
                    lu = wb[:, d * 256 + 128:d * 256 + 256]
                    r = yl[:, d * T + th * 512: d * T + th * 512 + 512]
                    fng.append(lambda e, l=lg, r=r, f=(d == 0), la=(d == DC - 1), pi=pi:
                               e.matmul(p_g[pi][:, :], l, r, start=f, stop=la))
                    fnu.append(lambda e, l=lu, r=r, f=(d == 0), la=(d == DC - 1), pi=pi:
                               e.matmul(p_u[pi][:, :], l, r, start=f, stop=la))
                reads = (wtag,) + tuple(f"ln3_{d}" for d in range(DC))
                p.multi("tensor", fng, reads=reads, writes=("p_g0", "p_g1"))
                p.multi("tensor", fnu, reads=reads, writes=("p_u0", "p_u1"))
                p.op("scalar", lambda e, pi=pi: e.activation(gs[:, :], p_g[pi][:, :], AF.Silu),
                     reads=(f"p_g{pi}",), writes=("gs",))
                hdst = u1[:, hc * 512:(hc + 1) * 512]
                p.op("vector", lambda e, pi=pi, hdst=hdst: e.tensor_tensor(
                    hdst, gs[:, :], p_u[pi][:, :], ALU.mult),
                     reads=("gs", f"p_u{pi}"), writes=(f"h_{hc}", f"a1_{hc // 2}"))
            for oc in range(DC):
                wb = cm.wbuf[oc % 2]
                wtag = f"wb{oc % 2}"
                src = wprT[hh * HHALF:(hh + 1) * HHALF, oc * 128:(oc + 1) * 128]
                src_r = src.rearrange("(hc p) c -> p hc c", p=128)
                fns = []
                for piece in range(4):
                    fns.append(lambda e, piece=piece, wb=wb, src_r=src_r: e.dma_start(
                        out=wb[:, piece * 1024:(piece + 1) * 1024],
                        in_=src_r[:, piece * 8:(piece + 1) * 8, :]))
                p.multi("sync", fns, reads=(), writes=(wtag,), dma=True, dsem=f"Dw{oc % 2}")
                pi = oc % 2
                fns = []
                for hc in range(32):
                    l = wb[:, hc * 128:(hc + 1) * 128]
                    r = u1[:, hc * 512:(hc + 1) * 512]
                    fns.append(lambda e, l=l, r=r, f=(hc == 0), la=(hc == 31), pi=pi:
                               e.matmul(cm.pproj[pi][:, :], l, r, start=f, stop=la))
                reads = (wtag,) + tuple(f"h_{hc}" for hc in range(32))
                p.multi("tensor", fns, reads=reads, writes=(f"pproj{pi}",))
                ydst = y3[:, oc * 512:(oc + 1) * 512]
                if hh == 0:
                    p.op("scalar", lambda e, pi=pi, ydst=ydst: e.copy(ydst, cm.pproj[pi][:, :]),
                         reads=(f"pproj{pi}",), writes=(f"y3_{oc}",))
                else:
                    p.op("vector", lambda e, pi=pi, ydst=ydst: e.tensor_tensor(
                        ydst, ydst, cm.pproj[pi][:, :], ALU.add),
                         reads=(f"pproj{pi}", f"y3_{oc}"), writes=(f"y3_{oc}",))
        cm.ln_post_add(lambda d: y3[:, d * 512:(d + 1) * 512], lambda d: f"y3_{d}", False,
                       lambda d, th=th: x_sb[:, d * T + th * 512: d * T + th * 512 + 512],
                       lambda d: f"x_{d}", 512)

    p.op("sync", lambda e: e.dma_start(out=xout.rearrange("(d p) t -> p d t", p=128), in_=x_sb[:, :]),
         reads=tuple(f"x_{d}" for d in range(DC)), writes=("out_x",), dma=True, dsem="Dout")
    p.finish(["out_x"])
    p.emit()
    return nc



def host_tables():
    half = 64
    theta = 10000.0 ** (-np.arange(half, dtype=np.float64) * 2.0 / 128.0)
    pos = np.arange(S, dtype=np.float64)
    ang = pos[:, None] * theta[None, :]
    cos = np.cos(ang).astype(np.float32)
    sin = np.sin(ang).astype(np.float32)
    ident = np.eye(128, dtype=np.float32)
    ppp = np.arange(128)[:, None]
    fff = np.arange(512)[None, :]
    masks = np.stack([(fff >= ppp + 128 * r).astype(np.float32) for r in range(4)], 0)
    start = 2.0 ** (-8.0 / H)
    slopes = np.array([start * start ** i for i in range(H)], dtype=np.float64)
    d0 = (ppp - fff).astype(np.float64)
    return cos, sin, ident, masks, slopes, d0


_CACHE = {}


def _ncs():
    if "a" not in _CACHE:
        _CACHE["a"] = build_launch_a()
        _CACHE["b0"] = build_attention(0)
        _CACHE["c"] = build_launch_c()
        _CACHE["b1"] = build_attention(1)
        _CACHE["e"] = build_launch_e()
    return _CACHE


def kernel(L0_attn_x, L0_attn_Wq, L0_attn_Wk, L0_attn_Wv, L0_attn_Wo,
           L0_ffn_weight, L1_attn_wq, L1_attn_bq, L1_attn_wk, L1_attn_bk,
           L1_attn_wv, L1_attn_bv, L1_attn_wo, L1_attn_bo,
           L1_ffn_w_gate_up, L1_ffn_w_proj):
    cos, sin, ident, masks, slopes, d0 = host_tables()
    ncs = _ncs()
    x = np.ascontiguousarray(np.asarray(L0_attn_x, dtype=np.float32).reshape(S, D))
    wq0T = np.ascontiguousarray(np.asarray(L0_attn_Wq, np.float32).T)
    wk0T = np.ascontiguousarray(np.asarray(L0_attn_Wk, np.float32).T)
    wv0T = np.ascontiguousarray(np.asarray(L0_attn_Wv, np.float32).T)
    wo0T = np.ascontiguousarray(np.asarray(L0_attn_Wo, np.float32).T)
    w0T = np.ascontiguousarray(np.asarray(L0_ffn_weight, np.float32).T)
    wgu_bf = np.asarray(L1_ffn_w_gate_up, np.float32).astype(BF)
    wpr_bf = np.asarray(L1_ffn_w_proj, np.float32).astype(BF)

    tok_rows = {c: np.r_[np.arange(512 * c, 512 * c + 512),
                         np.arange(512 * (15 - c), 512 * (15 - c) + 512)]
                for c in range(NC)}

    in_a = []
    for c in range(NC):
        rows = tok_rows[c]
        cc = cos[rows].T
        ss = sin[rows].T
        in_a.append({
            "xT": np.ascontiguousarray(x[rows].T),
            "wq0T": wq0T, "wk0T": wk0T, "wv0T": wv0T,
            "ccat": np.ascontiguousarray(np.concatenate([cc, cc], 0)),
            "scat": np.ascontiguousarray(np.concatenate([-ss, ss], 0)),
            "w0T": w0T,
            "w0T_rows": np.ascontiguousarray(w0T[256 * c:256 * c + 256]),
            "wgu": np.ascontiguousarray(wgu_bf[2048 * c:2048 * (c + 1)]),
            "wpr": np.ascontiguousarray(wpr_bf[256 * c:256 * (c + 1)]),
            "identf": ident,
        })
    ra = run_bass_kernel_spmd(ncs["a"], in_a, core_ids=list(range(NC))).results

    q0f = np.zeros((H, 128, S), np.float32)
    k0f = np.zeros((H, 128, S), np.float32)
    v0f = np.zeros((H, S, 128), np.float32)
    for c in range(NC):
        rows = tok_rows[c]
        q0c = np.asarray(ra[c]["q0"])
        k0c = np.asarray(ra[c]["k0"])
        v0c = np.asarray(ra[c]["v0"])
        for h in range(H):
            q0f[h][:, rows] = q0c[h * 128:(h + 1) * 128]
            k0f[h][:, rows] = k0c[h * 128:(h + 1) * 128]
            v0f[h][rows, :] = v0c[:, h * 128:(h + 1) * 128]
    wq_hatT = np.ascontiguousarray(
        np.concatenate([np.asarray(ra[c]["wq_hat"]) for c in range(NC)], 0))
    wgu_hat = np.concatenate([np.asarray(ra[c]["wgu_hat"]) for c in range(NC)], 0)
    wpr_hat = np.concatenate([np.asarray(ra[c]["wpr_hat"]) for c in range(NC)], 0)
    wguT = np.ascontiguousarray(wgu_hat.T)
    wprT = np.ascontiguousarray(wpr_hat.T)

    in_b = []
    for c in range(NC):
        hs = heads_of(c)
        in_b.append({
            "qh": np.ascontiguousarray(np.concatenate([q0f[h] for h in hs], 0)),
            "kh": np.ascontiguousarray(np.concatenate([k0f[h] for h in hs], 0)),
            "vh": np.ascontiguousarray(np.stack([v0f[h] for h in hs], 0)),
            "identf": ident,
            "masks": np.ascontiguousarray(masks.reshape(4 * 128, 512)),
        })
    rb = run_bass_kernel_spmd(ncs["b0"], in_b, core_ids=list(range(NC))).results
    attn0f = np.zeros((H, 128, S), np.float32)
    for c in range(NC):
        a = np.asarray(rb[c]["attn"])
        hs = heads_of(c)
        attn0f[hs[0]] = a[0:128]
        attn0f[hs[1]] = a[128:256]

    bq = np.asarray(L1_attn_bq, np.float32).reshape(DC, 128).T
    bk = np.asarray(L1_attn_bk, np.float32).reshape(DC, 128).T
    bv = np.asarray(L1_attn_bv, np.float32).reshape(DC, 128).T
    biases = np.ascontiguousarray(np.concatenate([bq, bk, bv], 1))
    in_c = []
    for c in range(NC):
        rows = tok_rows[c]
        in_c.append({
            "xT": in_a[c]["xT"],
            "attn0": np.ascontiguousarray(attn0f[:, :, rows].reshape(D, T)),
            "wo0T": wo0T, "wq_hatT": wq_hatT,
            "wq1": np.asarray(L1_attn_wq, np.float32),
            "wk1": np.asarray(L1_attn_wk, np.float32),
            "wv1": np.asarray(L1_attn_wv, np.float32),
            "biases": biases, "identf": ident,
        })
    rc = run_bass_kernel_spmd(ncs["c"], in_c, core_ids=list(range(NC))).results
    q1f = np.zeros((H, 128, S), np.float32)
    k1f = np.zeros((H, 128, S), np.float32)
    v1f = np.zeros((H, S, 128), np.float32)
    x2f = {}
    for c in range(NC):
        rows = tok_rows[c]
        q1c = np.asarray(rc[c]["q1"])
        k1c = np.asarray(rc[c]["k1"])
        v1c = np.asarray(rc[c]["v1"])
        x2f[c] = np.asarray(rc[c]["x2"])
        for h in range(H):
            q1f[h][:, rows] = q1c[h * 128:(h + 1) * 128]
            k1f[h][:, rows] = k1c[h * 128:(h + 1) * 128]
            v1f[h][rows, :] = v1c[:, h * 128:(h + 1) * 128]

    in_d = []
    for c in range(NC):
        hs = heads_of(c)
        dts = []
        for h in hs:
            dh = slopes[h] * d0
            vars_ = [dh]
            for r in range(4):
                m = (np.arange(512)[None, :] < np.arange(128)[:, None] + 128 * r)
                vars_.append(np.where(m, NEG_BIG, dh))
            dts.append(np.stack(vars_, 0))
        dts = np.ascontiguousarray(np.stack(dts, 0).astype(np.float32).reshape(2 * 5 * 128, 512))
        btab = np.zeros((128, 2 * 68), np.float32)
        for si, h in enumerate(hs):
            for dk in range(-3, 64):
                btab[:, si * 68 + dk + 3] = np.float32(-slopes[h] * 128.0 * dk)
        in_d.append({
            "qh": np.ascontiguousarray(np.concatenate([q1f[h] for h in hs], 0)),
            "kh": np.ascontiguousarray(np.concatenate([k1f[h] for h in hs], 0)),
            "vh": np.ascontiguousarray(np.stack([v1f[h] for h in hs], 0)),
            "identf": ident, "dts": dts, "btab": btab,
        })
    rd = run_bass_kernel_spmd(ncs["b1"], in_d, core_ids=list(range(NC))).results
    attn1f = np.zeros((H, 128, S), np.float32)
    for c in range(NC):
        a = np.asarray(rd[c]["attn"])
        hs = heads_of(c)
        attn1f[hs[0]] = a[0:128]
        attn1f[hs[1]] = a[128:256]

    bo = np.ascontiguousarray(np.asarray(L1_attn_bo, np.float32).reshape(DC, 128).T)
    in_e = []
    for c in range(NC):
        rows = tok_rows[c]
        in_e.append({
            "x2": x2f[c],
            "attn1": np.ascontiguousarray(attn1f[:, :, rows].reshape(D, T)),
            "wo1": np.asarray(L1_attn_wo, np.float32),
            "bo": bo, "wguT": wguT, "wprT": wprT,
        })
    re_ = run_bass_kernel_spmd(ncs["e"], in_e, core_ids=list(range(NC))).results

    out = np.zeros((S, D), np.float32)
    for c in range(NC):
        rows = tok_rows[c]
        out[rows] = np.asarray(re_[c]["xout"]).T
    return out.reshape(1, S, D)


# version 04
# speedup vs baseline: 1.7796x; 1.1754x over previous
import contextlib
import numpy as np
import ml_dtypes
import concourse.bass as bass
import concourse.mybir as mybir
from concourse.bass_utils import run_bass_kernel_spmd

F32 = mybir.dt.float32
BF16 = mybir.dt.bfloat16
I32 = mybir.dt.int32
AF = mybir.ActivationFunctionType
ALU = mybir.AluOpType
AX = mybir.AxisListType

NC = 8
S = 8192
D = 2048
H = 16
CH = 512
T = 1024
NCH = 16
DC = 16
HHALF = 4096
EPS_LN = 1e-5
SCALE_L1 = float(1.0 / np.sqrt(np.float32(128.0)))
NEG_BIG = -1e30
MAGIC = 1.5 * 2.0 ** 23
BF = ml_dtypes.bfloat16
ENGINES = ("sync", "scalar", "vector", "tensor", "gpsimd")
CSEM = {"scalar": "Ts", "vector": "Tv", "tensor": "Tt", "gpsimd": "Tg"}
QSUF = {"sync": "Y", "scalar": "S", "gpsimd": "G"}
CUMU = ("Din", "Dout")


def heads_of(core):
    return (core, core + 8)


class Prog:

    def __init__(self, nc):
        self.nc = nc
        self.es = contextlib.ExitStack()
        self.ops = {e: [] for e in ENGINES}
        self.sems = {}
        self.cnt = {}
        self.lastwait = {}
        self.track = {}

    def sem(self, name):
        if name not in self.sems:
            self.sems[name] = self.es.enter_context(self.nc.semaphore(name))
            self.cnt[name] = 0
        return self.sems[name]

    def sb(self, name, shape, dtype):
        return self.es.enter_context(self.nc.sbuf_tensor(name, shape, dtype))

    def ps(self, name, shape, dtype=F32):
        return self.es.enter_context(self.nc.psum_tensor(name, shape, dtype))

    def _emit_waits(self, eng, toks):
        need = {}
        for (sname, val) in toks:
            if val > need.get(sname, 0):
                need[sname] = val
        out = []
        for sname, val in need.items():
            key = (eng, sname)
            if self.lastwait.get(key, 0) < val:
                self.lastwait[key] = val
                out.append((self.sem(sname), val))
        return out

    def _dep_toks(self, reads, writes):
        toks = []
        for t in reads:
            rec = self.track.get(t)
            if rec and rec[0] is not None:
                toks.append(rec[0])
        for t in writes:
            rec = self.track.get(t)
            if rec:
                if rec[0] is not None:
                    toks.append(rec[0])
                toks.extend(rec[1].items())
        toks = [(s, self.cnt[s]) if s.startswith(CUMU) else (s, v) for (s, v) in toks]
        return toks

    def op(self, eng, fn, reads=(), writes=(), dma=False, noinc=False, dsem=None):
        waits = self._emit_waits(eng, self._dep_toks(reads, writes))
        sname = (dsem + QSUF[eng]) if dma else CSEM[eng]
        sem = self.sem(sname)
        if noinc:
            tok = None
        else:
            self.cnt[sname] += 16 if dma else 1
            tok = (sname, self.cnt[sname])
        inc_by = 16 if dma else 1

        def closure(e, waits=waits, fn=fn, sem=sem, inc_by=inc_by, noinc=noinc):
            for (s, v) in waits:
                e.wait_ge(s, v)
            ins = fn(e)
            if not noinc:
                ins.then_inc(sem, inc_by)

        self.ops[eng].append(closure)
        if tok is not None:
            for t in writes:
                self.track[t] = [tok, {}]
            for t in reads:
                rec = self.track.setdefault(t, [None, {}])
                if tok[1] > rec[1].get(tok[0], 0):
                    rec[1][tok[0]] = tok[1]
        return tok

    def raw(self, eng, fn):
        self.ops[eng].append(lambda e, fn=fn: fn(e))

    def multi(self, eng, fns, reads=(), writes=(), dma=False, dsem=None):
        n = len(fns)
        if n == 1:
            return self.op(eng, fns[0], reads=reads, writes=writes, dma=dma, dsem=dsem)
        if dma:
            self.op(eng, fns[0], reads=reads, writes=writes, dma=True, dsem=dsem)
            sname = dsem + QSUF[eng]
            sem = self.sem(sname)
            for f in fns[1:]:
                self.cnt[sname] += 16
                self.ops[eng].append(lambda e, f=f, sem=sem: f(e).then_inc(sem, 16))
            tok = (sname, self.cnt[sname])
            for t in writes:
                self.track[t] = [tok, {}]
            return tok
        self.op(eng, fns[0], reads=reads, writes=writes, dma=dma, noinc=True)
        for f in fns[1:-1]:
            self.raw(eng, f)
        return self.op(eng, fns[-1], reads=reads, writes=writes, dma=dma)

    def finish(self, out_tags):
        toks = []
        for t in out_tags:
            rec = self.track.get(t)
            if rec and rec[0] is not None:
                toks.append(rec[0])
        waits = self._emit_waits("sync", toks)

        def closure(e, waits=waits):
            for (s, v) in waits:
                e.wait_ge(s, v)
        self.ops["sync"].append(closure)

    def emit(self):
        with self.nc.Block() as blk:
            for e in ENGINES:
                if not self.ops[e]:
                    continue
                dec = getattr(blk, e)

                def body(engobj, e=e):
                    for f in self.ops[e]:
                        f(engobj)
                dec(body)



class Common:
    def __init__(self, p, wbuf_cols=DC * 256):
        self.p = p
        self.xsq = [p.sb("xsq0", [128, T], BF16), p.sb("xsq1", [128, T], BF16)]
        self.p_stat = p.ps("p_stat", [1, 2 * T])
        self.p_mean = self.p_stat[:, 0:T]
        self.p_sq = self.p_stat[:, T:2 * T]
        self.srow = p.sb("statsrow", [1, T], F32)[:, :]
        self.srow2 = p.sb("statsrow2", [1, T], F32)[:, :]
        self.Rrow = p.sb("rrow", [1, T], F32)[:, :]
        self.MRrow = p.sb("mrrow", [1, T], F32)[:, :]
        self.Rrep = p.sb("Rrep", [128, T], F32)
        self.MRrep = p.sb("MRrep", [128, T], F32)
        self.ones_f32 = p.sb("ones_f32", [128, 1], F32)[:, :]
        self.ones_bf = p.sb("ones_bf", [128, 1], BF16)[:, :]
        self.ones_row = p.sb("ones_row", [1, 128], F32)[:, :]
        self.eps1 = p.sb("eps1", [1, 1], F32)
        _tl = p.sb("tlnbuf", [128, T], F32)
        self.tln = [_tl, _tl]
        self.wbuf = [p.sb("wbufA", [128, wbuf_cols], BF16),
                     p.sb("wbufB", [128, wbuf_cols], BF16)]
        self.pproj = [p.ps("pprojA", [128, 512]), p.ps("pprojB", [128, 512])]
        self.p_R = self.pproj[0]
        self.p_MR = self.pproj[1]
        p.op("gpsimd", lambda e: e.memset(self.ones_f32[:, :], 1.0), writes=("ones",))
        p.op("gpsimd", lambda e: e.memset(self.ones_bf[:, :], 1.0), writes=("ones",))
        p.op("gpsimd", lambda e: e.memset(self.ones_row[:, :], 1.0), writes=("ones",))
        p.op("gpsimd", lambda e: e.memset(self.eps1[:, :], EPS_LN), writes=("ones",))

    def ln_stats(self, src_ap_of, src_tag_of, src_is_f32, tcols):
        p = self.p
        nh = (tcols + 511) // 512
        for d in range(DC):
            src = src_ap_of(d)
            stag = src_tag_of(d)
            sq = self.xsq[d % 2]
            p.op("scalar", lambda e, s=src, o=sq, w=tcols: e.activation(o[:, :w], s, AF.Square),
                 reads=(stag,), writes=(f"xsq{d % 2}",))
            for th in range(nh):
                w = min(512, tcols - th * 512)
                sl = slice(th * 512, th * 512 + w)
                first, last = (d == 0), (d == DC - 1)
                ones = self.ones_f32 if src_is_f32 else self.ones_bf
                p.op("tensor", lambda e, s=src, sl=sl, f=first, l=last, o=ones:
                     e.matmul(self.p_mean[:, sl], o, s[:, sl], start=f, stop=l),
                     reads=(stag, "ones"),
                     writes=("p_mean",) if (first or last) else ())
                p.op("tensor", lambda e, sl=sl, q=sq, f=first, l=last:
                     e.matmul(self.p_sq[:, sl], self.ones_bf, q[:, sl], start=f, stop=l),
                     reads=(f"xsq{d % 2}", "ones"),
                     writes=("p_sq",) if (first or last) else ())
        self.ln_finish(tcols)

    def ln_finish(self, tcols):
        p = self.p
        w = tcols
        nh = (tcols + 511) // 512
        p.op("vector", lambda e: e.tensor_scalar(self.srow[:, :w], self.p_mean[:, :w],
                                                 1.0 / D, None, ALU.mult),
             reads=("p_mean",), writes=("srow",))
        p.op("vector", lambda e: e.tensor_tensor(self.srow2[:, :w], self.srow[:, :w],
                                                 self.srow[:, :w], ALU.mult),
             reads=("srow",), writes=("srow2",))
        p.op("vector", lambda e: e.scalar_tensor_tensor(self.srow2[:, :w], self.p_sq[:, :w],
                                                        1.0 / D, self.srow2[:, :w],
                                                        ALU.mult, ALU.subtract),
             reads=("p_sq", "srow2"), writes=("srow2",))
        p.op("scalar", lambda e: e.activation(self.srow2[:, :w], self.srow2[:, :w],
                                              AF.Sqrt, bias=self.eps1[0:1, 0:1]),
             reads=("srow2", "ones"), writes=("srow2",))
        p.op("vector", lambda e: e.reciprocal(self.Rrow[:, :w], self.srow2[:, :w]),
             reads=("srow2",), writes=("Rrow",))
        p.op("vector", lambda e: e.tensor_tensor(self.MRrow[:, :w], self.srow[:, :w],
                                                 self.Rrow[:, :w], ALU.mult),
             reads=("srow", "Rrow"), writes=("MRrow",))
        for th in range(nh):
            w2 = min(512, tcols - th * 512)
            sl = slice(th * 512, th * 512 + w2)
            p.op("tensor", lambda e, sl=sl, w2=w2:
                 e.matmul(self.p_R[:, :w2], self.ones_row, self.Rrow[:, sl], start=True, stop=True),
                 reads=("Rrow", "ones"), writes=("pproj0",))
            p.op("vector", lambda e, sl=sl, w2=w2: e.tensor_copy(self.Rrep[:, sl], self.p_R[:, :w2]),
                 reads=("pproj0",), writes=("Rrep",))
            p.op("tensor", lambda e, sl=sl, w2=w2:
                 e.matmul(self.p_MR[:, :w2], self.ones_row, self.MRrow[:, sl], start=True, stop=True),
                 reads=("MRrow", "ones"), writes=("pproj1",))
            p.op("vector", lambda e, sl=sl, w2=w2: e.tensor_copy(self.MRrep[:, sl], self.p_MR[:, :w2]),
                 reads=("pproj1",), writes=("MRrep",))

    def ln_pre(self, src_ap_of, src_tag_of, dst_ap_of, dst_tag_of, tcols, alias_of=None):
        p = self.p
        self.ln_stats(src_ap_of, src_tag_of, True, tcols)
        for d in range(DC):
            w = tcols
            sl = d % 2
            p.op("vector", lambda e, d=d, w=w, sl=sl: e.tensor_tensor(
                self.tln[sl][:, :w], src_ap_of(d), self.Rrep[:, :w], ALU.mult),
                 reads=(src_tag_of(d), "Rrep"), writes=("tln0", "tln1"))
            wt = (dst_tag_of(d),) if alias_of is None else (dst_tag_of(d), alias_of(d))
            p.op("vector", lambda e, d=d, w=w, sl=sl: e.tensor_tensor(
                dst_ap_of(d), self.tln[sl][:, :w], self.MRrep[:, :w], ALU.subtract),
                 reads=(f"tln{sl}", "MRrep"), writes=wt)

    def ln_post_add(self, y_ap_of, y_tag_of, y_is_f32, x_ap_of, x_tag_of, tcols):
        p = self.p
        self.ln_stats(y_ap_of, y_tag_of, y_is_f32, tcols)
        for d in range(DC):
            w = tcols
            sl = d % 2
            p.op("vector", lambda e, d=d, w=w, sl=sl: e.tensor_tensor(
                self.tln[sl][:, :w], y_ap_of(d), self.Rrep[:, :w], ALU.mult),
                 reads=(y_tag_of(d), "Rrep"), writes=("tln0", "tln1"))
            p.op("vector", lambda e, d=d, w=w, sl=sl: e.tensor_tensor(
                self.tln[sl][:, :w], self.tln[sl][:, :w], self.MRrep[:, :w], ALU.subtract),
                 reads=(f"tln{sl}", "MRrep"), writes=("tln0", "tln1"))
            p.op("vector", lambda e, d=d, w=w, sl=sl: e.tensor_tensor(
                x_ap_of(d), x_ap_of(d), self.tln[sl][:, :w], ALU.add),
                 reads=(f"tln{sl}", x_tag_of(d)), writes=(x_tag_of(d),))

    def projection(self, w_dram, act_sb, act_tag_of, act_cols, tcols,
                   out_cb, cast_dma, bias_sb=None, bias_tag=None, hook=None):
        p = self.p
        nth = (tcols + 511) // 512
        for og in range(8):
            wb = self.wbuf[og % 2]
            wtag = f"wb{og % 2}"
            eng = "gpsimd" if cast_dma else "scalar"
            wsrc = w_dram[:, og * 256:(og + 1) * 256].rearrange("(d p) c -> p d c", p=128)
            p.op(eng, lambda e, s=wsrc, wb=wb: e.dma_start(out=wb[:, :], in_=s),
                 reads=(), writes=(wtag,), dma=True, dsem=f"Dw{og % 2}")
            for oc2 in range(2):
                oc = og * 2 + oc2
                for th in range(nth):
                    w = min(512, tcols - th * 512)
                    pi = (oc2 * nth + th) % 2
                    ps = self.pproj[pi]
                    pstag = f"pproj{pi}"
                    fns = []
                    for d in range(DC):
                        l = wb[:, d * 256 + oc2 * 128: d * 256 + oc2 * 128 + 128]
                        r = act_sb[:, d * act_cols + th * 512: d * act_cols + th * 512 + w]
                        fns.append(lambda e, l=l, r=r, f=(d == 0), la=(d == DC - 1), ps=ps, w=w:
                                   e.matmul(ps[:, :w], l, r, start=f, stop=la))
                    reads = (wtag,) + tuple(act_tag_of(d) for d in range(DC))
                    p.multi("tensor", fns, reads=reads, writes=(pstag,))
                    out_cb(oc, th, ps[:, :w], pstag, bias_sb, bias_tag)
            if hook is not None:
                hook()


def evac_simple(p, dst_ap, dst_tag):
    def cb(oc, th, ps_ap, pstag, bias_sb, bias_tag):
        da = dst_ap(oc, th)
        wt = dst_tag(oc, th)
        wt = wt if isinstance(wt, tuple) else (wt,)
        if bias_sb is not None:
            bc = bias_sb[:, oc:oc + 1]
            p.op("scalar", lambda e, ps_ap=ps_ap, da=da, bc=bc:
                 e.activation(da, ps_ap, AF.Identity, bias=bc),
                 reads=(pstag, bias_tag), writes=wt)
        else:
            p.op("scalar", lambda e, ps_ap=ps_ap, da=da: e.copy(da, ps_ap),
                 reads=(pstag,), writes=wt)
    return cb



def build_launch_a():
    nc = bass.Bass("TRN2", target_bir_lowering=False)
    xT = nc.dram_tensor("xT", [D, T], F32, kind="ExternalInput").ap()
    wq0T = nc.dram_tensor("wq0T", [D, D], F32, kind="ExternalInput").ap()
    wk0T = nc.dram_tensor("wk0T", [D, D], F32, kind="ExternalInput").ap()
    wv0T = nc.dram_tensor("wv0T", [D, D], F32, kind="ExternalInput").ap()
    ccat_in = nc.dram_tensor("ccat", [128, T], F32, kind="ExternalInput").ap()
    scat_in = nc.dram_tensor("scat", [128, T], F32, kind="ExternalInput").ap()
    w0T = nc.dram_tensor("w0T", [D, D], F32, kind="ExternalInput").ap()
    w0T_rows = nc.dram_tensor("w0T_rows", [256, D], F32, kind="ExternalInput").ap()
    wgu = nc.dram_tensor("wgu", [2048, 2048], BF16, kind="ExternalInput").ap()
    wpr = nc.dram_tensor("wpr", [256, 8192], BF16, kind="ExternalInput").ap()
    ident_in = nc.dram_tensor("identf", [128, 128], F32, kind="ExternalInput").ap()

    q0 = nc.dram_tensor("q0", [D, T], F32, kind="ExternalOutput").ap()
    k0 = nc.dram_tensor("k0", [D, T], F32, kind="ExternalOutput").ap()
    v0 = nc.dram_tensor("v0", [T, D], F32, kind="ExternalOutput").ap()
    wq_hat = nc.dram_tensor("wq_hat", [256, D], BF16, kind="ExternalOutput").ap()
    wgu_hat = nc.dram_tensor("wgu_hat", [2048, 2048], BF16, kind="ExternalOutput").ap()
    wpr_hat = nc.dram_tensor("wpr_hat", [256, 8192], BF16, kind="ExternalOutput").ap()

    p = Prog(nc)
    cm = Common(p)
    lnx = p.sb("lnx", [128, DC * T], BF16)
    q_sb = p.sb("q_sb", [128, DC * T], BF16)
    k_sb = p.sb("k_sb", [128, DC * T], BF16)
    vst = p.sb("vst", [128, 512], BF16)
    vtt = p.sb("vtt", [128, 8 * 128], BF16)
    xch = [p.sb("xch0", [128, T], F32), p.sb("xch1", [128, T], F32)]
    ccat = p.sb("ccat_sb", [128, T], BF16)
    scat = p.sb("scat_sb", [128, T], BF16)
    ident_bf = p.sb("ident_bf", [128, 128], BF16)
    rt1 = p.sb("rt1", [128, T], BF16)
    rt2b = [p.sb("rt2a", [128, T], BF16), p.sb("rt2b", [128, T], BF16)]
    p_tr = p.ps("p_tr", [128, 128], BF16)

    qw = p.sb("qw", [128, 1024], BF16)
    qs_f = p.sb("qs_f", [128, 16], F32)
    qs_r = p.sb("qs_r", [128, 16], F32)
    qs_b = p.sb("qs_b", [128, 16], BF16)
    qt = p.sb("qt", [128, 1024], F32)
    qi = p.sb("qi", [128, 1024], I32)
    qq = p.sb("qq", [128, 1024], BF16)
    qh = p.sb("qh", [128, 1024], BF16)
    wsc_col = p.sb("wsc_col", [128, 1], F32)
    wsc_ps = cm.pproj[0][0:1, 0:1]
    wsc_rep = p.sb("wsc_rep", [128, 1], F32)
    p_rep1 = cm.pproj[1][:, 0:1]

    p.op("gpsimd", lambda e: e.dma_start(out=ident_bf[:, :], in_=ident_in),
         writes=("ident",), dma=True, dsem="Din")
    p.op("gpsimd", lambda e: e.dma_start(out=ccat[:, :], in_=ccat_in),
         writes=("ccat",), dma=True, dsem="Din")
    p.op("gpsimd", lambda e: e.dma_start(out=scat[:, :], in_=scat_in),
         writes=("scat",), dma=True, dsem="Din")

    quant_pieces = []

    def ternary(nrows, w_in, w_out, out_tag):
        npieces = w_in.shape[1] // 1024
        for r in range(nrows // 128):
            for pc in range(npieces):
                quant_pieces.append((r, pc, w_in, w_out, out_tag))

    def emit_quant_piece(args):
        r, pc, w_in, w_out, out_tag = args
        csl = slice(pc * 1024, pc * 1024 + 1024)
        if True:
            if True:
                p.op("scalar", lambda e, r=r, csl=csl, w_in=w_in:
                     e.dma_start(out=qw[:, :], in_=w_in[r * 128:(r + 1) * 128, csl]),
                     reads=(), writes=("qw",), dma=True, dsem="Dqw")
                g3 = bass.AP(qw, 0, [[1024, 128], [64, 16], [1, 64]])
                p.op("vector", lambda e, g3=g3: e.reduce_sum(qs_f[:, :], g3, axis=AX.X,
                                                             apply_absolute_value=True),
                     reads=("qw",), writes=("qs_f",))
                p.op("vector", lambda e: e.tensor_scalar(qs_f[:, :], qs_f[:, :],
                                                         1.0 / 64, 1e-8, ALU.mult, ALU.max),
                     reads=("qs_f",), writes=("qs_f",))
                p.op("vector", lambda e: e.tensor_copy(qs_b[:, :], qs_f[:, :]),
                     reads=("qs_f",), writes=("qs_b",))
                p.op("vector", lambda e: e.tensor_copy(qs_f[:, :], qs_b[:, :]),
                     reads=("qs_b",), writes=("qs_f",))
                p.op("vector", lambda e: e.reciprocal(qs_r[:, :], qs_f[:, :]),
                     reads=("qs_f",), writes=("qs_r",))
                rb = bass.AP(qs_r, 0, [[16, 128], [1, 16], [0, 64]])
                t3 = bass.AP(qt, 0, [[1024, 128], [64, 16], [1, 64]])
                p.op("vector", lambda e, g3=g3, rb=rb, t3=t3: e.tensor_tensor(t3, g3, rb, ALU.mult),
                     reads=("qw", "qs_r"), writes=("qt",))
                p.op("vector", lambda e: e.tensor_scalar(qt[:, :], qt[:, :], MAGIC, MAGIC,
                                                         ALU.add, ALU.subtract),
                     reads=("qt",), writes=("qt",))
                p.op("vector", lambda e: e.tensor_scalar(qi[:, :], qt[:, :], 0.0, None, ALU.add),
                     reads=("qt",), writes=("qi",))
                p.op("vector", lambda e: e.tensor_scalar(qq[:, :], qi[:, :], -1.0, 1.0,
                                                         ALU.max, ALU.min),
                     reads=("qi",), writes=("qq",))
                sb3 = bass.AP(qs_b, 0, [[16, 128], [1, 16], [0, 64]])
                q3 = bass.AP(qq, 0, [[1024, 128], [64, 16], [1, 64]])
                h3 = bass.AP(qh, 0, [[1024, 128], [64, 16], [1, 64]])
                p.op("vector", lambda e, q3=q3, sb3=sb3, h3=h3: e.tensor_tensor(h3, q3, sb3, ALU.mult),
                     reads=("qq", "qs_b"), writes=("qh",))
                p.op("scalar", lambda e, r=r, csl=csl, w_out=w_out:
                     e.dma_start(out=w_out[r * 128:(r + 1) * 128, csl], in_=qh[:, :]),
                     reads=("qh",), writes=(out_tag,), dma=True, dsem="Dout")

    qwork = []

    def scan_unit(r):
        p.op("gpsimd", lambda e, r=r: e.dma_start(
            out=qt[:, :], in_=w0T[(r // 2) * 128:(r // 2 + 1) * 128,
                                  (r % 2) * 1024:(r % 2) * 1024 + 1024]),
             writes=("qt",), dma=True, dsem="Dqt")
        p.op("vector", lambda e: e.reduce_sum(qs_f[:, 0:1], qt[:, :], axis=AX.X,
                                              apply_absolute_value=True),
             reads=("qt",), writes=("qs_f",))
        if r == 0:
            p.op("vector", lambda e: e.tensor_copy(wsc_col[:, :], qs_f[:, 0:1]),
                 reads=("qs_f",), writes=("wsc_col",))
        else:
            p.op("vector", lambda e: e.tensor_tensor(wsc_col[:, :], wsc_col[:, :],
                                                     qs_f[:, 0:1], ALU.add),
                 reads=("qs_f", "wsc_col"), writes=("wsc_col",))

    def wsc_finish():
        p.op("tensor", lambda e: e.matmul(wsc_ps[:, :], cm.ones_f32, wsc_col[:, :],
                                          start=True, stop=True),
             reads=("wsc_col", "ones"), writes=("pproj0",))
        p.op("vector", lambda e: e.tensor_scalar(wsc_col[0:1, 0:1], wsc_ps[:, :],
                                                 1.0 / (D * D), 1e-6, ALU.mult, ALU.max),
             reads=("pproj0",), writes=("wsc_col",))
        p.op("vector", lambda e: e.reciprocal(wsc_col[0:1, 0:1], wsc_col[0:1, 0:1]),
             reads=("wsc_col",), writes=("wsc_col",))
        p.op("tensor", lambda e: e.matmul(p_rep1[:, :], cm.ones_row, wsc_col[0:1, 0:1],
                                          start=True, stop=True),
             reads=("wsc_col", "ones"), writes=("pproj1",))
        p.op("vector", lambda e: e.tensor_copy(wsc_rep[:, :], p_rep1[:, :]),
             reads=("pproj1",), writes=("wsc_rep",))

    def shard_unit(r):
        rr, cc_ = r // 2, (r % 2) * 1024
        p.op("gpsimd", lambda e, rr=rr, cc_=cc_: e.dma_start(
            out=qt[:, :], in_=w0T_rows[rr * 128:(rr + 1) * 128, cc_:cc_ + 1024]),
             writes=("qt",), dma=True, dsem="Dqt")
        p.op("vector", lambda e: e.tensor_scalar(qt[:, :], qt[:, :], wsc_rep[:, 0:1], None, ALU.mult),
             reads=("qt", "wsc_rep"), writes=("qt",))
        p.op("vector", lambda e: e.tensor_scalar(qt[:, :], qt[:, :], MAGIC, MAGIC,
                                                 ALU.add, ALU.subtract),
             reads=("qt",), writes=("qt",))
        p.op("vector", lambda e: e.tensor_scalar(qi[:, :], qt[:, :], 0.0, None, ALU.add),
             reads=("qt",), writes=("qi",))
        p.op("vector", lambda e: e.tensor_scalar(qq[:, :], qi[:, :], -1.0, 1.0, ALU.max, ALU.min),
             reads=("qi",), writes=("qq",))
        p.op("scalar", lambda e, rr=rr, cc_=cc_: e.dma_start(
            out=wq_hat[rr * 128:(rr + 1) * 128, cc_:cc_ + 1024], in_=qq[:, :]),
             reads=("qq",), writes=("out_wq",), dma=True, dsem="Dout")

    for r in range(2 * DC):
        qwork.append(lambda r=r: scan_unit(r))
    qwork.append(wsc_finish)
    for r in range(4):
        qwork.append(lambda r=r: shard_unit(r))
    ternary(2048, wgu, wgu_hat, "out_gu")
    ternary(256, wpr, wpr_hat, "out_pr")
    for a in quant_pieces:
        qwork.append(lambda a=a: emit_quant_piece(a))
    qp_iter = iter(qwork)

    def emit_some_quant(n):
        for _ in range(n):
            f = next(qp_iter, None)
            if f is not None:
                f()

    def x_stream(d, slot):
        p.op("sync", lambda e, d=d, slot=slot: e.dma_start(
            out=xch[slot][:, :], in_=xT[d * 128:(d + 1) * 128, :]),
            writes=(f"xch{slot}",), dma=True, dsem=f"Dx{slot}")

    for d in range(DC):
        slot = d % 2
        x_stream(d, slot)
        p.op("scalar", lambda e, slot=slot: e.activation(cm.xsq[slot][:, :], xch[slot][:, :], AF.Square),
             reads=(f"xch{slot}",), writes=(f"xsq{slot}",))
        for th in range(2):
            sl = slice(th * 512, th * 512 + 512)
            first, last = (d == 0), (d == DC - 1)
            p.op("tensor", lambda e, sl=sl, slot=slot, f=first, l=last:
                 e.matmul(cm.p_mean[:, sl], cm.ones_f32, xch[slot][:, sl], start=f, stop=l),
                 reads=(f"xch{slot}", "ones"), writes=("p_mean",) if (first or last) else ())
            p.op("tensor", lambda e, sl=sl, slot=slot, f=first, l=last:
                 e.matmul(cm.p_sq[:, sl], cm.ones_bf, cm.xsq[slot][:, sl], start=f, stop=l),
                 reads=(f"xsq{slot}", "ones"), writes=("p_sq",) if (first or last) else ())
    cm.ln_finish(T)
    for d in range(DC):
        slot = d % 2
        x_stream(d, slot)
        p.op("vector", lambda e, slot=slot: e.tensor_tensor(cm.tln[slot][:, :], xch[slot][:, :],
                                                            cm.Rrep[:, :], ALU.mult),
             reads=(f"xch{slot}", "Rrep"), writes=("tln0", "tln1"))
        p.op("vector", lambda e, d=d, slot=slot: e.tensor_tensor(
            lnx[:, d * T:(d + 1) * T], cm.tln[slot][:, :], cm.MRrep[:, :], ALU.subtract),
             reads=(f"tln{slot}", "MRrep"), writes=(f"lnx_{d}",))

    def dst_chunk(dst):
        return lambda oc, th: dst[:, oc * T + th * 512: oc * T + th * 512 + 512]

    cm.projection(wq0T, lnx, lambda d: f"lnx_{d}", T, T,
                  evac_simple(p, dst_chunk(q_sb), lambda oc, th: f"q_{oc}"), True,
                  hook=lambda: emit_some_quant(4))
    cm.projection(wk0T, lnx, lambda d: f"lnx_{d}", T, T,
                  evac_simple(p, dst_chunk(k_sb), lambda oc, th: f"k_{oc}"), True,
                  hook=lambda: emit_some_quant(4))

    def v_cb(oc, th, ps_ap, pstag, bias_sb, bias_tag):
        p.op("scalar", lambda e, ps_ap=ps_ap: e.copy(vst[:, :], ps_ap),
             reads=(pstag,), writes=("vst",))
        for tt4 in range(4):
            tt = th * 4 + tt4
            p.op("tensor", lambda e, tt4=tt4: e.transpose(
                p_tr[:, :], vst[:, tt4 * 128:(tt4 + 1) * 128], ident_bf[:, :]),
                 reads=("vst", "ident"), writes=("p_tr",))
            dst = vtt[:, tt * 128:(tt + 1) * 128]
            p.op("scalar", lambda e, dst=dst: e.copy(dst, p_tr[:, :]),
                 reads=("p_tr",), writes=("vtt",))
        if th == 1:
            dst = bass.AP(v0.tensor, oc * 128, [[D, 128], [128 * D, 8], [1, 128]])
            src_ = bass.AP(vtt, 0, [[8 * 128, 128], [128, 8], [1, 128]])
            p.op("gpsimd", lambda e, dst=dst, src_=src_: e.dma_start(out=dst, in_=src_),
                 reads=("vtt",), writes=("out_v",), dma=True, dsem="Dout")
    cm.projection(wv0T, lnx, lambda d: f"lnx_{d}", T, T, v_cb, True,
                  hook=lambda: emit_some_quant(4))
    emit_some_quant(100)

    def rope(sb, pref, base):
        for hc in range(DC):
            sl = slice(hc * T, (hc + 1) * T)
            rtag = f"{pref}_{hc}"
            ri = (base + hc) % 2
            rt2 = rt2b[ri]
            rt = f"rt2{ri}"
            p.op("sync", lambda e, sl=sl, sb=sb, rt2=rt2: e.dma_start(out=rt2[0:64, :], in_=sb[64:128, sl]),
                 reads=(rtag,), writes=(rt,), dma=True, dsem=f"Drt{ri}")
            p.op("sync", lambda e, sl=sl, sb=sb, rt2=rt2: e.dma_start(out=rt2[64:128, :], in_=sb[0:64, sl]),
                 reads=(rtag,), writes=(rt,), dma=True, dsem=f"Drt{ri}")
            p.op("vector", lambda e, sl=sl, sb=sb: e.tensor_tensor(rt1[:, :], sb[:, sl], ccat[:, :], ALU.mult),
                 reads=(rtag, "ccat"), writes=("rt1",))
            p.op("vector", lambda e, rt2=rt2: e.tensor_tensor(rt2[:, :], rt2[:, :], scat[:, :], ALU.mult),
                 reads=(rt, "scat"), writes=(rt,))
            p.op("vector", lambda e, sl=sl, sb=sb, rt2=rt2: e.tensor_tensor(sb[:, sl], rt1[:, :], rt2[:, :], ALU.add),
                 reads=("rt1", rt), writes=(f"{pref}r_{hc}", rtag))
    rope(q_sb, "q", 0)
    rope(k_sb, "k", 0)

    p.op("gpsimd", lambda e: e.dma_start(out=q0.rearrange("(d p) t -> p d t", p=128), in_=q_sb[:, :]),
         reads=tuple(f"qr_{h}" for h in range(DC)), writes=("out_q",), dma=True, dsem="Dout")
    p.op("gpsimd", lambda e: e.dma_start(out=k0.rearrange("(d p) t -> p d t", p=128), in_=k_sb[:, :]),
         reads=tuple(f"kr_{h}" for h in range(DC)), writes=("out_k",), dma=True, dsem="Dout")
    p.finish(["out_q", "out_k", "out_v", "out_gu", "out_pr", "out_wq"])
    p.emit()
    return nc



def build_attention(layer):
    nc = bass.Bass("TRN2", target_bir_lowering=False)
    qh_in = nc.dram_tensor("qh", [2 * 128, S], F32, kind="ExternalInput").ap()
    kh_in = nc.dram_tensor("kh", [2 * 128, S], F32, kind="ExternalInput").ap()
    vh_in = nc.dram_tensor("vh", [2, S, 128], F32, kind="ExternalInput").ap()
    ident_in = nc.dram_tensor("identf", [128, 128], F32, kind="ExternalInput").ap()
    if layer == 0:
        masks_in = nc.dram_tensor("masks", [4 * 128, 512], F32, kind="ExternalInput").ap()
    else:
        dts_in = nc.dram_tensor("dts", [2 * 5 * 128, 512], F32, kind="ExternalInput").ap()
        btab_in = nc.dram_tensor("btab", [128, 2 * 68], F32, kind="ExternalInput").ap()
    attn_out = nc.dram_tensor("attn", [2 * 128, S], F32, kind="ExternalOutput").ap()

    p = Prog(nc)
    q_sb = p.sb("q_sb", [128, 2 * S], BF16)
    k_sb = p.sb("k_sb", [128, 2 * S], BF16)
    v_sb = p.sb("v_sb", [128, 2 * 64 * 132], BF16)
    ao_sb = p.sb("ao_sb", [128, 2 * S], BF16)
    ident_bf = p.sb("ident_bf", [128, 128], BF16)
    s2 = [p.sb("s2a", [128, 512], F32), p.sb("s2b", [128, 512], F32),
          p.sb("s2c", [128, 512], F32)]
    p4 = [p.sb("p4a", [128, 512], BF16), p.sb("p4b", [128, 512], BF16),
          p.sb("p4c", [128, 512], BF16)]
    att_st = p.sb("att_st", [128, 128], BF16)
    dcol = p.sb("dcol", [128, 1], F32)
    rden = p.sb("rden", [128, 1], F32)
    if layer == 0:
        masks = p.sb("masks_sb", [128, 4 * 512], BF16)
    else:
        dts = p.sb("dts_sb", [128, 2 * 5 * 512], F32)
        btab = p.sb("btab_sb", [128, 2 * 68], F32)
    p_s = [p.ps("p_s0", [128, 512]), p.ps("p_s1", [128, 512]), p.ps("p_s2", [128, 512])]
    p_av = [p.ps(f"p_av{i}", [128, 132]) for i in range(4)]
    p_tr = p.ps("p_tr", [128, 128], BF16)

    p.op("gpsimd", lambda e: e.memset(v_sb[:, :], 1.0), writes=("v",))
    p.op("gpsimd", lambda e: e.dma_start(out=ident_bf[:, :], in_=ident_in),
         writes=("ident",), dma=True, dsem="Din")
    if layer == 0:
        p.op("gpsimd", lambda e: e.dma_start(
            out=masks[:, :], in_=masks_in.rearrange("(m p) f -> p m f", p=128)),
            writes=("masks",), dma=True, dsem="Din")
    else:
        p.op("gpsimd", lambda e: e.dma_start(
            out=dts[:, :], in_=dts_in.rearrange("(m p) f -> p m f", p=128)),
            writes=("dts",), dma=True, dsem="Din")
        p.op("sync", lambda e: e.dma_start(out=btab[:, :], in_=btab_in),
             writes=("btab",), dma=True, dsem="Din")
    p.op("gpsimd", lambda e: e.dma_start(
        out=q_sb[:, :], in_=qh_in.rearrange("(s p) t -> p s t", p=128)),
        writes=("q",), dma=True, dsem="Din")
    p.op("gpsimd", lambda e: e.dma_start(
        out=k_sb[:, :], in_=kh_in.rearrange("(s p) t -> p s t", p=128)),
        writes=("k",), dma=True, dsem="Din")
    for s in range(2):
        for kg in range(8):
            src = vh_in[s, kg * 1024:(kg + 1) * 1024, :].rearrange("(kt p) c -> p kt c", p=128)
            dst = bass.AP(v_sb, s * 64 * 132 + kg * 8 * 132,
                          [[2 * 64 * 132, 128], [132, 8], [1, 128]])
            p.op("gpsimd", lambda e, src=src, dst=dst: e.dma_start(out=dst, in_=src),
                 writes=("v",), dma=True, dsem="Din")

    pend_epi = [None]
    for slot in range(2):
        for g in range(NCH):
            nkt = 4 * (g + 1)
            qcol = slot * S + g * 512

            def emit_qk(kt, slot=slot, qcol=qcol):
                pi = kt % 3
                l = k_sb[:, slot * S + kt * 128: slot * S + kt * 128 + 128]
                r = q_sb[:, qcol: qcol + 512]
                p.op("tensor", lambda e, l=l, r=r, pi=pi:
                     e.matmul(p_s[pi][:, :], l, r, start=True, stop=True),
                     reads=("k", "q"), writes=(f"p_s{pi}",))

            def emit_proc(kt, slot=slot, g=g, nkt=nkt):
                pi = kt % 3
                pstag = f"p_s{pi}"
                if layer == 0:
                    p.op("scalar", lambda e, pi=pi: e.activation(s2[pi][:, :], p_s[pi][:, :], AF.Square),
                         reads=(pstag,), writes=(f"s2{pi}",))
                    p.op("vector", lambda e, pi=pi: e.tensor_tensor(
                        p4[pi][:, :], s2[pi][:, :], s2[pi][:, :], ALU.mult),
                         reads=(f"s2{pi}",), writes=(f"p4{pi}",))
                    if kt >= 4 * g:
                        r_ = kt - 4 * g
                        p.op("vector", lambda e, pi=pi, r_=r_: e.tensor_tensor(
                            p4[pi][:, :], p4[pi][:, :], masks[:, r_ * 512:(r_ + 1) * 512], ALU.mult),
                             reads=("masks", f"p4{pi}"), writes=(f"p4{pi}",))
                else:
                    dv = 1 + (kt - 4 * g) if kt >= 4 * g else 0
                    dsl = slice((slot * 5 + dv) * 512, (slot * 5 + dv) * 512 + 512)
                    p.op("vector", lambda e, pi=pi, dsl=dsl: e.scalar_tensor_tensor(
                        s2[pi][:, :], p_s[pi][:, :], SCALE_L1, dts[:, dsl], ALU.mult, ALU.add),
                         reads=(pstag, "dts"), writes=(f"s2{pi}",))
                    dk = 4 * g - kt
                    bcol = btab[:, slot * 68 + dk + 3: slot * 68 + dk + 4]
                    p.op("scalar", lambda e, pi=pi, bcol=bcol: e.activation(
                        p4[pi][:, :], s2[pi][:, :], AF.Exp, bias=bcol),
                         reads=(f"s2{pi}", "btab"), writes=(f"p4{pi}",))
                first = (kt == 0)
                last = (kt == nkt - 1)
                vrhs = bass.AP(v_sb, slot * 64 * 132 + kt * 132,
                               [[2 * 64 * 132, 128], [1, 129]])
                lhs0 = p4[pi][:, 0:128]
                if first:
                    p.op("tensor", lambda e, lhs0=lhs0, vrhs=vrhs, last=last:
                         e.matmul(p_av[0][:, :129], lhs0, vrhs, start=True, stop=last),
                         reads=(f"p4{pi}", "v"),
                         writes=("p_av0", "p_av1", "p_av2", "p_av3"), noinc=True)
                else:
                    p.op("tensor", lambda e, lhs0=lhs0, vrhs=vrhs, last=last:
                         e.matmul(p_av[0][:, :129], lhs0, vrhs, start=False, stop=last),
                         reads=(f"p4{pi}",), writes=(), noinc=True)
                for qs in (1, 2):
                    lh = p4[pi][:, qs * 128:(qs + 1) * 128]
                    p.raw("tensor", lambda e, lh=lh, vrhs=vrhs, first=first, last=last, qs=qs:
                          e.matmul(p_av[qs][:, :129], lh, vrhs, start=first, stop=last))
                lh3 = p4[pi][:, 384:512]
                if last:
                    p.op("tensor", lambda e, lh3=lh3, vrhs=vrhs, first=first:
                         e.matmul(p_av[3][:, :129], lh3, vrhs, start=first, stop=True),
                         reads=(f"p4{pi}", "v"),
                         writes=("p_av0", "p_av1", "p_av2", "p_av3"))
                else:
                    p.op("tensor", lambda e, lh3=lh3, vrhs=vrhs, first=first:
                         e.matmul(p_av[3][:, :129], lh3, vrhs, start=first, stop=False),
                         reads=(f"p4{pi}", "v"), writes=())

            emit_qk(0)
            emit_qk(1)
            if pend_epi[0] is not None:
                pend_epi[0]()
                pend_epi[0] = None
            for kt in range(2, nkt):
                emit_qk(kt)
                emit_proc(kt - 2)
            emit_proc(nkt - 2)
            emit_proc(nkt - 1)

            def emit_epi(slot=slot, g=g):
                for qs in range(4):
                    ptag = "p_av3"
                    if layer == 0:
                        p.op("vector", lambda e, qs=qs: e.tensor_scalar(
                            dcol[:, :], p_av[qs][:, 128:129], 1e-6, None, ALU.max),
                             reads=(ptag,), writes=("dcol",))
                    else:
                        p.op("vector", lambda e, qs=qs: e.tensor_copy(dcol[:, :], p_av[qs][:, 128:129]),
                             reads=(ptag,), writes=("dcol",))
                    p.op("vector", lambda e: e.reciprocal(rden[:, :], dcol[:, :]),
                         reads=("dcol",), writes=("rden",))
                    p.op("vector", lambda e, qs=qs: e.tensor_scalar(
                        att_st[:, :], p_av[qs][:, 0:128], rden[:, 0:1], None, ALU.mult),
                         reads=(ptag, "rden"), writes=("att_st",))
                    p.op("tensor", lambda e: e.transpose(p_tr[:, :], att_st[:, :], ident_bf[:, :]),
                         reads=("att_st", "ident"), writes=("p_tr",))
                    dst = ao_sb[:, slot * S + g * 512 + qs * 128: slot * S + g * 512 + qs * 128 + 128]
                    p.op("scalar", lambda e, dst=dst: e.copy(dst, p_tr[:, :]),
                         reads=("p_tr",), writes=("ao",))
            pend_epi[0] = emit_epi

    if pend_epi[0] is not None:
        pend_epi[0]()
        pend_epi[0] = None
    p.op("gpsimd", lambda e: e.dma_start(
        out=attn_out.rearrange("(s p) t -> p s t", p=128), in_=ao_sb[:, :]),
        reads=("ao",), writes=("out_a",), dma=True, dsem="Dout")
    p.finish(["out_a"])
    p.emit()
    return nc



def build_launch_c():
    nc = bass.Bass("TRN2", target_bir_lowering=False)
    xT = nc.dram_tensor("xT", [D, T], F32, kind="ExternalInput").ap()
    attn0 = nc.dram_tensor("attn0", [D, T], F32, kind="ExternalInput").ap()
    wo0T = nc.dram_tensor("wo0T", [D, D], F32, kind="ExternalInput").ap()
    wq_hatT = nc.dram_tensor("wq_hatT", [D, D], BF16, kind="ExternalInput").ap()
    wq1 = nc.dram_tensor("wq1", [D, D], F32, kind="ExternalInput").ap()
    wk1 = nc.dram_tensor("wk1", [D, D], F32, kind="ExternalInput").ap()
    wv1 = nc.dram_tensor("wv1", [D, D], F32, kind="ExternalInput").ap()
    biases = nc.dram_tensor("biases", [128, 3 * DC], F32, kind="ExternalInput").ap()
    ident_in = nc.dram_tensor("identf", [128, 128], F32, kind="ExternalInput").ap()

    q1 = nc.dram_tensor("q1", [D, T], F32, kind="ExternalOutput").ap()
    k1 = nc.dram_tensor("k1", [D, T], F32, kind="ExternalOutput").ap()
    v1 = nc.dram_tensor("v1", [T, D], F32, kind="ExternalOutput").ap()
    x2 = nc.dram_tensor("x2", [D, T], F32, kind="ExternalOutput").ap()

    p = Prog(nc)
    cm = Common(p)
    x_sb = p.sb("x_sb", [128, DC * T], F32)
    lnb = p.sb("lnb", [128, DC * T], BF16)
    y_sb = p.sb("y_sb", [128, DC * T], BF16)
    vst = p.sb("vst", [128, 512], BF16)
    vtt = p.sb("vtt", [128, 8 * 128], BF16)
    ident_bf = p.sb("ident_bf", [128, 128], BF16)
    bias_sb = p.sb("bias_sb", [128, 3 * DC], F32)
    p_tr = p.ps("p_tr", [128, 128], BF16)
    amax = p.sb("amax", [128, 8], F32)
    mtmp = p.sb("mtmp", [128, 1], F32)
    rqrow = cm.Rrow
    rqrep = p.sb("rqrep", [128, T], F32)
    p_rq = cm.pproj[0]
    qi = p.sb("qi", [128, T], I32)
    ident_f32 = p.sb("ident_f32", [128, 128], F32)
    rq8 = p.sb("rq8", [8, 128], F32)

    p.op("gpsimd", lambda e: e.dma_start(out=ident_bf[:, :], in_=ident_in),
         writes=("ident",), dma=True, dsem="Din")
    p.op("sync", lambda e: e.dma_start(out=ident_f32[:, :], in_=ident_in),
         writes=("identf32",), dma=True, dsem="Din")
    p.op("sync", lambda e: e.dma_start(out=bias_sb[:, :], in_=biases),
         writes=("biasb",), dma=True, dsem="Din")
    p.op("sync", lambda e: e.dma_start(out=x_sb[:, :], in_=xT.rearrange("(d p) t -> p d t", p=128)),
         writes=tuple(f"x_{d}" for d in range(DC)), dma=True, dsem="Din")
    p.op("gpsimd", lambda e: e.dma_start(
        out=lnb[:, :], in_=attn0.rearrange("(d p) t -> p d t", p=128)),
        writes=tuple(f"a0_{d}" for d in range(DC)), dma=True, dsem="Din")

    def ych(oc, th):
        return y_sb[:, oc * T + th * 512: oc * T + th * 512 + 512]

    cm.projection(wo0T, lnb, lambda d: f"a0_{d}", T, T,
                  evac_simple(p, ych, lambda oc, th: f"y_{oc}"), True)
    cm.ln_post_add(lambda d: y_sb[:, d * T:(d + 1) * T], lambda d: f"y_{d}", False,
                   lambda d: x_sb[:, d * T:(d + 1) * T], lambda d: f"x_{d}", T)
    cm.ln_pre(lambda d: x_sb[:, d * T:(d + 1) * T], lambda d: f"x_{d}",
              lambda d: lnb[:, d * T:(d + 1) * T], lambda d: f"ln_{d}", T,
              alias_of=lambda d: f"a0_{d}")

    for tt in range(8):
        for d in range(DC):
            src = lnb[:, d * T + tt * 128: d * T + tt * 128 + 128]
            p.op("tensor", lambda e, src=src: e.transpose(p_tr[:, :], src, ident_bf[:, :]),
                 reads=(f"ln_{d}", "ident"), writes=("p_tr",))
            if d == 0:
                p.op("vector", lambda e, tt=tt: e.reduce_max(
                    amax[:, tt:tt + 1], p_tr[:, :], axis=AX.X, apply_absolute_value=True),
                     reads=("p_tr",), writes=("amax",))
            else:
                p.op("vector", lambda e: e.reduce_max(
                    mtmp[:, :], p_tr[:, :], axis=AX.X, apply_absolute_value=True),
                     reads=("p_tr",), writes=("mtmp",))
                p.op("vector", lambda e, tt=tt: e.tensor_tensor(
                    amax[:, tt:tt + 1], amax[:, tt:tt + 1], mtmp[:, :], ALU.max),
                     reads=("mtmp", "amax"), writes=("amax",))
    p.op("vector", lambda e: e.tensor_scalar(amax[:, :], amax[:, :], 1e-8, None, ALU.max),
         reads=("amax",), writes=("amax",))
    p.op("vector", lambda e: e.reciprocal(amax[:, :], amax[:, :]),
         reads=("amax",), writes=("amax",))
    p.op("vector", lambda e: e.tensor_scalar(amax[:, :], amax[:, :], 127.0, None, ALU.mult),
         reads=("amax",), writes=("amax",))
    p.op("tensor", lambda e: e.transpose(cm.pproj[0][0:8, 0:128], amax[:, :], ident_f32[:, :]),
         reads=("amax", "identf32"), writes=("pproj0",))
    p.op("vector", lambda e: e.tensor_copy(rq8[:, :], cm.pproj[0][0:8, 0:128]),
         reads=("pproj0",), writes=("rq8",))
    rq_dst = bass.AP(rqrow.tensor, 0, [[T, 1], [1, T]])
    p.op("sync", lambda e: e.dma_start(out=rq_dst, in_=rq8[:, :]),
         reads=("rq8",), writes=("Rrow",), dma=True, dsem="Drq")
    for th in range(2):
        sl = slice(th * 512, th * 512 + 512)
        p.op("tensor", lambda e, sl=sl: e.matmul(p_rq[:, :512], cm.ones_row, rqrow[:, sl],
                                                 start=True, stop=True),
             reads=("Rrow", "ones"), writes=("pproj0",))
        p.op("vector", lambda e, sl=sl: e.tensor_copy(rqrep[:, sl], p_rq[:, :512]),
             reads=("pproj0",), writes=("rqrep",))
    for d in range(DC):
        sl = d % 2
        p.op("vector", lambda e, d=d, sl=sl: e.tensor_tensor(
            cm.tln[sl][:, :], lnb[:, d * T:(d + 1) * T], rqrep[:, :], ALU.mult),
             reads=(f"ln_{d}", "rqrep"), writes=("tln0", "tln1"))
        p.op("vector", lambda e, sl=sl: e.tensor_scalar(cm.tln[sl][:, :], cm.tln[sl][:, :],
                                                        MAGIC, MAGIC, ALU.add, ALU.subtract),
             reads=(f"tln{sl}",), writes=("tln0", "tln1"))
        p.op("vector", lambda e, sl=sl: e.tensor_scalar(qi[:, :], cm.tln[sl][:, :], 0.0, None, ALU.add),
             reads=(f"tln{sl}",), writes=("qi",))
        p.op("vector", lambda e, d=d: e.tensor_scalar(
            lnb[:, d * T:(d + 1) * T], qi[:, :], 0.0, None, ALU.add),
             reads=("qi",), writes=(f"xq_{d}", f"ln_{d}"))
    cm.projection(wq_hatT, lnb, lambda d: f"xq_{d}", T, T,
                  evac_simple(p, ych, lambda oc, th: f"y_{oc}"), False)
    cm.ln_post_add(lambda d: y_sb[:, d * T:(d + 1) * T], lambda d: f"y_{d}", False,
                   lambda d: x_sb[:, d * T:(d + 1) * T], lambda d: f"x_{d}", T)
    p.op("sync", lambda e: e.dma_start(out=x2.rearrange("(d p) t -> p d t", p=128), in_=x_sb[:, :]),
         reads=tuple(f"x_{d}" for d in range(DC)), writes=("out_x2",), dma=True, dsem="Dout")
    cm.ln_pre(lambda d: x_sb[:, d * T:(d + 1) * T], lambda d: f"x_{d}",
              lambda d: lnb[:, d * T:(d + 1) * T], lambda d: f"ln2_{d}", T,
              alias_of=lambda d: f"xq_{d}")

    def qdst(oc, th):
        return y_sb[:, oc * T + th * 512: oc * T + th * 512 + 512]
    cm.projection(wq1, lnb, lambda d: f"ln2_{d}", T, T,
                  evac_simple(p, qdst, lambda oc, th: (f"qq_{oc}", f"y_{oc}")), True,
                  bias_sb=bias_sb[:, 0:DC], bias_tag="biasb")
    p.op("gpsimd", lambda e: e.dma_start(out=q1.rearrange("(d p) t -> p d t", p=128), in_=y_sb[:, :]),
         reads=tuple(f"qq_{d}" for d in range(DC)), writes=("out_q1",), dma=True, dsem="Dout")
    cm.projection(wk1, lnb, lambda d: f"ln2_{d}", T, T,
                  evac_simple(p, qdst, lambda oc, th: (f"kk_{oc}", f"qq_{oc}")), True,
                  bias_sb=bias_sb[:, DC:2 * DC], bias_tag="biasb")
    p.op("gpsimd", lambda e: e.dma_start(out=k1.rearrange("(d p) t -> p d t", p=128), in_=y_sb[:, :]),
         reads=tuple(f"kk_{d}" for d in range(DC)), writes=("out_k1",), dma=True, dsem="Dout")

    def v_cb(oc, th, ps_ap, pstag, bias_sb_, bias_tag_):
        bc = bias_sb_[:, oc:oc + 1]
        p.op("scalar", lambda e, ps_ap=ps_ap, bc=bc: e.activation(vst[:, :], ps_ap, AF.Identity, bias=bc),
             reads=(pstag, bias_tag_), writes=("vst",))
        for tt4 in range(4):
            tt = th * 4 + tt4
            p.op("tensor", lambda e, tt4=tt4: e.transpose(
                p_tr[:, :], vst[:, tt4 * 128:(tt4 + 1) * 128], ident_bf[:, :]),
                 reads=("vst", "ident"), writes=("p_tr",))
            dst = vtt[:, tt * 128:(tt + 1) * 128]
            p.op("scalar", lambda e, dst=dst: e.copy(dst, p_tr[:, :]),
                 reads=("p_tr",), writes=("vtt",))
        if th == 1:
            dst = bass.AP(v1.tensor, oc * 128, [[D, 128], [128 * D, 8], [1, 128]])
            src_ = bass.AP(vtt, 0, [[8 * 128, 128], [128, 8], [1, 128]])
            p.op("gpsimd", lambda e, dst=dst, src_=src_: e.dma_start(out=dst, in_=src_),
                 reads=("vtt",), writes=("out_v1",), dma=True, dsem="Dout")
    cm.projection(wv1, lnb, lambda d: f"ln2_{d}", T, T, v_cb, True,
                  bias_sb=bias_sb[:, 2 * DC:3 * DC], bias_tag="biasb")
    p.finish(["out_q1", "out_k1", "out_v1", "out_x2"])
    p.emit()
    return nc



def build_launch_e():
    nc = bass.Bass("TRN2", target_bir_lowering=False)
    x2 = nc.dram_tensor("x2", [D, T], F32, kind="ExternalInput").ap()
    attn1 = nc.dram_tensor("attn1", [D, T], F32, kind="ExternalInput").ap()
    wo1 = nc.dram_tensor("wo1", [D, D], F32, kind="ExternalInput").ap()
    bo_in = nc.dram_tensor("bo", [128, DC], F32, kind="ExternalInput").ap()
    wguT = nc.dram_tensor("wguT", [D, 16384], BF16, kind="ExternalInput").ap()
    wprT = nc.dram_tensor("wprT", [8192, D], BF16, kind="ExternalInput").ap()
    xout = nc.dram_tensor("xout", [D, T], F32, kind="ExternalOutput").ap()

    p = Prog(nc)
    cm = Common(p)
    x_sb = p.sb("x_sb", [128, DC * T], F32)
    u1 = p.sb("u1", [128, DC * T], BF16)
    yl = p.sb("yl", [128, DC * T], BF16)
    y3 = p.sb("y3", [128, DC * 512], BF16)
    bo_sb = p.sb("bo_sb", [128, DC], F32)
    gs = p.sb("gs", [128, 512], BF16)
    _pg = p.ps("p_g", [128, 512])
    _pu = p.ps("p_u", [128, 512])
    p_g = [_pg, _pg]
    p_u = [_pu, _pu]

    p.op("sync", lambda e: e.dma_start(out=bo_sb[:, :], in_=bo_in), writes=("bo",), dma=True, dsem="Din")
    p.op("sync", lambda e: e.dma_start(out=x_sb[:, :], in_=x2.rearrange("(d p) t -> p d t", p=128)),
         writes=tuple(f"x_{d}" for d in range(DC)), dma=True, dsem="Din")
    p.op("gpsimd", lambda e: e.dma_start(
        out=u1[:, :], in_=attn1.rearrange("(d p) t -> p d t", p=128)),
        writes=tuple(f"a1_{d}" for d in range(DC)), dma=True, dsem="Din")

    def ych(oc, th):
        return yl[:, oc * T + th * 512: oc * T + th * 512 + 512]
    cm.projection(wo1, u1, lambda d: f"a1_{d}", T, T,
                  evac_simple(p, ych, lambda oc, th: f"y_{oc}"), True,
                  bias_sb=bo_sb, bias_tag="bo")
    cm.ln_post_add(lambda d: yl[:, d * T:(d + 1) * T], lambda d: f"y_{d}", False,
                   lambda d: x_sb[:, d * T:(d + 1) * T], lambda d: f"x_{d}", T)
    cm.ln_pre(lambda d: x_sb[:, d * T:(d + 1) * T], lambda d: f"x_{d}",
              lambda d: yl[:, d * T:(d + 1) * T], lambda d: f"ln3_{d}", T,
              alias_of=lambda d: f"y_{d}")

    for th in range(2):
        for hh in range(2):
            for hc in range(32):
                hrow = hh * HHALF + hc * 128
                wb = cm.wbuf[hc % 2]
                wtag = f"wb{hc % 2}"
                gsrc = wguT[:, hrow:hrow + 128].rearrange("(d p) c -> p d c", p=128)
                usrc = wguT[:, 8192 + hrow: 8192 + hrow + 128].rearrange("(d p) c -> p d c", p=128)
                gdst = bass.AP(wb, 0, [[4096, 128], [256, 16], [1, 128]])
                udst = bass.AP(wb, 128, [[4096, 128], [256, 16], [1, 128]])
                fns = [lambda e, s=gsrc, gdst=gdst: e.dma_start(out=gdst, in_=s),
                       lambda e, s=usrc, udst=udst: e.dma_start(out=udst, in_=s)]
                p.multi("scalar", fns, reads=(), writes=(wtag,), dma=True, dsem=f"Dw{hc % 2}")
                pi = hc % 2
                fng, fnu = [], []
                for d in range(DC):
                    lg = wb[:, d * 256:d * 256 + 128]
                    lu = wb[:, d * 256 + 128:d * 256 + 256]
                    r = yl[:, d * T + th * 512: d * T + th * 512 + 512]
                    fng.append(lambda e, l=lg, r=r, f=(d == 0), la=(d == DC - 1), pi=pi:
                               e.matmul(p_g[pi][:, :], l, r, start=f, stop=la))
                    fnu.append(lambda e, l=lu, r=r, f=(d == 0), la=(d == DC - 1), pi=pi:
                               e.matmul(p_u[pi][:, :], l, r, start=f, stop=la))
                reads = (wtag,) + tuple(f"ln3_{d}" for d in range(DC))
                p.multi("tensor", fng, reads=reads, writes=("p_g0", "p_g1"))
                p.multi("tensor", fnu, reads=reads, writes=("p_u0", "p_u1"))
                p.op("scalar", lambda e, pi=pi: e.activation(gs[:, :], p_g[pi][:, :], AF.Silu),
                     reads=(f"p_g{pi}",), writes=("gs",))
                hdst = u1[:, hc * 512:(hc + 1) * 512]
                p.op("vector", lambda e, pi=pi, hdst=hdst: e.tensor_tensor(
                    hdst, gs[:, :], p_u[pi][:, :], ALU.mult),
                     reads=("gs", f"p_u{pi}"), writes=(f"h_{hc}", f"a1_{hc // 2}"))
            for oc in range(DC):
                wb = cm.wbuf[oc % 2]
                wtag = f"wb{oc % 2}"
                src = wprT[hh * HHALF:(hh + 1) * HHALF, oc * 128:(oc + 1) * 128]
                src_r = src.rearrange("(hc p) c -> p hc c", p=128)
                fns = []
                for piece in range(4):
                    fns.append(lambda e, piece=piece, wb=wb, src_r=src_r: e.dma_start(
                        out=wb[:, piece * 1024:(piece + 1) * 1024],
                        in_=src_r[:, piece * 8:(piece + 1) * 8, :]))
                p.multi("sync", fns, reads=(), writes=(wtag,), dma=True, dsem=f"Dw{oc % 2}")
                pi = oc % 2
                fns = []
                for hc in range(32):
                    l = wb[:, hc * 128:(hc + 1) * 128]
                    r = u1[:, hc * 512:(hc + 1) * 512]
                    fns.append(lambda e, l=l, r=r, f=(hc == 0), la=(hc == 31), pi=pi:
                               e.matmul(cm.pproj[pi][:, :], l, r, start=f, stop=la))
                reads = (wtag,) + tuple(f"h_{hc}" for hc in range(32))
                p.multi("tensor", fns, reads=reads, writes=(f"pproj{pi}",))
                ydst = y3[:, oc * 512:(oc + 1) * 512]
                if hh == 0:
                    p.op("scalar", lambda e, pi=pi, ydst=ydst: e.copy(ydst, cm.pproj[pi][:, :]),
                         reads=(f"pproj{pi}",), writes=(f"y3_{oc}",))
                else:
                    p.op("vector", lambda e, pi=pi, ydst=ydst: e.tensor_tensor(
                        ydst, ydst, cm.pproj[pi][:, :], ALU.add),
                         reads=(f"pproj{pi}", f"y3_{oc}"), writes=(f"y3_{oc}",))
        cm.ln_post_add(lambda d: y3[:, d * 512:(d + 1) * 512], lambda d: f"y3_{d}", False,
                       lambda d, th=th: x_sb[:, d * T + th * 512: d * T + th * 512 + 512],
                       lambda d: f"x_{d}", 512)

    p.op("sync", lambda e: e.dma_start(out=xout.rearrange("(d p) t -> p d t", p=128), in_=x_sb[:, :]),
         reads=tuple(f"x_{d}" for d in range(DC)), writes=("out_x",), dma=True, dsem="Dout")
    p.finish(["out_x"])
    p.emit()
    return nc



def host_tables():
    half = 64
    theta = 10000.0 ** (-np.arange(half, dtype=np.float64) * 2.0 / 128.0)
    pos = np.arange(S, dtype=np.float64)
    ang = pos[:, None] * theta[None, :]
    cos = np.cos(ang).astype(np.float32)
    sin = np.sin(ang).astype(np.float32)
    ident = np.eye(128, dtype=np.float32)
    ppp = np.arange(128)[:, None]
    fff = np.arange(512)[None, :]
    masks = np.stack([(fff >= ppp + 128 * r).astype(np.float32) for r in range(4)], 0)
    start = 2.0 ** (-8.0 / H)
    slopes = np.array([start * start ** i for i in range(H)], dtype=np.float64)
    d0 = (ppp - fff).astype(np.float64)
    return cos, sin, ident, masks, slopes, d0


_CACHE = {}


def _ncs():
    if "a" not in _CACHE:
        _CACHE["a"] = build_launch_a()
        _CACHE["b0"] = build_attention(0)
        _CACHE["c"] = build_launch_c()
        _CACHE["b1"] = build_attention(1)
        _CACHE["e"] = build_launch_e()
    return _CACHE


def kernel(L0_attn_x, L0_attn_Wq, L0_attn_Wk, L0_attn_Wv, L0_attn_Wo,
           L0_ffn_weight, L1_attn_wq, L1_attn_bq, L1_attn_wk, L1_attn_bk,
           L1_attn_wv, L1_attn_bv, L1_attn_wo, L1_attn_bo,
           L1_ffn_w_gate_up, L1_ffn_w_proj):
    cos, sin, ident, masks, slopes, d0 = host_tables()
    ncs = _ncs()
    x = np.ascontiguousarray(np.asarray(L0_attn_x, dtype=np.float32).reshape(S, D))
    wq0T = np.ascontiguousarray(np.asarray(L0_attn_Wq, np.float32).T)
    wk0T = np.ascontiguousarray(np.asarray(L0_attn_Wk, np.float32).T)
    wv0T = np.ascontiguousarray(np.asarray(L0_attn_Wv, np.float32).T)
    wo0T = np.ascontiguousarray(np.asarray(L0_attn_Wo, np.float32).T)
    w0T = np.ascontiguousarray(np.asarray(L0_ffn_weight, np.float32).T)
    wgu_bf = np.asarray(L1_ffn_w_gate_up, np.float32).astype(BF)
    wpr_bf = np.asarray(L1_ffn_w_proj, np.float32).astype(BF)

    tok_rows = {c: np.r_[np.arange(512 * c, 512 * c + 512),
                         np.arange(512 * (15 - c), 512 * (15 - c) + 512)]
                for c in range(NC)}

    in_a = []
    for c in range(NC):
        rows = tok_rows[c]
        cc = cos[rows].T
        ss = sin[rows].T
        in_a.append({
            "xT": np.ascontiguousarray(x[rows].T),
            "wq0T": wq0T, "wk0T": wk0T, "wv0T": wv0T,
            "ccat": np.ascontiguousarray(np.concatenate([cc, cc], 0)),
            "scat": np.ascontiguousarray(np.concatenate([-ss, ss], 0)),
            "w0T": w0T,
            "w0T_rows": np.ascontiguousarray(w0T[256 * c:256 * c + 256]),
            "wgu": np.ascontiguousarray(wgu_bf[2048 * c:2048 * (c + 1)]),
            "wpr": np.ascontiguousarray(wpr_bf[256 * c:256 * (c + 1)]),
            "identf": ident,
        })
    ra = run_bass_kernel_spmd(ncs["a"], in_a, core_ids=list(range(NC))).results

    q0f = np.zeros((H, 128, S), np.float32)
    k0f = np.zeros((H, 128, S), np.float32)
    v0f = np.zeros((H, S, 128), np.float32)
    for c in range(NC):
        rows = tok_rows[c]
        q0c = np.asarray(ra[c]["q0"])
        k0c = np.asarray(ra[c]["k0"])
        v0c = np.asarray(ra[c]["v0"])
        for h in range(H):
            q0f[h][:, rows] = q0c[h * 128:(h + 1) * 128]
            k0f[h][:, rows] = k0c[h * 128:(h + 1) * 128]
            v0f[h][rows, :] = v0c[:, h * 128:(h + 1) * 128]
    wq_hatT = np.ascontiguousarray(
        np.concatenate([np.asarray(ra[c]["wq_hat"]) for c in range(NC)], 0))
    wgu_hat = np.concatenate([np.asarray(ra[c]["wgu_hat"]) for c in range(NC)], 0)
    wpr_hat = np.concatenate([np.asarray(ra[c]["wpr_hat"]) for c in range(NC)], 0)
    wguT = np.ascontiguousarray(wgu_hat.T)
    wprT = np.ascontiguousarray(wpr_hat.T)

    in_b = []
    for c in range(NC):
        hs = heads_of(c)
        in_b.append({
            "qh": np.ascontiguousarray(np.concatenate([q0f[h] for h in hs], 0)),
            "kh": np.ascontiguousarray(np.concatenate([k0f[h] for h in hs], 0)),
            "vh": np.ascontiguousarray(np.stack([v0f[h] for h in hs], 0)),
            "identf": ident,
            "masks": np.ascontiguousarray(masks.reshape(4 * 128, 512)),
        })
    rb = run_bass_kernel_spmd(ncs["b0"], in_b, core_ids=list(range(NC))).results
    attn0f = np.zeros((H, 128, S), np.float32)
    for c in range(NC):
        a = np.asarray(rb[c]["attn"])
        hs = heads_of(c)
        attn0f[hs[0]] = a[0:128]
        attn0f[hs[1]] = a[128:256]

    bq = np.asarray(L1_attn_bq, np.float32).reshape(DC, 128).T
    bk = np.asarray(L1_attn_bk, np.float32).reshape(DC, 128).T
    bv = np.asarray(L1_attn_bv, np.float32).reshape(DC, 128).T
    biases = np.ascontiguousarray(np.concatenate([bq, bk, bv], 1))
    in_c = []
    for c in range(NC):
        rows = tok_rows[c]
        in_c.append({
            "xT": in_a[c]["xT"],
            "attn0": np.ascontiguousarray(attn0f[:, :, rows].reshape(D, T)),
            "wo0T": wo0T, "wq_hatT": wq_hatT,
            "wq1": np.asarray(L1_attn_wq, np.float32),
            "wk1": np.asarray(L1_attn_wk, np.float32),
            "wv1": np.asarray(L1_attn_wv, np.float32),
            "biases": biases, "identf": ident,
        })
    rc = run_bass_kernel_spmd(ncs["c"], in_c, core_ids=list(range(NC))).results
    q1f = np.zeros((H, 128, S), np.float32)
    k1f = np.zeros((H, 128, S), np.float32)
    v1f = np.zeros((H, S, 128), np.float32)
    x2f = {}
    for c in range(NC):
        rows = tok_rows[c]
        q1c = np.asarray(rc[c]["q1"])
        k1c = np.asarray(rc[c]["k1"])
        v1c = np.asarray(rc[c]["v1"])
        x2f[c] = np.asarray(rc[c]["x2"])
        for h in range(H):
            q1f[h][:, rows] = q1c[h * 128:(h + 1) * 128]
            k1f[h][:, rows] = k1c[h * 128:(h + 1) * 128]
            v1f[h][rows, :] = v1c[:, h * 128:(h + 1) * 128]

    in_d = []
    for c in range(NC):
        hs = heads_of(c)
        dts = []
        for h in hs:
            dh = slopes[h] * d0
            vars_ = [dh]
            for r in range(4):
                m = (np.arange(512)[None, :] < np.arange(128)[:, None] + 128 * r)
                vars_.append(np.where(m, NEG_BIG, dh))
            dts.append(np.stack(vars_, 0))
        dts = np.ascontiguousarray(np.stack(dts, 0).astype(np.float32).reshape(2 * 5 * 128, 512))
        btab = np.zeros((128, 2 * 68), np.float32)
        for si, h in enumerate(hs):
            for dk in range(-3, 64):
                btab[:, si * 68 + dk + 3] = np.float32(-slopes[h] * 128.0 * dk)
        in_d.append({
            "qh": np.ascontiguousarray(np.concatenate([q1f[h] for h in hs], 0)),
            "kh": np.ascontiguousarray(np.concatenate([k1f[h] for h in hs], 0)),
            "vh": np.ascontiguousarray(np.stack([v1f[h] for h in hs], 0)),
            "identf": ident, "dts": dts, "btab": btab,
        })
    rd = run_bass_kernel_spmd(ncs["b1"], in_d, core_ids=list(range(NC))).results
    attn1f = np.zeros((H, 128, S), np.float32)
    for c in range(NC):
        a = np.asarray(rd[c]["attn"])
        hs = heads_of(c)
        attn1f[hs[0]] = a[0:128]
        attn1f[hs[1]] = a[128:256]

    bo = np.ascontiguousarray(np.asarray(L1_attn_bo, np.float32).reshape(DC, 128).T)
    in_e = []
    for c in range(NC):
        rows = tok_rows[c]
        in_e.append({
            "x2": x2f[c],
            "attn1": np.ascontiguousarray(attn1f[:, :, rows].reshape(D, T)),
            "wo1": np.asarray(L1_attn_wo, np.float32),
            "bo": bo, "wguT": wguT, "wprT": wprT,
        })
    re_ = run_bass_kernel_spmd(ncs["e"], in_e, core_ids=list(range(NC))).results

    out = np.zeros((S, D), np.float32)
    for c in range(NC):
        rows = tok_rows[c]
        out[rows] = np.asarray(re_[c]["xout"]).T
    return out.reshape(1, S, D)


# version 05
# speedup vs baseline: 1.9006x; 1.0680x over previous
import contextlib
import numpy as np
import ml_dtypes
import concourse.bass as bass
import concourse.mybir as mybir
from concourse.bass_utils import run_bass_kernel_spmd

F32 = mybir.dt.float32
BF16 = mybir.dt.bfloat16
I32 = mybir.dt.int32
AF = mybir.ActivationFunctionType
ALU = mybir.AluOpType
AX = mybir.AxisListType

NC = 8
S = 8192
D = 2048
H = 16
CH = 512
T = 1024
NCH = 16
DC = 16
HHALF = 4096
EPS_LN = 1e-5
SCALE_L1 = float(1.0 / np.sqrt(np.float32(128.0)))
NEG_BIG = -1e30
MAGIC = 1.5 * 2.0 ** 23
BF = ml_dtypes.bfloat16
ENGINES = ("sync", "scalar", "vector", "tensor", "gpsimd")
CSEM = {"scalar": "Ts", "vector": "Tv", "tensor": "Tt", "gpsimd": "Tg"}
QSUF = {"sync": "Y", "scalar": "S", "gpsimd": "G"}
CUMU = ("Din", "Dout")


def heads_of(core):
    return (core, core + 8)


def tile_w(wT):
    Dn, O = wT.shape
    out = np.empty((8 * 128, (Dn // 128) * 256), dtype=wT.dtype)
    for og in range(8):
        sl = wT[:, og * 256:(og + 1) * 256]
        out[og * 128:(og + 1) * 128] = (
            sl.reshape(Dn // 128, 128, 256).transpose(1, 0, 2).reshape(128, -1))
    return np.ascontiguousarray(out)


class Prog:

    def __init__(self, nc):
        self.nc = nc
        self.es = contextlib.ExitStack()
        self.ops = {e: [] for e in ENGINES}
        self.sems = {}
        self.cnt = {}
        self.lastwait = {}
        self.track = {}

    def sem(self, name):
        if name not in self.sems:
            self.sems[name] = self.es.enter_context(self.nc.semaphore(name))
            self.cnt[name] = 0
        return self.sems[name]

    def sb(self, name, shape, dtype):
        return self.es.enter_context(self.nc.sbuf_tensor(name, shape, dtype))

    def ps(self, name, shape, dtype=F32):
        return self.es.enter_context(self.nc.psum_tensor(name, shape, dtype))

    def _emit_waits(self, eng, toks):
        need = {}
        for (sname, val) in toks:
            if val > need.get(sname, 0):
                need[sname] = val
        out = []
        for sname, val in need.items():
            key = (eng, sname)
            if self.lastwait.get(key, 0) < val:
                self.lastwait[key] = val
                out.append((self.sem(sname), val))
        return out

    def _dep_toks(self, reads, writes):
        toks = []
        for t in reads:
            rec = self.track.get(t)
            if rec and rec[0] is not None:
                toks.append(rec[0])
        for t in writes:
            rec = self.track.get(t)
            if rec:
                if rec[0] is not None:
                    toks.append(rec[0])
                toks.extend(rec[1].items())
        toks = [(s, self.cnt[s]) if s.startswith(CUMU) else (s, v) for (s, v) in toks]
        return toks

    def op(self, eng, fn, reads=(), writes=(), dma=False, noinc=False, dsem=None):
        waits = self._emit_waits(eng, self._dep_toks(reads, writes))
        sname = (dsem + QSUF[eng]) if dma else CSEM[eng]
        sem = self.sem(sname)
        if noinc:
            tok = None
        else:
            self.cnt[sname] += 16 if dma else 1
            tok = (sname, self.cnt[sname])
        inc_by = 16 if dma else 1

        def closure(e, waits=waits, fn=fn, sem=sem, inc_by=inc_by, noinc=noinc):
            for (s, v) in waits:
                e.wait_ge(s, v)
            ins = fn(e)
            if not noinc:
                ins.then_inc(sem, inc_by)

        self.ops[eng].append(closure)
        if tok is not None:
            for t in writes:
                self.track[t] = [tok, {}]
            for t in reads:
                rec = self.track.setdefault(t, [None, {}])
                if tok[1] > rec[1].get(tok[0], 0):
                    rec[1][tok[0]] = tok[1]
        return tok

    def raw(self, eng, fn):
        self.ops[eng].append(lambda e, fn=fn: fn(e))

    def multi(self, eng, fns, reads=(), writes=(), dma=False, dsem=None):
        n = len(fns)
        if n == 1:
            return self.op(eng, fns[0], reads=reads, writes=writes, dma=dma, dsem=dsem)
        if dma:
            self.op(eng, fns[0], reads=reads, writes=writes, dma=True, dsem=dsem)
            sname = dsem + QSUF[eng]
            sem = self.sem(sname)
            for f in fns[1:]:
                self.cnt[sname] += 16
                self.ops[eng].append(lambda e, f=f, sem=sem: f(e).then_inc(sem, 16))
            tok = (sname, self.cnt[sname])
            for t in writes:
                self.track[t] = [tok, {}]
            return tok
        self.op(eng, fns[0], reads=reads, writes=writes, dma=dma, noinc=True)
        for f in fns[1:-1]:
            self.raw(eng, f)
        return self.op(eng, fns[-1], reads=reads, writes=writes, dma=dma)

    def finish(self, out_tags):
        toks = []
        for t in out_tags:
            rec = self.track.get(t)
            if rec and rec[0] is not None:
                toks.append(rec[0])
        waits = self._emit_waits("sync", toks)

        def closure(e, waits=waits):
            for (s, v) in waits:
                e.wait_ge(s, v)
        self.ops["sync"].append(closure)

    def emit(self):
        with self.nc.Block() as blk:
            for e in ENGINES:
                if not self.ops[e]:
                    continue
                dec = getattr(blk, e)

                def body(engobj, e=e):
                    for f in self.ops[e]:
                        f(engobj)
                dec(body)



class Common:
    def __init__(self, p, wbuf_cols=DC * 256):
        self.p = p
        self.xsq = [p.sb("xsq0", [128, T], BF16), p.sb("xsq1", [128, T], BF16)]
        self.p_stat = p.ps("p_stat", [1, 2 * T])
        self.p_mean = self.p_stat[:, 0:T]
        self.p_sq = self.p_stat[:, T:2 * T]
        self.srow = p.sb("statsrow", [1, T], F32)[:, :]
        self.srow2 = p.sb("statsrow2", [1, T], F32)[:, :]
        self.Rrow = p.sb("rrow", [1, T], F32)[:, :]
        self.MRrow = p.sb("mrrow", [1, T], F32)[:, :]
        self.Rrep = p.sb("Rrep", [128, T], F32)
        self.MRrep = p.sb("MRrep", [128, T], F32)
        self.ones_f32 = p.sb("ones_f32", [128, 1], F32)[:, :]
        self.ones_bf = p.sb("ones_bf", [128, 1], BF16)[:, :]
        self.ones_row = p.sb("ones_row", [1, 128], F32)[:, :]
        self.eps1 = p.sb("eps1", [1, 1], F32)
        _tl = p.sb("tlnbuf", [128, T], F32)
        self.tln = [_tl, _tl]
        self.wbuf = [p.sb("wbufA", [128, wbuf_cols], BF16),
                     p.sb("wbufB", [128, wbuf_cols], BF16)]
        self.pproj = [p.ps("pprojA", [128, 512]), p.ps("pprojB", [128, 512])]
        self.p_R = self.pproj[0]
        self.p_MR = self.pproj[1]
        p.op("gpsimd", lambda e: e.memset(self.ones_f32[:, :], 1.0), writes=("ones",))
        p.op("gpsimd", lambda e: e.memset(self.ones_bf[:, :], 1.0), writes=("ones",))
        p.op("gpsimd", lambda e: e.memset(self.ones_row[:, :], 1.0), writes=("ones",))
        p.op("gpsimd", lambda e: e.memset(self.eps1[:, :], EPS_LN), writes=("ones",))

    def ln_stats(self, src_ap_of, src_tag_of, src_is_f32, tcols):
        p = self.p
        nh = (tcols + 511) // 512
        for d in range(DC):
            src = src_ap_of(d)
            stag = src_tag_of(d)
            sq = self.xsq[d % 2]
            p.op("scalar", lambda e, s=src, o=sq, w=tcols: e.activation(o[:, :w], s, AF.Square),
                 reads=(stag,), writes=(f"xsq{d % 2}",))
            for th in range(nh):
                w = min(512, tcols - th * 512)
                sl = slice(th * 512, th * 512 + w)
                first, last = (d == 0), (d == DC - 1)
                ones = self.ones_f32 if src_is_f32 else self.ones_bf
                p.op("tensor", lambda e, s=src, sl=sl, f=first, l=last, o=ones:
                     e.matmul(self.p_mean[:, sl], o, s[:, sl], start=f, stop=l),
                     reads=(stag, "ones"),
                     writes=("p_mean",) if (first or last) else ())
                p.op("tensor", lambda e, sl=sl, q=sq, f=first, l=last:
                     e.matmul(self.p_sq[:, sl], self.ones_bf, q[:, sl], start=f, stop=l),
                     reads=(f"xsq{d % 2}", "ones"),
                     writes=("p_sq",) if (first or last) else ())
        self.ln_finish(tcols)

    def ln_finish(self, tcols):
        p = self.p
        w = tcols
        nh = (tcols + 511) // 512
        p.op("vector", lambda e: e.tensor_scalar(self.srow[:, :w], self.p_mean[:, :w],
                                                 1.0 / D, None, ALU.mult),
             reads=("p_mean",), writes=("srow",))
        p.op("vector", lambda e: e.tensor_tensor(self.srow2[:, :w], self.srow[:, :w],
                                                 self.srow[:, :w], ALU.mult),
             reads=("srow",), writes=("srow2",))
        p.op("vector", lambda e: e.scalar_tensor_tensor(self.srow2[:, :w], self.p_sq[:, :w],
                                                        1.0 / D, self.srow2[:, :w],
                                                        ALU.mult, ALU.subtract),
             reads=("p_sq", "srow2"), writes=("srow2",))
        p.op("scalar", lambda e: e.activation(self.srow2[:, :w], self.srow2[:, :w],
                                              AF.Sqrt, bias=self.eps1[0:1, 0:1]),
             reads=("srow2", "ones"), writes=("srow2",))
        p.op("vector", lambda e: e.reciprocal(self.Rrow[:, :w], self.srow2[:, :w]),
             reads=("srow2",), writes=("Rrow",))
        p.op("vector", lambda e: e.tensor_tensor(self.MRrow[:, :w], self.srow[:, :w],
                                                 self.Rrow[:, :w], ALU.mult),
             reads=("srow", "Rrow"), writes=("MRrow",))
        for th in range(nh):
            w2 = min(512, tcols - th * 512)
            sl = slice(th * 512, th * 512 + w2)
            p.op("tensor", lambda e, sl=sl, w2=w2:
                 e.matmul(self.p_R[:, :w2], self.ones_row, self.Rrow[:, sl], start=True, stop=True),
                 reads=("Rrow", "ones"), writes=("pproj0",))
            p.op("vector", lambda e, sl=sl, w2=w2: e.tensor_copy(self.Rrep[:, sl], self.p_R[:, :w2]),
                 reads=("pproj0",), writes=("Rrep",))
            p.op("tensor", lambda e, sl=sl, w2=w2:
                 e.matmul(self.p_MR[:, :w2], self.ones_row, self.MRrow[:, sl], start=True, stop=True),
                 reads=("MRrow", "ones"), writes=("pproj1",))
            p.op("vector", lambda e, sl=sl, w2=w2: e.tensor_copy(self.MRrep[:, sl], self.p_MR[:, :w2]),
                 reads=("pproj1",), writes=("MRrep",))

    def ln_pre(self, src_ap_of, src_tag_of, dst_ap_of, dst_tag_of, tcols, alias_of=None):
        p = self.p
        self.ln_stats(src_ap_of, src_tag_of, True, tcols)
        for d in range(DC):
            w = tcols
            sl = d % 2
            p.op("vector", lambda e, d=d, w=w, sl=sl: e.tensor_tensor(
                self.tln[sl][:, :w], src_ap_of(d), self.Rrep[:, :w], ALU.mult),
                 reads=(src_tag_of(d), "Rrep"), writes=("tln0", "tln1"))
            wt = (dst_tag_of(d),) if alias_of is None else (dst_tag_of(d), alias_of(d))
            p.op("vector", lambda e, d=d, w=w, sl=sl: e.tensor_tensor(
                dst_ap_of(d), self.tln[sl][:, :w], self.MRrep[:, :w], ALU.subtract),
                 reads=(f"tln{sl}", "MRrep"), writes=wt)

    def ln_post_add(self, y_ap_of, y_tag_of, y_is_f32, x_ap_of, x_tag_of, tcols):
        p = self.p
        self.ln_stats(y_ap_of, y_tag_of, y_is_f32, tcols)
        for d in range(DC):
            w = tcols
            sl = d % 2
            p.op("vector", lambda e, d=d, w=w, sl=sl: e.tensor_tensor(
                self.tln[sl][:, :w], y_ap_of(d), self.Rrep[:, :w], ALU.mult),
                 reads=(y_tag_of(d), "Rrep"), writes=("tln0", "tln1"))
            p.op("vector", lambda e, d=d, w=w, sl=sl: e.tensor_tensor(
                self.tln[sl][:, :w], self.tln[sl][:, :w], self.MRrep[:, :w], ALU.subtract),
                 reads=(f"tln{sl}", "MRrep"), writes=("tln0", "tln1"))
            p.op("vector", lambda e, d=d, w=w, sl=sl: e.tensor_tensor(
                x_ap_of(d), x_ap_of(d), self.tln[sl][:, :w], ALU.add),
                 reads=(f"tln{sl}", x_tag_of(d)), writes=(x_tag_of(d),))

    def projection(self, w_dram, act_sb, act_tag_of, act_cols, tcols,
                   out_cb, cast_dma, bias_sb=None, bias_tag=None, hook=None):
        p = self.p
        nth = (tcols + 511) // 512
        for og in range(8):
            wb = self.wbuf[og % 2]
            wtag = f"wb{og % 2}"
            eng = "gpsimd" if cast_dma else "scalar"
            wsrc = w_dram[og * 128:(og + 1) * 128, :]
            p.op(eng, lambda e, s=wsrc, wb=wb: e.dma_start(out=wb[:, :], in_=s),
                 reads=(), writes=(wtag,), dma=True, dsem=f"Dw{og % 2}")
            for oc2 in range(2):
                oc = og * 2 + oc2
                for th in range(nth):
                    w = min(512, tcols - th * 512)
                    pi = (oc2 * nth + th) % 2
                    ps = self.pproj[pi]
                    pstag = f"pproj{pi}"
                    fns = []
                    for d in range(DC):
                        l = wb[:, d * 256 + oc2 * 128: d * 256 + oc2 * 128 + 128]
                        r = act_sb[:, d * act_cols + th * 512: d * act_cols + th * 512 + w]
                        fns.append(lambda e, l=l, r=r, f=(d == 0), la=(d == DC - 1), ps=ps, w=w:
                                   e.matmul(ps[:, :w], l, r, start=f, stop=la))
                    reads = (wtag,) + tuple(act_tag_of(d) for d in range(DC))
                    p.multi("tensor", fns, reads=reads, writes=(pstag,))
                    out_cb(oc, th, ps[:, :w], pstag, bias_sb, bias_tag)
            if hook is not None:
                hook()


def evac_simple(p, dst_ap, dst_tag):
    def cb(oc, th, ps_ap, pstag, bias_sb, bias_tag):
        da = dst_ap(oc, th)
        wt = dst_tag(oc, th)
        wt = wt if isinstance(wt, tuple) else (wt,)
        if bias_sb is not None:
            bc = bias_sb[:, oc:oc + 1]
            p.op("scalar", lambda e, ps_ap=ps_ap, da=da, bc=bc:
                 e.activation(da, ps_ap, AF.Identity, bias=bc),
                 reads=(pstag, bias_tag), writes=wt)
        else:
            p.op("scalar", lambda e, ps_ap=ps_ap, da=da: e.copy(da, ps_ap),
                 reads=(pstag,), writes=wt)
    return cb



def build_launch_a():
    nc = bass.Bass("TRN2", target_bir_lowering=False)
    xT = nc.dram_tensor("xT", [D, T], F32, kind="ExternalInput").ap()
    wq0T = nc.dram_tensor("wq0T", [8 * 128, DC * 256], F32, kind="ExternalInput").ap()
    wk0T = nc.dram_tensor("wk0T", [8 * 128, DC * 256], F32, kind="ExternalInput").ap()
    wv0T = nc.dram_tensor("wv0T", [8 * 128, DC * 256], F32, kind="ExternalInput").ap()
    ccat_in = nc.dram_tensor("ccat", [128, T], F32, kind="ExternalInput").ap()
    scat_in = nc.dram_tensor("scat", [128, T], F32, kind="ExternalInput").ap()
    w0T = nc.dram_tensor("w0T", [D, D], F32, kind="ExternalInput").ap()
    w0T_rows = nc.dram_tensor("w0T_rows", [256, D], F32, kind="ExternalInput").ap()
    wgu = nc.dram_tensor("wgu", [2048, 2048], BF16, kind="ExternalInput").ap()
    wpr = nc.dram_tensor("wpr", [256, 8192], BF16, kind="ExternalInput").ap()
    ident_in = nc.dram_tensor("identf", [128, 128], F32, kind="ExternalInput").ap()

    q0 = nc.dram_tensor("q0", [D, T], F32, kind="ExternalOutput").ap()
    k0 = nc.dram_tensor("k0", [D, T], F32, kind="ExternalOutput").ap()
    v0 = nc.dram_tensor("v0", [T, D], F32, kind="ExternalOutput").ap()
    wq_hat = nc.dram_tensor("wq_hat", [256, D], BF16, kind="ExternalOutput").ap()
    wgu_hat = nc.dram_tensor("wgu_hat", [2048, 2048], BF16, kind="ExternalOutput").ap()
    wpr_hat = nc.dram_tensor("wpr_hat", [256, 8192], BF16, kind="ExternalOutput").ap()

    p = Prog(nc)
    cm = Common(p)
    lnx = p.sb("lnx", [128, DC * T], BF16)
    q_sb = p.sb("q_sb", [128, DC * T], BF16)
    k_sb = p.sb("k_sb", [128, DC * T], BF16)
    vst = p.sb("vst", [128, 512], BF16)
    vtt = p.sb("vtt", [128, 8 * 128], BF16)
    xch = [p.sb("xch0", [128, T], F32), p.sb("xch1", [128, T], F32)]
    ccat = p.sb("ccat_sb", [128, T], BF16)
    scat = p.sb("scat_sb", [128, T], BF16)
    ident_bf = p.sb("ident_bf", [128, 128], BF16)
    rt1 = p.sb("rt1", [128, T], BF16)
    rt2b = [p.sb("rt2a", [128, T], BF16), p.sb("rt2b", [128, T], BF16)]
    p_tr = p.ps("p_tr", [128, 128], BF16)

    qw = p.sb("qw", [128, 1024], BF16)
    qs_f = p.sb("qs_f", [128, 16], F32)
    qs_r = p.sb("qs_r", [128, 16], F32)
    qs_b = p.sb("qs_b", [128, 16], BF16)
    qt = p.sb("qt", [128, 1024], F32)
    qi = p.sb("qi", [128, 1024], I32)
    qq = p.sb("qq", [128, 1024], BF16)
    qh = p.sb("qh", [128, 1024], BF16)
    wsc_col = p.sb("wsc_col", [128, 1], F32)
    wsc_ps = cm.pproj[0][0:1, 0:1]
    wsc_rep = p.sb("wsc_rep", [128, 1], F32)
    p_rep1 = cm.pproj[1][:, 0:1]

    p.op("gpsimd", lambda e: e.dma_start(out=ident_bf[:, :], in_=ident_in),
         writes=("ident",), dma=True, dsem="Din")
    p.op("gpsimd", lambda e: e.dma_start(out=ccat[:, :], in_=ccat_in),
         writes=("ccat",), dma=True, dsem="Din")
    p.op("gpsimd", lambda e: e.dma_start(out=scat[:, :], in_=scat_in),
         writes=("scat",), dma=True, dsem="Din")

    quant_pieces = []

    def ternary(nrows, w_in, w_out, out_tag):
        npieces = w_in.shape[1] // 1024
        for r in range(nrows // 128):
            for pc in range(npieces):
                quant_pieces.append((r, pc, w_in, w_out, out_tag))

    def emit_quant_piece(args):
        r, pc, w_in, w_out, out_tag = args
        csl = slice(pc * 1024, pc * 1024 + 1024)
        if True:
            if True:
                p.op("scalar", lambda e, r=r, csl=csl, w_in=w_in:
                     e.dma_start(out=qw[:, :], in_=w_in[r * 128:(r + 1) * 128, csl]),
                     reads=(), writes=("qw",), dma=True, dsem="Dqw")
                g3 = bass.AP(qw, 0, [[1024, 128], [64, 16], [1, 64]])
                p.op("vector", lambda e, g3=g3: e.reduce_sum(qs_f[:, :], g3, axis=AX.X,
                                                             apply_absolute_value=True),
                     reads=("qw",), writes=("qs_f",))
                p.op("vector", lambda e: e.tensor_scalar(qs_f[:, :], qs_f[:, :],
                                                         1.0 / 64, 1e-8, ALU.mult, ALU.max),
                     reads=("qs_f",), writes=("qs_f",))
                p.op("vector", lambda e: e.tensor_copy(qs_b[:, :], qs_f[:, :]),
                     reads=("qs_f",), writes=("qs_b",))
                p.op("vector", lambda e: e.tensor_copy(qs_f[:, :], qs_b[:, :]),
                     reads=("qs_b",), writes=("qs_f",))
                p.op("vector", lambda e: e.reciprocal(qs_r[:, :], qs_f[:, :]),
                     reads=("qs_f",), writes=("qs_r",))
                rb = bass.AP(qs_r, 0, [[16, 128], [1, 16], [0, 64]])
                t3 = bass.AP(qt, 0, [[1024, 128], [64, 16], [1, 64]])
                p.op("vector", lambda e, g3=g3, rb=rb, t3=t3: e.tensor_tensor(t3, g3, rb, ALU.mult),
                     reads=("qw", "qs_r"), writes=("qt",))
                p.op("vector", lambda e: e.tensor_scalar(qt[:, :], qt[:, :], MAGIC, MAGIC,
                                                         ALU.add, ALU.subtract),
                     reads=("qt",), writes=("qt",))
                p.op("vector", lambda e: e.tensor_scalar(qi[:, :], qt[:, :], 0.0, None, ALU.add),
                     reads=("qt",), writes=("qi",))
                p.op("vector", lambda e: e.tensor_scalar(qq[:, :], qi[:, :], -1.0, 1.0,
                                                         ALU.max, ALU.min),
                     reads=("qi",), writes=("qq",))
                sb3 = bass.AP(qs_b, 0, [[16, 128], [1, 16], [0, 64]])
                q3 = bass.AP(qq, 0, [[1024, 128], [64, 16], [1, 64]])
                h3 = bass.AP(qh, 0, [[1024, 128], [64, 16], [1, 64]])
                p.op("vector", lambda e, q3=q3, sb3=sb3, h3=h3: e.tensor_tensor(h3, q3, sb3, ALU.mult),
                     reads=("qq", "qs_b"), writes=("qh",))
                p.op("scalar", lambda e, r=r, csl=csl, w_out=w_out:
                     e.dma_start(out=w_out[r * 128:(r + 1) * 128, csl], in_=qh[:, :]),
                     reads=("qh",), writes=(out_tag,), dma=True, dsem="Dout")

    qwork = []

    def scan_unit(r):
        p.op("gpsimd", lambda e, r=r: e.dma_start(
            out=qt[:, :], in_=w0T[(r // 2) * 128:(r // 2 + 1) * 128,
                                  (r % 2) * 1024:(r % 2) * 1024 + 1024]),
             writes=("qt",), dma=True, dsem="Dqt")
        p.op("vector", lambda e: e.reduce_sum(qs_f[:, 0:1], qt[:, :], axis=AX.X,
                                              apply_absolute_value=True),
             reads=("qt",), writes=("qs_f",))
        if r == 0:
            p.op("vector", lambda e: e.tensor_copy(wsc_col[:, :], qs_f[:, 0:1]),
                 reads=("qs_f",), writes=("wsc_col",))
        else:
            p.op("vector", lambda e: e.tensor_tensor(wsc_col[:, :], wsc_col[:, :],
                                                     qs_f[:, 0:1], ALU.add),
                 reads=("qs_f", "wsc_col"), writes=("wsc_col",))

    def wsc_finish():
        p.op("tensor", lambda e: e.matmul(wsc_ps[:, :], cm.ones_f32, wsc_col[:, :],
                                          start=True, stop=True),
             reads=("wsc_col", "ones"), writes=("pproj0",))
        p.op("vector", lambda e: e.tensor_scalar(wsc_col[0:1, 0:1], wsc_ps[:, :],
                                                 1.0 / (D * D), 1e-6, ALU.mult, ALU.max),
             reads=("pproj0",), writes=("wsc_col",))
        p.op("vector", lambda e: e.reciprocal(wsc_col[0:1, 0:1], wsc_col[0:1, 0:1]),
             reads=("wsc_col",), writes=("wsc_col",))
        p.op("tensor", lambda e: e.matmul(p_rep1[:, :], cm.ones_row, wsc_col[0:1, 0:1],
                                          start=True, stop=True),
             reads=("wsc_col", "ones"), writes=("pproj1",))
        p.op("vector", lambda e: e.tensor_copy(wsc_rep[:, :], p_rep1[:, :]),
             reads=("pproj1",), writes=("wsc_rep",))

    def shard_unit(r):
        rr, cc_ = r // 2, (r % 2) * 1024
        p.op("gpsimd", lambda e, rr=rr, cc_=cc_: e.dma_start(
            out=qt[:, :], in_=w0T_rows[rr * 128:(rr + 1) * 128, cc_:cc_ + 1024]),
             writes=("qt",), dma=True, dsem="Dqt")
        p.op("vector", lambda e: e.tensor_scalar(qt[:, :], qt[:, :], wsc_rep[:, 0:1], None, ALU.mult),
             reads=("qt", "wsc_rep"), writes=("qt",))
        p.op("vector", lambda e: e.tensor_scalar(qt[:, :], qt[:, :], MAGIC, MAGIC,
                                                 ALU.add, ALU.subtract),
             reads=("qt",), writes=("qt",))
        p.op("vector", lambda e: e.tensor_scalar(qi[:, :], qt[:, :], 0.0, None, ALU.add),
             reads=("qt",), writes=("qi",))
        p.op("vector", lambda e: e.tensor_scalar(qq[:, :], qi[:, :], -1.0, 1.0, ALU.max, ALU.min),
             reads=("qi",), writes=("qq",))
        p.op("scalar", lambda e, rr=rr, cc_=cc_: e.dma_start(
            out=wq_hat[rr * 128:(rr + 1) * 128, cc_:cc_ + 1024], in_=qq[:, :]),
             reads=("qq",), writes=("out_wq",), dma=True, dsem="Dout")

    for r in range(2 * DC):
        qwork.append(lambda r=r: scan_unit(r))
    qwork.append(wsc_finish)
    for r in range(4):
        qwork.append(lambda r=r: shard_unit(r))
    ternary(2048, wgu, wgu_hat, "out_gu")
    ternary(256, wpr, wpr_hat, "out_pr")
    for a in quant_pieces:
        qwork.append(lambda a=a: emit_quant_piece(a))
    qp_iter = iter(qwork)

    def emit_some_quant(n):
        for _ in range(n):
            f = next(qp_iter, None)
            if f is not None:
                f()

    def x_stream(d, slot):
        p.op("sync", lambda e, d=d, slot=slot: e.dma_start(
            out=xch[slot][:, :], in_=xT[d * 128:(d + 1) * 128, :]),
            writes=(f"xch{slot}",), dma=True, dsem=f"Dx{slot}")

    for d in range(DC):
        slot = d % 2
        x_stream(d, slot)
        p.op("scalar", lambda e, slot=slot: e.activation(cm.xsq[slot][:, :], xch[slot][:, :], AF.Square),
             reads=(f"xch{slot}",), writes=(f"xsq{slot}",))
        for th in range(2):
            sl = slice(th * 512, th * 512 + 512)
            first, last = (d == 0), (d == DC - 1)
            p.op("tensor", lambda e, sl=sl, slot=slot, f=first, l=last:
                 e.matmul(cm.p_mean[:, sl], cm.ones_f32, xch[slot][:, sl], start=f, stop=l),
                 reads=(f"xch{slot}", "ones"), writes=("p_mean",) if (first or last) else ())
            p.op("tensor", lambda e, sl=sl, slot=slot, f=first, l=last:
                 e.matmul(cm.p_sq[:, sl], cm.ones_bf, cm.xsq[slot][:, sl], start=f, stop=l),
                 reads=(f"xsq{slot}", "ones"), writes=("p_sq",) if (first or last) else ())
    cm.ln_finish(T)
    for d in range(DC):
        slot = d % 2
        x_stream(d, slot)
        p.op("vector", lambda e, slot=slot: e.tensor_tensor(cm.tln[slot][:, :], xch[slot][:, :],
                                                            cm.Rrep[:, :], ALU.mult),
             reads=(f"xch{slot}", "Rrep"), writes=("tln0", "tln1"))
        p.op("vector", lambda e, d=d, slot=slot: e.tensor_tensor(
            lnx[:, d * T:(d + 1) * T], cm.tln[slot][:, :], cm.MRrep[:, :], ALU.subtract),
             reads=(f"tln{slot}", "MRrep"), writes=(f"lnx_{d}",))

    def dst_chunk(dst):
        return lambda oc, th: dst[:, oc * T + th * 512: oc * T + th * 512 + 512]

    cm.projection(wq0T, lnx, lambda d: f"lnx_{d}", T, T,
                  evac_simple(p, dst_chunk(q_sb), lambda oc, th: f"q_{oc}"), True,
                  hook=lambda: emit_some_quant(4))
    cm.projection(wk0T, lnx, lambda d: f"lnx_{d}", T, T,
                  evac_simple(p, dst_chunk(k_sb), lambda oc, th: f"k_{oc}"), True,
                  hook=lambda: emit_some_quant(4))

    def v_cb(oc, th, ps_ap, pstag, bias_sb, bias_tag):
        p.op("scalar", lambda e, ps_ap=ps_ap: e.copy(vst[:, :], ps_ap),
             reads=(pstag,), writes=("vst",))
        for tt4 in range(4):
            tt = th * 4 + tt4
            p.op("tensor", lambda e, tt4=tt4: e.transpose(
                p_tr[:, :], vst[:, tt4 * 128:(tt4 + 1) * 128], ident_bf[:, :]),
                 reads=("vst", "ident"), writes=("p_tr",))
            dst = vtt[:, tt * 128:(tt + 1) * 128]
            p.op("scalar", lambda e, dst=dst: e.copy(dst, p_tr[:, :]),
                 reads=("p_tr",), writes=("vtt",))
        if th == 1:
            dst = bass.AP(v0.tensor, oc * 128, [[D, 128], [128 * D, 8], [1, 128]])
            src_ = bass.AP(vtt, 0, [[8 * 128, 128], [128, 8], [1, 128]])
            p.op("gpsimd", lambda e, dst=dst, src_=src_: e.dma_start(out=dst, in_=src_),
                 reads=("vtt",), writes=("out_v",), dma=True, dsem="Dout")
    cm.projection(wv0T, lnx, lambda d: f"lnx_{d}", T, T, v_cb, True,
                  hook=lambda: emit_some_quant(4))
    emit_some_quant(100)

    def rope(sb, pref, base):
        for hc in range(DC):
            sl = slice(hc * T, (hc + 1) * T)
            rtag = f"{pref}_{hc}"
            ri = (base + hc) % 2
            rt2 = rt2b[ri]
            rt = f"rt2{ri}"
            p.op("sync", lambda e, sl=sl, sb=sb, rt2=rt2: e.dma_start(out=rt2[0:64, :], in_=sb[64:128, sl]),
                 reads=(rtag,), writes=(rt,), dma=True, dsem=f"Drt{ri}")
            p.op("sync", lambda e, sl=sl, sb=sb, rt2=rt2: e.dma_start(out=rt2[64:128, :], in_=sb[0:64, sl]),
                 reads=(rtag,), writes=(rt,), dma=True, dsem=f"Drt{ri}")
            p.op("vector", lambda e, sl=sl, sb=sb: e.tensor_tensor(rt1[:, :], sb[:, sl], ccat[:, :], ALU.mult),
                 reads=(rtag, "ccat"), writes=("rt1",))
            p.op("vector", lambda e, rt2=rt2: e.tensor_tensor(rt2[:, :], rt2[:, :], scat[:, :], ALU.mult),
                 reads=(rt, "scat"), writes=(rt,))
            p.op("vector", lambda e, sl=sl, sb=sb, rt2=rt2: e.tensor_tensor(sb[:, sl], rt1[:, :], rt2[:, :], ALU.add),
                 reads=("rt1", rt), writes=(f"{pref}r_{hc}", rtag))
    rope(q_sb, "q", 0)
    rope(k_sb, "k", 0)

    p.op("gpsimd", lambda e: e.dma_start(out=q0.rearrange("(d p) t -> p d t", p=128), in_=q_sb[:, :]),
         reads=tuple(f"qr_{h}" for h in range(DC)), writes=("out_q",), dma=True, dsem="Dout")
    p.op("gpsimd", lambda e: e.dma_start(out=k0.rearrange("(d p) t -> p d t", p=128), in_=k_sb[:, :]),
         reads=tuple(f"kr_{h}" for h in range(DC)), writes=("out_k",), dma=True, dsem="Dout")
    p.finish(["out_q", "out_k", "out_v", "out_gu", "out_pr", "out_wq"])
    p.emit()
    return nc



def build_attention(layer):
    nc = bass.Bass("TRN2", target_bir_lowering=False)
    qh_in = nc.dram_tensor("qh", [2 * 128, S], F32, kind="ExternalInput").ap()
    kh_in = nc.dram_tensor("kh", [2 * 128, S], F32, kind="ExternalInput").ap()
    vh_in = nc.dram_tensor("vh", [2, S, 128], F32, kind="ExternalInput").ap()
    ident_in = nc.dram_tensor("identf", [128, 128], F32, kind="ExternalInput").ap()
    if layer == 0:
        masks_in = nc.dram_tensor("masks", [4 * 128, 512], F32, kind="ExternalInput").ap()
    else:
        dts_in = nc.dram_tensor("dts", [2 * 5 * 128, 512], F32, kind="ExternalInput").ap()
        btab_in = nc.dram_tensor("btab", [128, 2 * 68], F32, kind="ExternalInput").ap()
    attn_out = nc.dram_tensor("attn", [2 * 128, S], F32, kind="ExternalOutput").ap()

    p = Prog(nc)
    q_sb = p.sb("q_sb", [128, 2 * S], BF16)
    k_sb = p.sb("k_sb", [128, 2 * S], BF16)
    v_sb = p.sb("v_sb", [128, 2 * 64 * 132], BF16)
    ao_sb = p.sb("ao_sb", [128, 2 * S], BF16)
    ident_bf = p.sb("ident_bf", [128, 128], BF16)
    s2 = [p.sb("s2a", [128, 512], F32), p.sb("s2b", [128, 512], F32),
          p.sb("s2c", [128, 512], F32)]
    p4 = [p.sb("p4a", [128, 512], BF16), p.sb("p4b", [128, 512], BF16),
          p.sb("p4c", [128, 512], BF16)]
    att_st = p.sb("att_st", [128, 128], BF16)
    dcol = p.sb("dcol", [128, 1], F32)
    rden = p.sb("rden", [128, 1], F32)
    if layer == 0:
        masks = p.sb("masks_sb", [128, 4 * 512], BF16)
    else:
        dts = p.sb("dts_sb", [128, 2 * 5 * 512], F32)
        btab = p.sb("btab_sb", [128, 2 * 68], F32)
    p_s = [p.ps("p_s0", [128, 512]), p.ps("p_s1", [128, 512]), p.ps("p_s2", [128, 512])]
    p_av = [p.ps(f"p_av{i}", [128, 132]) for i in range(4)]
    p_tr = p.ps("p_tr", [128, 128], BF16)

    p.op("gpsimd", lambda e: e.memset(v_sb[:, :], 1.0), writes=("v",))
    p.op("gpsimd", lambda e: e.dma_start(out=ident_bf[:, :], in_=ident_in),
         writes=("ident",), dma=True, dsem="Din")
    if layer == 0:
        p.op("gpsimd", lambda e: e.dma_start(
            out=masks[:, :], in_=masks_in.rearrange("(m p) f -> p m f", p=128)),
            writes=("masks",), dma=True, dsem="Din")
    else:
        p.op("gpsimd", lambda e: e.dma_start(
            out=dts[:, :], in_=dts_in.rearrange("(m p) f -> p m f", p=128)),
            writes=("dts",), dma=True, dsem="Din")
        p.op("sync", lambda e: e.dma_start(out=btab[:, :], in_=btab_in),
             writes=("btab",), dma=True, dsem="Din")
    p.op("gpsimd", lambda e: e.dma_start(
        out=q_sb[:, :], in_=qh_in.rearrange("(s p) t -> p s t", p=128)),
        writes=("q",), dma=True, dsem="Din")
    p.op("gpsimd", lambda e: e.dma_start(
        out=k_sb[:, :], in_=kh_in.rearrange("(s p) t -> p s t", p=128)),
        writes=("k",), dma=True, dsem="Din")
    for s in range(2):
        for kg in range(8):
            src = vh_in[s, kg * 1024:(kg + 1) * 1024, :].rearrange("(kt p) c -> p kt c", p=128)
            dst = bass.AP(v_sb, s * 64 * 132 + kg * 8 * 132,
                          [[2 * 64 * 132, 128], [132, 8], [1, 128]])
            p.op("gpsimd", lambda e, src=src, dst=dst: e.dma_start(out=dst, in_=src),
                 writes=("v",), dma=True, dsem="Din")

    pend_epi = [None]
    for slot in range(2):
        for g in range(NCH):
            nkt = 4 * (g + 1)
            qcol = slot * S + g * 512

            def emit_qk(kt, slot=slot, qcol=qcol):
                pi = kt % 3
                l = k_sb[:, slot * S + kt * 128: slot * S + kt * 128 + 128]
                r = q_sb[:, qcol: qcol + 512]
                p.op("tensor", lambda e, l=l, r=r, pi=pi:
                     e.matmul(p_s[pi][:, :], l, r, start=True, stop=True),
                     reads=("k", "q"), writes=(f"p_s{pi}",))

            def emit_proc(kt, slot=slot, g=g, nkt=nkt):
                pi = kt % 3
                pstag = f"p_s{pi}"
                if layer == 0:
                    p.op("scalar", lambda e, pi=pi: e.activation(s2[pi][:, :], p_s[pi][:, :], AF.Square),
                         reads=(pstag,), writes=(f"s2{pi}",))
                    p.op("vector", lambda e, pi=pi: e.tensor_tensor(
                        p4[pi][:, :], s2[pi][:, :], s2[pi][:, :], ALU.mult),
                         reads=(f"s2{pi}",), writes=(f"p4{pi}",))
                    if kt >= 4 * g:
                        r_ = kt - 4 * g
                        p.op("vector", lambda e, pi=pi, r_=r_: e.tensor_tensor(
                            p4[pi][:, :], p4[pi][:, :], masks[:, r_ * 512:(r_ + 1) * 512], ALU.mult),
                             reads=("masks", f"p4{pi}"), writes=(f"p4{pi}",))
                else:
                    dv = 1 + (kt - 4 * g) if kt >= 4 * g else 0
                    dsl = slice((slot * 5 + dv) * 512, (slot * 5 + dv) * 512 + 512)
                    p.op("vector", lambda e, pi=pi, dsl=dsl: e.scalar_tensor_tensor(
                        s2[pi][:, :], p_s[pi][:, :], SCALE_L1, dts[:, dsl], ALU.mult, ALU.add),
                         reads=(pstag, "dts"), writes=(f"s2{pi}",))
                    dk = 4 * g - kt
                    bcol = btab[:, slot * 68 + dk + 3: slot * 68 + dk + 4]
                    p.op("scalar", lambda e, pi=pi, bcol=bcol: e.activation(
                        p4[pi][:, :], s2[pi][:, :], AF.Exp, bias=bcol),
                         reads=(f"s2{pi}", "btab"), writes=(f"p4{pi}",))
                first = (kt == 0)
                last = (kt == nkt - 1)
                vrhs = bass.AP(v_sb, slot * 64 * 132 + kt * 132,
                               [[2 * 64 * 132, 128], [1, 129]])
                lhs0 = p4[pi][:, 0:128]
                if first:
                    p.op("tensor", lambda e, lhs0=lhs0, vrhs=vrhs, last=last:
                         e.matmul(p_av[0][:, :129], lhs0, vrhs, start=True, stop=last),
                         reads=(f"p4{pi}", "v"),
                         writes=("p_av0", "p_av1", "p_av2", "p_av3"), noinc=True)
                else:
                    p.op("tensor", lambda e, lhs0=lhs0, vrhs=vrhs, last=last:
                         e.matmul(p_av[0][:, :129], lhs0, vrhs, start=False, stop=last),
                         reads=(f"p4{pi}",), writes=(), noinc=True)
                for qs in (1, 2):
                    lh = p4[pi][:, qs * 128:(qs + 1) * 128]
                    p.raw("tensor", lambda e, lh=lh, vrhs=vrhs, first=first, last=last, qs=qs:
                          e.matmul(p_av[qs][:, :129], lh, vrhs, start=first, stop=last))
                lh3 = p4[pi][:, 384:512]
                if last:
                    p.op("tensor", lambda e, lh3=lh3, vrhs=vrhs, first=first:
                         e.matmul(p_av[3][:, :129], lh3, vrhs, start=first, stop=True),
                         reads=(f"p4{pi}", "v"),
                         writes=("p_av0", "p_av1", "p_av2", "p_av3"))
                else:
                    p.op("tensor", lambda e, lh3=lh3, vrhs=vrhs, first=first:
                         e.matmul(p_av[3][:, :129], lh3, vrhs, start=first, stop=False),
                         reads=(f"p4{pi}", "v"), writes=())

            emit_qk(0)
            emit_qk(1)
            if pend_epi[0] is not None:
                pend_epi[0]()
                pend_epi[0] = None
            for kt in range(2, nkt):
                emit_qk(kt)
                emit_proc(kt - 2)
            emit_proc(nkt - 2)
            emit_proc(nkt - 1)

            def emit_epi(slot=slot, g=g):
                for qs in range(4):
                    ptag = "p_av3"
                    if layer == 0:
                        p.op("vector", lambda e, qs=qs: e.tensor_scalar(
                            dcol[:, :], p_av[qs][:, 128:129], 1e-6, None, ALU.max),
                             reads=(ptag,), writes=("dcol",))
                    else:
                        p.op("vector", lambda e, qs=qs: e.tensor_copy(dcol[:, :], p_av[qs][:, 128:129]),
                             reads=(ptag,), writes=("dcol",))
                    p.op("vector", lambda e: e.reciprocal(rden[:, :], dcol[:, :]),
                         reads=("dcol",), writes=("rden",))
                    p.op("vector", lambda e, qs=qs: e.tensor_scalar(
                        att_st[:, :], p_av[qs][:, 0:128], rden[:, 0:1], None, ALU.mult),
                         reads=(ptag, "rden"), writes=("att_st",))
                    p.op("tensor", lambda e: e.transpose(p_tr[:, :], att_st[:, :], ident_bf[:, :]),
                         reads=("att_st", "ident"), writes=("p_tr",))
                    dst = ao_sb[:, slot * S + g * 512 + qs * 128: slot * S + g * 512 + qs * 128 + 128]
                    p.op("scalar", lambda e, dst=dst: e.copy(dst, p_tr[:, :]),
                         reads=("p_tr",), writes=("ao",))
            pend_epi[0] = emit_epi

    if pend_epi[0] is not None:
        pend_epi[0]()
        pend_epi[0] = None
    p.op("gpsimd", lambda e: e.dma_start(
        out=attn_out.rearrange("(s p) t -> p s t", p=128), in_=ao_sb[:, :]),
        reads=("ao",), writes=("out_a",), dma=True, dsem="Dout")
    p.finish(["out_a"])
    p.emit()
    return nc



def build_launch_c():
    nc = bass.Bass("TRN2", target_bir_lowering=False)
    xT = nc.dram_tensor("xT", [D, T], F32, kind="ExternalInput").ap()
    attn0 = nc.dram_tensor("attn0", [D, T], F32, kind="ExternalInput").ap()
    wo0T = nc.dram_tensor("wo0T", [8 * 128, DC * 256], F32, kind="ExternalInput").ap()
    wq_hatT = nc.dram_tensor("wq_hatT", [8 * 128, DC * 256], BF16, kind="ExternalInput").ap()
    wq1 = nc.dram_tensor("wq1", [8 * 128, DC * 256], F32, kind="ExternalInput").ap()
    wk1 = nc.dram_tensor("wk1", [8 * 128, DC * 256], F32, kind="ExternalInput").ap()
    wv1 = nc.dram_tensor("wv1", [8 * 128, DC * 256], F32, kind="ExternalInput").ap()
    biases = nc.dram_tensor("biases", [128, 3 * DC], F32, kind="ExternalInput").ap()
    ident_in = nc.dram_tensor("identf", [128, 128], F32, kind="ExternalInput").ap()

    q1 = nc.dram_tensor("q1", [D, T], F32, kind="ExternalOutput").ap()
    k1 = nc.dram_tensor("k1", [D, T], F32, kind="ExternalOutput").ap()
    v1 = nc.dram_tensor("v1", [T, D], F32, kind="ExternalOutput").ap()
    x2 = nc.dram_tensor("x2", [D, T], F32, kind="ExternalOutput").ap()

    p = Prog(nc)
    cm = Common(p)
    x_sb = p.sb("x_sb", [128, DC * T], F32)
    lnb = p.sb("lnb", [128, DC * T], BF16)
    y_sb = p.sb("y_sb", [128, DC * T], BF16)
    vst = p.sb("vst", [128, 512], BF16)
    vtt = p.sb("vtt", [128, 8 * 128], BF16)
    ident_bf = p.sb("ident_bf", [128, 128], BF16)
    bias_sb = p.sb("bias_sb", [128, 3 * DC], F32)
    p_tr = p.ps("p_tr", [128, 128], BF16)
    amax = p.sb("amax", [128, 8], F32)
    mtmp = p.sb("mtmp", [128, 1], F32)
    rqrow = cm.Rrow
    rqrep = p.sb("rqrep", [128, T], F32)
    p_rq = cm.pproj[0]
    qi = p.sb("qi", [128, T], I32)
    ident_f32 = p.sb("ident_f32", [128, 128], F32)
    rq8 = p.sb("rq8", [8, 128], F32)

    p.op("gpsimd", lambda e: e.dma_start(out=ident_bf[:, :], in_=ident_in),
         writes=("ident",), dma=True, dsem="Din")
    p.op("sync", lambda e: e.dma_start(out=ident_f32[:, :], in_=ident_in),
         writes=("identf32",), dma=True, dsem="Din")
    p.op("sync", lambda e: e.dma_start(out=bias_sb[:, :], in_=biases),
         writes=("biasb",), dma=True, dsem="Din")
    p.op("sync", lambda e: e.dma_start(out=x_sb[:, :], in_=xT.rearrange("(d p) t -> p d t", p=128)),
         writes=tuple(f"x_{d}" for d in range(DC)), dma=True, dsem="Din")
    p.op("gpsimd", lambda e: e.dma_start(
        out=lnb[:, :], in_=attn0.rearrange("(d p) t -> p d t", p=128)),
        writes=tuple(f"a0_{d}" for d in range(DC)), dma=True, dsem="Din")

    def ych(oc, th):
        return y_sb[:, oc * T + th * 512: oc * T + th * 512 + 512]

    cm.projection(wo0T, lnb, lambda d: f"a0_{d}", T, T,
                  evac_simple(p, ych, lambda oc, th: f"y_{oc}"), True)
    cm.ln_post_add(lambda d: y_sb[:, d * T:(d + 1) * T], lambda d: f"y_{d}", False,
                   lambda d: x_sb[:, d * T:(d + 1) * T], lambda d: f"x_{d}", T)
    cm.ln_pre(lambda d: x_sb[:, d * T:(d + 1) * T], lambda d: f"x_{d}",
              lambda d: lnb[:, d * T:(d + 1) * T], lambda d: f"ln_{d}", T,
              alias_of=lambda d: f"a0_{d}")

    for tt in range(8):
        for d in range(DC):
            src = lnb[:, d * T + tt * 128: d * T + tt * 128 + 128]
            p.op("tensor", lambda e, src=src: e.transpose(p_tr[:, :], src, ident_bf[:, :]),
                 reads=(f"ln_{d}", "ident"), writes=("p_tr",))
            if d == 0:
                p.op("vector", lambda e, tt=tt: e.reduce_max(
                    amax[:, tt:tt + 1], p_tr[:, :], axis=AX.X, apply_absolute_value=True),
                     reads=("p_tr",), writes=("amax",))
            else:
                p.op("vector", lambda e: e.reduce_max(
                    mtmp[:, :], p_tr[:, :], axis=AX.X, apply_absolute_value=True),
                     reads=("p_tr",), writes=("mtmp",))
                p.op("vector", lambda e, tt=tt: e.tensor_tensor(
                    amax[:, tt:tt + 1], amax[:, tt:tt + 1], mtmp[:, :], ALU.max),
                     reads=("mtmp", "amax"), writes=("amax",))
    p.op("vector", lambda e: e.tensor_scalar(amax[:, :], amax[:, :], 1e-8, None, ALU.max),
         reads=("amax",), writes=("amax",))
    p.op("vector", lambda e: e.reciprocal(amax[:, :], amax[:, :]),
         reads=("amax",), writes=("amax",))
    p.op("vector", lambda e: e.tensor_scalar(amax[:, :], amax[:, :], 127.0, None, ALU.mult),
         reads=("amax",), writes=("amax",))
    p.op("tensor", lambda e: e.transpose(cm.pproj[0][0:8, 0:128], amax[:, :], ident_f32[:, :]),
         reads=("amax", "identf32"), writes=("pproj0",))
    p.op("vector", lambda e: e.tensor_copy(rq8[:, :], cm.pproj[0][0:8, 0:128]),
         reads=("pproj0",), writes=("rq8",))
    rq_dst = bass.AP(rqrow.tensor, 0, [[T, 1], [1, T]])
    p.op("sync", lambda e: e.dma_start(out=rq_dst, in_=rq8[:, :]),
         reads=("rq8",), writes=("Rrow",), dma=True, dsem="Drq")
    for th in range(2):
        sl = slice(th * 512, th * 512 + 512)
        p.op("tensor", lambda e, sl=sl: e.matmul(p_rq[:, :512], cm.ones_row, rqrow[:, sl],
                                                 start=True, stop=True),
             reads=("Rrow", "ones"), writes=("pproj0",))
        p.op("vector", lambda e, sl=sl: e.tensor_copy(rqrep[:, sl], p_rq[:, :512]),
             reads=("pproj0",), writes=("rqrep",))
    for d in range(DC):
        sl = d % 2
        p.op("vector", lambda e, d=d, sl=sl: e.tensor_tensor(
            cm.tln[sl][:, :], lnb[:, d * T:(d + 1) * T], rqrep[:, :], ALU.mult),
             reads=(f"ln_{d}", "rqrep"), writes=("tln0", "tln1"))
        p.op("vector", lambda e, sl=sl: e.tensor_scalar(cm.tln[sl][:, :], cm.tln[sl][:, :],
                                                        MAGIC, MAGIC, ALU.add, ALU.subtract),
             reads=(f"tln{sl}",), writes=("tln0", "tln1"))
        p.op("vector", lambda e, sl=sl: e.tensor_scalar(qi[:, :], cm.tln[sl][:, :], 0.0, None, ALU.add),
             reads=(f"tln{sl}",), writes=("qi",))
        p.op("vector", lambda e, d=d: e.tensor_scalar(
            lnb[:, d * T:(d + 1) * T], qi[:, :], 0.0, None, ALU.add),
             reads=("qi",), writes=(f"xq_{d}", f"ln_{d}"))
    cm.projection(wq_hatT, lnb, lambda d: f"xq_{d}", T, T,
                  evac_simple(p, ych, lambda oc, th: f"y_{oc}"), False)
    cm.ln_post_add(lambda d: y_sb[:, d * T:(d + 1) * T], lambda d: f"y_{d}", False,
                   lambda d: x_sb[:, d * T:(d + 1) * T], lambda d: f"x_{d}", T)
    p.op("sync", lambda e: e.dma_start(out=x2.rearrange("(d p) t -> p d t", p=128), in_=x_sb[:, :]),
         reads=tuple(f"x_{d}" for d in range(DC)), writes=("out_x2",), dma=True, dsem="Dout")
    cm.ln_pre(lambda d: x_sb[:, d * T:(d + 1) * T], lambda d: f"x_{d}",
              lambda d: lnb[:, d * T:(d + 1) * T], lambda d: f"ln2_{d}", T,
              alias_of=lambda d: f"xq_{d}")

    def qdst(oc, th):
        return y_sb[:, oc * T + th * 512: oc * T + th * 512 + 512]
    cm.projection(wq1, lnb, lambda d: f"ln2_{d}", T, T,
                  evac_simple(p, qdst, lambda oc, th: (f"qq_{oc}", f"y_{oc}")), True,
                  bias_sb=bias_sb[:, 0:DC], bias_tag="biasb")
    p.op("gpsimd", lambda e: e.dma_start(out=q1.rearrange("(d p) t -> p d t", p=128), in_=y_sb[:, :]),
         reads=tuple(f"qq_{d}" for d in range(DC)), writes=("out_q1",), dma=True, dsem="Dout")
    cm.projection(wk1, lnb, lambda d: f"ln2_{d}", T, T,
                  evac_simple(p, qdst, lambda oc, th: (f"kk_{oc}", f"qq_{oc}")), True,
                  bias_sb=bias_sb[:, DC:2 * DC], bias_tag="biasb")
    p.op("gpsimd", lambda e: e.dma_start(out=k1.rearrange("(d p) t -> p d t", p=128), in_=y_sb[:, :]),
         reads=tuple(f"kk_{d}" for d in range(DC)), writes=("out_k1",), dma=True, dsem="Dout")

    def v_cb(oc, th, ps_ap, pstag, bias_sb_, bias_tag_):
        bc = bias_sb_[:, oc:oc + 1]
        p.op("scalar", lambda e, ps_ap=ps_ap, bc=bc: e.activation(vst[:, :], ps_ap, AF.Identity, bias=bc),
             reads=(pstag, bias_tag_), writes=("vst",))
        for tt4 in range(4):
            tt = th * 4 + tt4
            p.op("tensor", lambda e, tt4=tt4: e.transpose(
                p_tr[:, :], vst[:, tt4 * 128:(tt4 + 1) * 128], ident_bf[:, :]),
                 reads=("vst", "ident"), writes=("p_tr",))
            dst = vtt[:, tt * 128:(tt + 1) * 128]
            p.op("scalar", lambda e, dst=dst: e.copy(dst, p_tr[:, :]),
                 reads=("p_tr",), writes=("vtt",))
        if th == 1:
            dst = bass.AP(v1.tensor, oc * 128, [[D, 128], [128 * D, 8], [1, 128]])
            src_ = bass.AP(vtt, 0, [[8 * 128, 128], [128, 8], [1, 128]])
            p.op("gpsimd", lambda e, dst=dst, src_=src_: e.dma_start(out=dst, in_=src_),
                 reads=("vtt",), writes=("out_v1",), dma=True, dsem="Dout")
    cm.projection(wv1, lnb, lambda d: f"ln2_{d}", T, T, v_cb, True,
                  bias_sb=bias_sb[:, 2 * DC:3 * DC], bias_tag="biasb")
    p.finish(["out_q1", "out_k1", "out_v1", "out_x2"])
    p.emit()
    return nc



def build_launch_e():
    nc = bass.Bass("TRN2", target_bir_lowering=False)
    x2 = nc.dram_tensor("x2", [D, T], F32, kind="ExternalInput").ap()
    attn1 = nc.dram_tensor("attn1", [D, T], F32, kind="ExternalInput").ap()
    wo1 = nc.dram_tensor("wo1", [8 * 128, DC * 256], F32, kind="ExternalInput").ap()
    bo_in = nc.dram_tensor("bo", [128, DC], F32, kind="ExternalInput").ap()
    wgut = nc.dram_tensor("wgut", [64 * 128, 2 * 2048], BF16, kind="ExternalInput").ap()
    wprt = nc.dram_tensor("wprt", [2 * 16 * 128, 32 * 128], BF16, kind="ExternalInput").ap()
    xout = nc.dram_tensor("xout", [D, T], F32, kind="ExternalOutput").ap()

    p = Prog(nc)
    cm = Common(p)
    x_sb = p.sb("x_sb", [128, DC * T], F32)
    u1 = p.sb("u1", [128, DC * T], BF16)
    yl = p.sb("yl", [128, DC * T], BF16)
    y3 = p.sb("y3", [128, DC * 512], BF16)
    bo_sb = p.sb("bo_sb", [128, DC], F32)
    gs = p.sb("gs", [128, 512], BF16)
    _pg = p.ps("p_g", [128, 512])
    _pu = p.ps("p_u", [128, 512])
    p_g = [_pg, _pg]
    p_u = [_pu, _pu]

    p.op("sync", lambda e: e.dma_start(out=bo_sb[:, :], in_=bo_in), writes=("bo",), dma=True, dsem="Din")
    p.op("sync", lambda e: e.dma_start(out=x_sb[:, :], in_=x2.rearrange("(d p) t -> p d t", p=128)),
         writes=tuple(f"x_{d}" for d in range(DC)), dma=True, dsem="Din")
    p.op("gpsimd", lambda e: e.dma_start(
        out=u1[:, :], in_=attn1.rearrange("(d p) t -> p d t", p=128)),
        writes=tuple(f"a1_{d}" for d in range(DC)), dma=True, dsem="Din")

    def ych(oc, th):
        return yl[:, oc * T + th * 512: oc * T + th * 512 + 512]
    cm.projection(wo1, u1, lambda d: f"a1_{d}", T, T,
                  evac_simple(p, ych, lambda oc, th: f"y_{oc}"), True,
                  bias_sb=bo_sb, bias_tag="bo")
    cm.ln_post_add(lambda d: yl[:, d * T:(d + 1) * T], lambda d: f"y_{d}", False,
                   lambda d: x_sb[:, d * T:(d + 1) * T], lambda d: f"x_{d}", T)
    cm.ln_pre(lambda d: x_sb[:, d * T:(d + 1) * T], lambda d: f"x_{d}",
              lambda d: yl[:, d * T:(d + 1) * T], lambda d: f"ln3_{d}", T,
              alias_of=lambda d: f"y_{d}")

    for th in range(2):
        for hh in range(2):
            for hc in range(32):
                hrow = hh * HHALF + hc * 128
                wb = cm.wbuf[hc % 2]
                wtag = f"wb{hc % 2}"
                ghc = hh * 32 + hc
                gsrc = wgut[ghc * 128:(ghc + 1) * 128, :]
                p.op("scalar", lambda e, s=gsrc, wb=wb: e.dma_start(out=wb[:, :], in_=s),
                     reads=(), writes=(wtag,), dma=True, dsem=f"Dw{hc % 2}")
                pi = hc % 2
                fng, fnu = [], []
                for d in range(DC):
                    lg = wb[:, d * 128:d * 128 + 128]
                    lu = wb[:, 2048 + d * 128:2048 + d * 128 + 128]
                    r = yl[:, d * T + th * 512: d * T + th * 512 + 512]
                    fng.append(lambda e, l=lg, r=r, f=(d == 0), la=(d == DC - 1), pi=pi:
                               e.matmul(p_g[pi][:, :], l, r, start=f, stop=la))
                    fnu.append(lambda e, l=lu, r=r, f=(d == 0), la=(d == DC - 1), pi=pi:
                               e.matmul(p_u[pi][:, :], l, r, start=f, stop=la))
                reads = (wtag,) + tuple(f"ln3_{d}" for d in range(DC))
                p.multi("tensor", fng, reads=reads, writes=("p_g0", "p_g1"))
                p.multi("tensor", fnu, reads=reads, writes=("p_u0", "p_u1"))
                p.op("scalar", lambda e, pi=pi: e.activation(gs[:, :], p_g[pi][:, :], AF.Silu),
                     reads=(f"p_g{pi}",), writes=("gs",))
                hdst = u1[:, hc * 512:(hc + 1) * 512]
                p.op("vector", lambda e, pi=pi, hdst=hdst: e.tensor_tensor(
                    hdst, gs[:, :], p_u[pi][:, :], ALU.mult),
                     reads=("gs", f"p_u{pi}"), writes=(f"h_{hc}", f"a1_{hc // 2}"))
            for oc in range(DC):
                wb = cm.wbuf[oc % 2]
                wtag = f"wb{oc % 2}"
                poc = hh * 16 + oc
                src_r = wprt[poc * 128:(poc + 1) * 128, :]
                p.op("sync", lambda e, wb=wb, src_r=src_r: e.dma_start(out=wb[:, :], in_=src_r),
                     reads=(), writes=(wtag,), dma=True, dsem=f"Dw{oc % 2}")
                pi = oc % 2
                fns = []
                for hc in range(32):
                    l = wb[:, hc * 128:(hc + 1) * 128]
                    r = u1[:, hc * 512:(hc + 1) * 512]
                    fns.append(lambda e, l=l, r=r, f=(hc == 0), la=(hc == 31), pi=pi:
                               e.matmul(cm.pproj[pi][:, :], l, r, start=f, stop=la))
                reads = (wtag,) + tuple(f"h_{hc}" for hc in range(32))
                p.multi("tensor", fns, reads=reads, writes=(f"pproj{pi}",))
                ydst = y3[:, oc * 512:(oc + 1) * 512]
                if hh == 0:
                    p.op("scalar", lambda e, pi=pi, ydst=ydst: e.copy(ydst, cm.pproj[pi][:, :]),
                         reads=(f"pproj{pi}",), writes=(f"y3_{oc}",))
                else:
                    p.op("vector", lambda e, pi=pi, ydst=ydst: e.tensor_tensor(
                        ydst, ydst, cm.pproj[pi][:, :], ALU.add),
                         reads=(f"pproj{pi}", f"y3_{oc}"), writes=(f"y3_{oc}",))
        cm.ln_post_add(lambda d: y3[:, d * 512:(d + 1) * 512], lambda d: f"y3_{d}", False,
                       lambda d, th=th: x_sb[:, d * T + th * 512: d * T + th * 512 + 512],
                       lambda d: f"x_{d}", 512)

    p.op("sync", lambda e: e.dma_start(out=xout.rearrange("(d p) t -> p d t", p=128), in_=x_sb[:, :]),
         reads=tuple(f"x_{d}" for d in range(DC)), writes=("out_x",), dma=True, dsem="Dout")
    p.finish(["out_x"])
    p.emit()
    return nc



def host_tables():
    half = 64
    theta = 10000.0 ** (-np.arange(half, dtype=np.float64) * 2.0 / 128.0)
    pos = np.arange(S, dtype=np.float64)
    ang = pos[:, None] * theta[None, :]
    cos = np.cos(ang).astype(np.float32)
    sin = np.sin(ang).astype(np.float32)
    ident = np.eye(128, dtype=np.float32)
    ppp = np.arange(128)[:, None]
    fff = np.arange(512)[None, :]
    masks = np.stack([(fff >= ppp + 128 * r).astype(np.float32) for r in range(4)], 0)
    start = 2.0 ** (-8.0 / H)
    slopes = np.array([start * start ** i for i in range(H)], dtype=np.float64)
    d0 = (ppp - fff).astype(np.float64)
    return cos, sin, ident, masks, slopes, d0


_CACHE = {}


def _ncs():
    if "a" not in _CACHE:
        _CACHE["a"] = build_launch_a()
        _CACHE["b0"] = build_attention(0)
        _CACHE["c"] = build_launch_c()
        _CACHE["b1"] = build_attention(1)
        _CACHE["e"] = build_launch_e()
    return _CACHE


def kernel(L0_attn_x, L0_attn_Wq, L0_attn_Wk, L0_attn_Wv, L0_attn_Wo,
           L0_ffn_weight, L1_attn_wq, L1_attn_bq, L1_attn_wk, L1_attn_bk,
           L1_attn_wv, L1_attn_bv, L1_attn_wo, L1_attn_bo,
           L1_ffn_w_gate_up, L1_ffn_w_proj):
    cos, sin, ident, masks, slopes, d0 = host_tables()
    ncs = _ncs()
    x = np.ascontiguousarray(np.asarray(L0_attn_x, dtype=np.float32).reshape(S, D))
    wq0T = tile_w(np.asarray(L0_attn_Wq, np.float32).T)
    wk0T = tile_w(np.asarray(L0_attn_Wk, np.float32).T)
    wv0T = tile_w(np.asarray(L0_attn_Wv, np.float32).T)
    wo0T = tile_w(np.asarray(L0_attn_Wo, np.float32).T)
    w0T = np.ascontiguousarray(np.asarray(L0_ffn_weight, np.float32).T)
    wgu_bf = np.asarray(L1_ffn_w_gate_up, np.float32).astype(BF)
    wpr_bf = np.asarray(L1_ffn_w_proj, np.float32).astype(BF)

    tok_rows = {c: np.r_[np.arange(512 * c, 512 * c + 512),
                         np.arange(512 * (15 - c), 512 * (15 - c) + 512)]
                for c in range(NC)}

    in_a = []
    for c in range(NC):
        rows = tok_rows[c]
        cc = cos[rows].T
        ss = sin[rows].T
        in_a.append({
            "xT": np.ascontiguousarray(x[rows].T),
            "wq0T": wq0T, "wk0T": wk0T, "wv0T": wv0T,
            "ccat": np.ascontiguousarray(np.concatenate([cc, cc], 0)),
            "scat": np.ascontiguousarray(np.concatenate([-ss, ss], 0)),
            "w0T": w0T,
            "w0T_rows": np.ascontiguousarray(w0T[256 * c:256 * c + 256]),
            "wgu": np.ascontiguousarray(wgu_bf[2048 * c:2048 * (c + 1)]),
            "wpr": np.ascontiguousarray(wpr_bf[256 * c:256 * (c + 1)]),
            "identf": ident,
        })
    ra = run_bass_kernel_spmd(ncs["a"], in_a, core_ids=list(range(NC))).results

    q0f = np.zeros((H, 128, S), np.float32)
    k0f = np.zeros((H, 128, S), np.float32)
    v0f = np.zeros((H, S, 128), np.float32)
    for c in range(NC):
        rows = tok_rows[c]
        q0c = np.asarray(ra[c]["q0"])
        k0c = np.asarray(ra[c]["k0"])
        v0c = np.asarray(ra[c]["v0"])
        for h in range(H):
            q0f[h][:, rows] = q0c[h * 128:(h + 1) * 128]
            k0f[h][:, rows] = k0c[h * 128:(h + 1) * 128]
            v0f[h][rows, :] = v0c[:, h * 128:(h + 1) * 128]
    wq_hatT = tile_w(np.concatenate([np.asarray(ra[c]["wq_hat"]) for c in range(NC)], 0))
    wgu_hat = np.concatenate([np.asarray(ra[c]["wgu_hat"]) for c in range(NC)], 0)
    wpr_hat = np.concatenate([np.asarray(ra[c]["wpr_hat"]) for c in range(NC)], 0)
    wguT = wgu_hat.T
    wgut = np.empty((64 * 128, 2 * 2048), dtype=wgu_hat.dtype)
    for ghc in range(64):
        hrow = ghc * 128
        g = wguT[:, hrow:hrow + 128].reshape(16, 128, 128).transpose(1, 0, 2).reshape(128, 2048)
        u = wguT[:, 8192 + hrow:8192 + hrow + 128].reshape(16, 128, 128).transpose(1, 0, 2).reshape(128, 2048)
        wgut[ghc * 128:(ghc + 1) * 128, 0:2048] = g
        wgut[ghc * 128:(ghc + 1) * 128, 2048:4096] = u
    wgut = np.ascontiguousarray(wgut)
    wprT = wpr_hat.T
    wprt = np.empty((2 * 16 * 128, 32 * 128), dtype=wpr_hat.dtype)
    for hh in range(2):
        for oc in range(16):
            sl = wprT[hh * 4096:(hh + 1) * 4096, oc * 128:(oc + 1) * 128]
            wprt[(hh * 16 + oc) * 128:(hh * 16 + oc + 1) * 128] = (
                sl.reshape(32, 128, 128).transpose(1, 0, 2).reshape(128, 4096))
    wprt = np.ascontiguousarray(wprt)

    in_b = []
    for c in range(NC):
        hs = heads_of(c)
        in_b.append({
            "qh": np.ascontiguousarray(np.concatenate([q0f[h] for h in hs], 0)),
            "kh": np.ascontiguousarray(np.concatenate([k0f[h] for h in hs], 0)),
            "vh": np.ascontiguousarray(np.stack([v0f[h] for h in hs], 0)),
            "identf": ident,
            "masks": np.ascontiguousarray(masks.reshape(4 * 128, 512)),
        })
    rb = run_bass_kernel_spmd(ncs["b0"], in_b, core_ids=list(range(NC))).results
    attn0f = np.zeros((H, 128, S), np.float32)
    for c in range(NC):
        a = np.asarray(rb[c]["attn"])
        hs = heads_of(c)
        attn0f[hs[0]] = a[0:128]
        attn0f[hs[1]] = a[128:256]

    wq1t = tile_w(np.asarray(L1_attn_wq, np.float32))
    wk1t = tile_w(np.asarray(L1_attn_wk, np.float32))
    wv1t = tile_w(np.asarray(L1_attn_wv, np.float32))
    bq = np.asarray(L1_attn_bq, np.float32).reshape(DC, 128).T
    bk = np.asarray(L1_attn_bk, np.float32).reshape(DC, 128).T
    bv = np.asarray(L1_attn_bv, np.float32).reshape(DC, 128).T
    biases = np.ascontiguousarray(np.concatenate([bq, bk, bv], 1))
    in_c = []
    for c in range(NC):
        rows = tok_rows[c]
        in_c.append({
            "xT": in_a[c]["xT"],
            "attn0": np.ascontiguousarray(attn0f[:, :, rows].reshape(D, T)),
            "wo0T": wo0T, "wq_hatT": wq_hatT,
            "wq1": wq1t, "wk1": wk1t, "wv1": wv1t,
            "biases": biases, "identf": ident,
        })
    rc = run_bass_kernel_spmd(ncs["c"], in_c, core_ids=list(range(NC))).results
    q1f = np.zeros((H, 128, S), np.float32)
    k1f = np.zeros((H, 128, S), np.float32)
    v1f = np.zeros((H, S, 128), np.float32)
    x2f = {}
    for c in range(NC):
        rows = tok_rows[c]
        q1c = np.asarray(rc[c]["q1"])
        k1c = np.asarray(rc[c]["k1"])
        v1c = np.asarray(rc[c]["v1"])
        x2f[c] = np.asarray(rc[c]["x2"])
        for h in range(H):
            q1f[h][:, rows] = q1c[h * 128:(h + 1) * 128]
            k1f[h][:, rows] = k1c[h * 128:(h + 1) * 128]
            v1f[h][rows, :] = v1c[:, h * 128:(h + 1) * 128]

    in_d = []
    for c in range(NC):
        hs = heads_of(c)
        dts = []
        for h in hs:
            dh = slopes[h] * d0
            vars_ = [dh]
            for r in range(4):
                m = (np.arange(512)[None, :] < np.arange(128)[:, None] + 128 * r)
                vars_.append(np.where(m, NEG_BIG, dh))
            dts.append(np.stack(vars_, 0))
        dts = np.ascontiguousarray(np.stack(dts, 0).astype(np.float32).reshape(2 * 5 * 128, 512))
        btab = np.zeros((128, 2 * 68), np.float32)
        for si, h in enumerate(hs):
            for dk in range(-3, 64):
                btab[:, si * 68 + dk + 3] = np.float32(-slopes[h] * 128.0 * dk)
        in_d.append({
            "qh": np.ascontiguousarray(np.concatenate([q1f[h] for h in hs], 0)),
            "kh": np.ascontiguousarray(np.concatenate([k1f[h] for h in hs], 0)),
            "vh": np.ascontiguousarray(np.stack([v1f[h] for h in hs], 0)),
            "identf": ident, "dts": dts, "btab": btab,
        })
    rd = run_bass_kernel_spmd(ncs["b1"], in_d, core_ids=list(range(NC))).results
    attn1f = np.zeros((H, 128, S), np.float32)
    for c in range(NC):
        a = np.asarray(rd[c]["attn"])
        hs = heads_of(c)
        attn1f[hs[0]] = a[0:128]
        attn1f[hs[1]] = a[128:256]

    bo = np.ascontiguousarray(np.asarray(L1_attn_bo, np.float32).reshape(DC, 128).T)
    wo1t = tile_w(np.asarray(L1_attn_wo, np.float32))
    in_e = []
    for c in range(NC):
        rows = tok_rows[c]
        in_e.append({
            "x2": x2f[c],
            "attn1": np.ascontiguousarray(attn1f[:, :, rows].reshape(D, T)),
            "wo1": wo1t,
            "bo": bo, "wgut": wgut, "wprt": wprt,
        })
    re_ = run_bass_kernel_spmd(ncs["e"], in_e, core_ids=list(range(NC))).results

    out = np.zeros((S, D), np.float32)
    for c in range(NC):
        rows = tok_rows[c]
        out[rows] = np.asarray(re_[c]["xout"]).T
    return out.reshape(1, S, D)
